# version 08
# baseline (speedup 1.0000x reference)
import numpy as np
import ml_dtypes

import concourse.bass as bass
import concourse.mybir as mybir
from concourse.bass_utils import run_bass_kernel_spmd

F32 = mybir.dt.float32
BF16 = mybir.dt.bfloat16
AF = mybir.ActivationFunctionType
ALU = mybir.AluOpType
AX = mybir.AxisListType.X
BF = ml_dtypes.bfloat16

T = 2048
HALO = 4
TH = T + HALO
D = 1024
KD = 8
DI = 2048
NCT = 16
DS = 16
Q = 128
NCH = 16
NE = 8
HID = 2048
NCORE = 8

ENGS = ("sync", "gpsimd", "tensor", "vector", "scalar")

SCHEDS = []


class Sched:

    def __init__(self, nc, sems):
        self.nc = nc
        self.ops = {e: [] for e in ENGS}
        self.sems = sems

    def add(self, eng, fn, waits=(), inc=None, inc_n=1):
        w = [(s, v) for (s, v) in waits]
        self.ops[eng].append((fn, w, inc, inc_n))
        if inc is not None:
            self.sems[inc][1] += inc_n
        return self.sems[inc][1] if inc is not None else None

    def val(self, sem):
        return self.sems[sem][1]

    def emit(self, block):
        for eng_name in ENGS:
            ops = self.ops[eng_name]
            if not ops:
                continue

            def body(eng, ops=ops, eng_name=eng_name):
                maxw = {}
                for fn, waits, inc, inc_n in ops:
                    for s, v in waits:
                        if v <= 0:
                            continue
                        if maxw.get(s, -1) >= v:
                            continue
                        eng.wait_ge(self.sems[s][0], v)
                        maxw[s] = v
                    instr = fn(eng)
                    if inc is not None:
                        instr.then_inc(self.sems[inc][0], inc_n)
            getattr(block, eng_name)(body)


def build(debug=False):
    nc = bass.Bass("TRN2", target_bir_lowering=False, num_devices=NCORE,
                   detect_race_conditions=False)

    def ein(name, shape):
        return nc.dram_tensor(name, list(shape), F32, kind="ExternalInput")

    x_own = ein("x_own", (T, D))
    x_halo = ein("x_halo", (HALO, D))
    winT = ein("winT", (D, 2 * DI))
    wstack = ein("wstack", (DI, 64))
    conv_w = ein("conv_w", (128, 64))
    conv_b = ein("conv_b", (128, NCT))
    d_col = ein("d_col", (128, NCT))
    a_log = ein("a_log", (1, DS))
    delta_b = ein("delta_b", (1, DS))
    woutT = ein("woutT", (DI, D))
    wrT = ein("wrT", (D, NE))
    br_r = ein("br_r", (1, NE))
    bn_r = ein("bn_r", (1, NE))
    bn16 = ein("bn16", (1, 128))
    wl1 = ein("wl1", (NE, D, 2 * HID))
    bl1c = ein("bl1c", (128, 256))
    wl2 = ein("wl2", (NE, HID, D))
    bl2 = ein("bl2", (NE, D))
    oh_in = ein("oh", (1, NCORE))
    wmaskT = ein("wmaskT", (NCORE, NCORE))
    mvec = ein("mvec", (NCORE, 1))

    out = nc.dram_tensor("out", [T, D], F32, kind="ExternalOutput")
    cc_buf = nc.dram_tensor("cc_buf", [NCORE, 32], F32)

    dbg = {}
    if debug:
        for name, shape in (
            ("dbg_xn", (128, D)), ("dbg_xconv", (128, T)), ("dbg_sg", (128, T)),
            ("dbg_delta", (128, 256)), ("dbg_ysm", (128, NCH)), ("dbg_ew", (128, 128)),
            ("dbg_yac", (128, D)), ("dbg_su", (8, 32)), ("dbg_g", (128, 256)),
            ("dbg_ycol", (128, NCH)), ("dbg_xnT", (128, 128)),
            ("dbg_sm", (128, 48)), ("dbg_a", (128, 256)), ("dbg_bcum", (128, 256)),
            ("dbg_yloc", (128, NCH)), ("dbg_sm2", (128, 48)), ("dbg_cr", (128, 48)),
            ("dbg_w2", (1, 256)), ("dbg_er", (1, 256)), ("dbg_hr", (1, 272)),
        ):
            dbg[name] = nc.dram_tensor(name, list(shape), F32, kind="ExternalOutput")

    ident_bf = nc.inline_tensor(np.eye(128, dtype=BF), "c_identbf")
    ident_f = nc.inline_tensor(np.eye(128, dtype=np.float32), "c_identf")
    tri = np.tril(np.ones((128, 128), dtype=np.float32))
    triT_c = nc.inline_tensor(np.ascontiguousarray(tri.T), "c_triT")
    ones_row_c = nc.inline_tensor(np.ones((1, 128), dtype=np.float32), "c_onesrow")
    sel_np = np.zeros((NE, NE * 128), dtype=BF)
    for n in range(NE):
        sel_np[n, n * 128:(n + 1) * 128] = 1
    sel_c = nc.inline_tensor(sel_np, "c_sel")

    BASE = 16512
    cur = [BASE]

    def at(name, shape, dtype, offset=None):
        if offset is None:
            offset = cur[0]
            sz = int(np.prod(shape[1:])) * mybir.dt.size(dtype)
            cur[0] = (offset + sz + 31) & ~31
        return nc.alloc_sbuf_tensor_at(name, list(shape), dtype, offset=offset)

    xnT = at("xnT", (128, KD * TH), BF16)
    B1 = cur[0]
    xconvT = at("xconvT", (128, NCT * T), BF16)
    y_acc = at("y_acc", (128, 16 * D), F32, offset=B1)
    B2 = cur[0]
    sgT = at("sgT", (128, NCT * T), BF16)
    yskT = at("yskT", (128, NCT * T), BF16, offset=B2)
    gT = at("gT", (128, NCT * T), BF16, offset=B2)
    S0 = cur[0]
    xs = at("xs", (128, TH), F32, offset=S0)
    convA = at("convA", (128, 512), F32, offset=S0 + 8224)
    convB = at("convB", (128, 512), F32, offset=S0 + 8224 + 2048)
    winst = at("winst", (128, 2048), BF16, offset=S0 + 8224 + 4096)
    xld = at("xld", (128, 2 * D), F32, offset=S0 + 16416)
    sdbc = at("sdbc", (64, T), F32, offset=S0)
    woutst = at("woutst", (128, NCT * 512), BF16, offset=S0)
    wabst = at("wabst", (128, 4 * D), BF16, offset=S0)
    wl2st = at("wl2st", (128, 8 * D), BF16, offset=S0 + 8192)
    cur[0] = S0 + 24640
    R0 = cur[0]
    dpre = at("dpre", (128, 256), F32, offset=R0)
    sBs = at("sBs", (128, 256), F32, offset=R0 + 1024)
    sCs = at("sCs", (128, 256), F32, offset=R0 + 2048)
    sA_ = at("sA_", (128, 256), F32, offset=R0 + 3072)
    sG = at("sG", (128, 256), F32, offset=R0 + 4096)
    sT1 = at("sT1", (128, 256), F32, offset=R0 + 5120)
    sT2 = at("sT2", (128, 256), F32, offset=R0 + 6144)
    tmp2 = at("tmp2", (128, 1024), F32, offset=R0)
    saA = at("saA", (128, 512), F32, offset=R0)
    saB = at("saB", (128, 512), F32, offset=R0 + 2048)
    convA2 = at("convA2", (128, 512), F32, offset=R0 + 4096)
    convB2 = at("convB2", (128, 512), F32, offset=R0 + 6144)
    cur[0] = R0 + 7168
    YB = cur[0]
    y_bc = at("y_bc", (128, T), BF16)
    ew_bcT = at("ew_bcT", (128, T), BF16, offset=YB)
    ew = at("ew", (128, 128), F32, offset=R0 + 4096)
    ewT2 = at("ewT2", (NE, T), BF16)
    xavg = at("xavg", (128, NCH), F32)
    y_col = at("y_col", (128, NCH), F32)
    yfin = xavg
    glr = dpre[0:1, 0:256]
    blr = sBs[0:1, 0:256]
    gcum = sdbc[0:1, 0:272]
    hrow = sdbc[0:1, 272:544]
    yrow_stage = dpre[0:1, 0:128]
    erow = sCs[0:1, 0:DS]
    wrow = sCs[0:1, DS:2 * DS]
    arow_r = sCs[0:1, 2 * DS:3 * DS]
    hin = sCs[0:1, 3 * DS:4 * DS]
    arow = sCs[0:1, 0:DS]
    abc = at("abc", (128, DS), F32)
    dbbc = at("dbbc", (128, DS), F32)
    brbc = at("brbc", (128, NE), F32)
    bnbc = at("bnbc", (128, NE), F32)
    su8 = at("su8", (NCORE, 32), F32)
    gput = at("gput", (NCORE, 32), F32)
    e8 = at("e8", (NCORE, DS), F32)
    ctrb = at("ctrb", (NCORE, DS), F32)
    wstk = at("wstk", (128, 2 * 64), BF16)
    cwsb = at("cwsb", (128, 64), F32)
    cbsb = at("cbsb", (128, NCT), F32)
    dsb = at("dsb", (128, NCT), F32)
    alsb = at("alsb", (1, DS), F32)
    dbsb = at("dbsb", (1, DS), F32)
    brsb = at("brsb", (1, NE), F32)
    bnsb = at("bnsb", (1, NE), F32)
    bl1sb = at("bl1sb", (128, 256), F32)
    bl2sb = at("bl2sb", (NE, D), BF16)
    wrsb = at("wrsb", (128, 64), BF16)
    ohsb = at("ohsb", (1, NCORE), F32)
    wmsb = at("wmsb", (NCORE, NCORE), F32)
    mvsb = at("mvsb", (NCORE, 1), F32)
    idbf = at("idbf", (128, 128), BF16)
    idf = at("idf", (128, 128), F32)
    trit = at("trit", (128, 128), F32)
    onesrow = at("onesrow", (1, 128), F32)
    selsb = at("selsb", (NE, NE * 128), BF16)

    sbuf_used = cur[0]

    pA = nc.alloc_psum_tensor("pA", [128, 512], F32)
    pB = nc.alloc_psum_tensor("pB", [128, 512], F32)
    pC = nc.alloc_psum_tensor("pC", [128, 512], F32)
    pD = nc.alloc_psum_tensor("pD", [128, 512], F32)
    pE = nc.alloc_psum_tensor("pE", [128, 512], F32)
    pF = nc.alloc_psum_tensor("pF", [128, 512], F32)
    pG = nc.alloc_psum_tensor("pG", [128, 512], F32)
    pH = nc.alloc_psum_tensor("pH", [128, 1024], BF16)
    pAB = [pA, pB]
    pCD = [pC, pD]
    pEF = [pE, pF]

    sems = {k: [nc.alloc_semaphore(k), 0] for k in ("L", "W", "T", "V", "S", "C")}

    def blockrun(fill):
        sch = Sched(nc, sems)
        fill(sch)
        SCHEDS.append((fill.__name__, {e: [(w, inc, n) for (f, w, inc, n) in sch.ops[e]]
                                       for e in ENGS}))
        with nc.Block() as blk:
            sch.emit(blk)

    def dma(dst, src):
        return lambda e: e.dma_start(out=dst, in_=src)

    def cp(dst, src):
        return lambda e: e.tensor_copy(out=dst, in_=src)

    def p0a(s):
        for dst, src in (
            (cwsb[:, :], conv_w[:, :]), (cbsb[:, :], conv_b[:, :]),
            (dsb[:, :], d_col[:, :]), (alsb[:, :], a_log[:, :]),
            (dbsb[:, :], delta_b[:, :]), (brsb[:, :], br_r[:, :]),
            (bnsb[:, :], bn_r[:, :]), (bl1sb[:, :], bl1c[:, :]),
            (ohsb[:, :], oh_in[:, :]), (wmsb[:, :], wmaskT[:, :]),
            (mvsb[:, :], mvec[:, :]), (idbf[:, :], ident_bf[:, :]),
            (idf[:, :], ident_f[:, :]), (trit[:, :], triT_c[:, :]),
            (onesrow[:, :], ones_row_c[:, :]),
            (selsb[:, :], sel_c[:, :]),
        ):
            s.add("sync", dma(dst, src), inc="L", inc_n=16)
        s.add("gpsimd", dma(bl2sb[:, :], bl2[:, :]), inc="W", inc_n=16)
        s.add("gpsimd",
              dma(wrsb.ap().rearrange("p (k n) -> p k n", k=KD),
                  wrT.ap().rearrange("(k p) n -> p k n", p=128)),
              inc="W", inc_n=16)
        s.add("gpsimd",
              dma(winst.ap()[:, 0:1024].rearrange("p (k n) -> p k n", k=KD),
                  winT.ap()[:, 0:128].rearrange("(k p) n -> p k n", p=128)),
              inc="W", inc_n=16)
        s.add("gpsimd", dma(wstk[:, 0:64], wstack[0:128, :]), inc="W", inc_n=16)
        lv = s.val("L")
        s.add("scalar", lambda e: e.activation(out=arow[:, :], in_=alsb[:, :],
                                               func=AF.Exp),
              waits=[("L", lv)], inc="S")
        s.add("vector", lambda e: e.tensor_scalar(out=arow[:, :], in0=arow[:, :],
                                                  scalar1=-1.0, scalar2=None,
                                                  op0=ALU.mult),
              waits=[("S", s.val("S"))], inc="V")
        vv = s.val("V")
        s.add("tensor", lambda e: e.matmul(out=pG[:, 0:DS], lhsT=onesrow[:, :],
                                           rhs=arow[:, :], start=True, stop=True),
              waits=[("V", vv), ("L", lv)], inc="T")
        s.add("vector", cp(abc[:, :], pG[:, 0:DS]), waits=[("T", s.val("T"))], inc="V")
        vv = s.val("V")
        s.add("tensor", lambda e: e.matmul(out=pG[:, 64:64 + DS], lhsT=onesrow[:, :],
                                           rhs=dbsb[:, :], start=True, stop=True),
              waits=[("V", vv)], inc="T")
        s.add("vector", cp(dbbc[:, :], pG[:, 64:64 + DS]), waits=[("T", s.val("T"))], inc="V")
        vv = s.val("V")
        s.add("tensor", lambda e: e.matmul(out=pG[:, 128:128 + NE], lhsT=onesrow[:, :],
                                           rhs=brsb[:, :], start=True, stop=True),
              waits=[("V", vv)], inc="T")
        s.add("vector", cp(brbc[:, :], pG[:, 128:128 + NE]), waits=[("T", s.val("T"))], inc="V")
        vv = s.val("V")
        s.add("tensor", lambda e: e.matmul(out=pG[:, 192:192 + NE], lhsT=onesrow[:, :],
                                           rhs=bnsb[:, :], start=True, stop=True),
              waits=[("V", vv)], inc="T")
        s.add("vector", cp(bnbc[:, :], pG[:, 192:192 + NE]), waits=[("T", s.val("T"))], inc="V")

    blockrun(p0a)

    def p0b(s):
        center_done = {}
        s.add("vector", lambda e: e.memset(convA[:, 6:7], 1e-5), inc="V")
        for ti in range(-1, 16):
            rows = HALO if ti < 0 else 128
            if ti < 0:
                ldst = xld[0:HALO, 0:D]
                lsrc = x_halo[:, :]
            else:
                sl = (ti % 2) * D
                ldst = xld[:, sl:sl + D]
                lsrc = x_own[ti * 128:(ti + 1) * 128, :]
            prev_user = center_done.get(ti - 2)
            if ti == 0:
                prev_user = center_done.get(-1)
            s.add("sync", dma(ldst, lsrc),
                  waits=[("V", prev_user)] if prev_user else (), inc="L", inc_n=16)
            lv = s.val("L")
            src = ldst
            s.add("vector", lambda e, src=src, rows=rows: e.reduce_sum(
                out=convA[0:rows, 0:1], in_=src, axis=AX), waits=[("L", lv)], inc="V")
            vv = s.val("V")
            s.add("scalar", lambda e, rows=rows: e.activation(
                out=convA[0:rows, 1:2], in_=convA[0:rows, 0:1], func=AF.Identity,
                scale=1.0 / D), waits=[("V", vv)], inc="S")
            sv = s.val("S")
            s.add("vector", lambda e, src=src, rows=rows: e.tensor_scalar(
                out=xs[0:rows, 0:D], in0=src, scalar1=convA[0:rows, 1:2],
                scalar2=None, op0=ALU.subtract), waits=[("S", sv)], inc="V")
            center_done[ti] = s.val("V")
            vv = s.val("V")
            s.add("scalar", lambda e, rows=rows: e.activation(
                out=tmp2[0:rows, 0:D], in_=xs[0:rows, 0:D], func=AF.Square),
                waits=[("V", vv)], inc="S")
            sv = s.val("S")
            s.add("vector", lambda e, rows=rows: e.reduce_sum(
                out=convA[0:rows, 2:3], in_=tmp2[0:rows, 0:D], axis=AX),
                waits=[("S", sv)], inc="V")
            vv = s.val("V")
            s.add("scalar", lambda e, rows=rows: e.activation(
                out=convA[0:rows, 4:5], in_=convA[0:rows, 2:3], func=AF.Ln,
                scale=1.0 / D, bias=convA[0:rows, 6:7]), waits=[("V", vv)], inc="S")
            s.add("scalar", lambda e, rows=rows: e.activation(
                out=convA[0:rows, 5:6], in_=convA[0:rows, 4:5], func=AF.Exp,
                scale=-0.5), inc="S")
            wyb = [("S", s.val("S"))]
            if debug and ti == 1:
                wyb.append(("W", dbg_xn_w))
            s.add("vector", lambda e, rows=rows: e.tensor_scalar(
                out=y_bc[0:rows, 0:D], in0=xs[0:rows, 0:D],
                scalar1=convA[0:rows, 5:6], scalar2=None, op0=ALU.mult),
                waits=wyb, inc="V")
            vv = s.val("V")
            dbg_gate = None
            if debug and ti == 0:
                s.add("gpsimd", dma(dbg["dbg_xn"][:, :], y_bc[:, 0:D]),
                      waits=[("V", s.val("V"))], inc="W", inc_n=16)
                dbg_xn_w = s.val("W")
            for k in range(KD):
                tv_prev = s.val("V")
                s.add("tensor", lambda e, k=k, rows=rows: e.transpose(
                    out=pH[:, k * 128:k * 128 + rows],
                    in_=y_bc[0:rows, k * 128:(k + 1) * 128],
                    identity=idbf[0:rows, 0:rows]),
                    waits=[("V", vv)], inc="T")
            tv = s.val("T")
            col = 0 if ti < 0 else HALO + ti * 128
            s.add("vector", lambda e, col=col, rows=rows: e.tensor_copy(
                out=xnT.ap().rearrange("p (k t) -> p k t", k=KD)[:, :, col:col + rows],
                in_=pH.ap().rearrange("p (k t) -> p k t", k=KD)[:, :, 0:rows]),
                waits=[("T", tv)], inc="V")

    blockrun(p0b)

    if debug:
        def p0b2(s):
            s.add("gpsimd", dma(dbg["dbg_xnT"][:, :],
                                xnT[:, HALO:HALO + 128]),
                  waits=[("V", s.val("V"))], inc="W", inc_n=16)
        blockrun(p0b2)

    def p0c(s):
        sc = tmp2[:, 0:128]
        sp_ = tmp2[:, 128:256]
        mk1 = tmp2[:, 256:384]
        sp2_ = tmp2[:, 384:512]
        mk = tmp2[:, 512:640]
        esub = tmp2[:, 640:768]
        eall = tmp2[:, 768:896]
        m1 = tmp2[:, 896:912]
        m2 = tmp2[:, 912:928]
        nm = tmp2[:, 928:944]
        den = tmp2[:, 944:960]
        rinv = tmp2[:, 976:992]

        def t3(ap2d):
            return ap2d.rearrange("p (t k) -> p t k", k=NE)

        def b3(ap16):
            return ap16.rearrange("p (t o) -> p t o", o=1).broadcast_to([128, 16, NE])

        s.add("sync", dma(convB[0:1, 0:128], bn16[:, :]), inc="L", inc_n=16)
        lv = s.val("L")
        s.add("tensor", lambda e: e.matmul(out=pG[:, 0:128], lhsT=onesrow[:, :],
                                           rhs=convB[0:1, 0:128], start=True,
                                           stop=True), waits=[("L", lv)], inc="T")
        s.add("vector", cp(convA[:, 0:128], pG[:, 0:128]),
              waits=[("T", s.val("T"))], inc="V")
        sc_done = {}
        for tt in range(16):
            p = pAB[tt % 2]
            w = [("W", 32)]
            pv = sc_done.get(tt - 2)
            if pv:
                w.append(("V", pv))
            for k in range(KD):
                s.add("tensor", lambda e, k=k, tt=tt, p=p: e.matmul(
                    out=p[:, 0:NE],
                    lhsT=xnT[:, k * TH + HALO + tt * 128: k * TH + HALO + (tt + 1) * 128],
                    rhs=wrsb[:, k * NE:(k + 1) * NE],
                    start=(k == 0), stop=(k == KD - 1)),
                    waits=w if k == 0 else (), inc="T")
            tv = s.val("T")
            s.add("vector", lambda e, p=p, tt=tt: e.tensor_add(
                out=sc[:, tt * 8:(tt + 1) * 8], in0=p[:, 0:NE], in1=brbc[:, :]),
                waits=[("T", tv)], inc="V")
            sc_done[tt] = s.val("V")
        s.add("vector", lambda e: e.tensor_add(out=sp_, in0=sc, in1=convA[:, 0:128]), inc="V")
        s.add("vector", lambda e: e.tensor_reduce(out=m1, in_=t3(sp_), axis=AX,
                                                  op=ALU.max), inc="V")
        s.add("vector", lambda e: e.tensor_tensor(out=t3(mk1), in0=t3(sp_),
                                                  in1=b3(m1), op=ALU.is_ge), inc="V")
        s.add("vector", lambda e: e.scalar_tensor_tensor(
            out=sp2_, in0=mk1, scalar=-1e30, in1=sp_, op0=ALU.mult, op1=ALU.add), inc="V")
        s.add("vector", lambda e: e.tensor_reduce(out=m2, in_=t3(sp2_), axis=AX,
                                                  op=ALU.max), inc="V")
        s.add("vector", lambda e: e.tensor_tensor(out=t3(esub), in0=t3(sp2_),
                                                  in1=b3(m2), op=ALU.is_ge), inc="V")
        s.add("vector", lambda e: e.tensor_add(out=mk, in0=mk1, in1=esub), inc="V")
        s.add("vector", lambda e: e.tensor_reduce(out=nm, in_=t3(sc), axis=AX,
                                                  op=ALU.max), inc="V")
        s.add("vector", lambda e: e.tensor_tensor(out=t3(esub), in0=t3(sc),
                                                  in1=b3(nm), op=ALU.subtract), inc="V")
        vv = s.val("V")
        s.add("scalar", lambda e: e.activation(out=eall, in_=esub, func=AF.Exp),
              waits=[("V", vv)], inc="S")
        sv = s.val("S")
        s.add("vector", lambda e: e.tensor_mul(out=esub, in0=eall, in1=mk),
              waits=[("S", sv)], inc="V")
        s.add("vector", lambda e: e.tensor_reduce(out=den, in_=t3(esub), axis=AX,
                                                  op=ALU.add), inc="V")
        vv = s.val("V")
        s.add("scalar", lambda e: e.activation(out=tmp2[:, 960:976], in_=den,
                                               func=AF.Ln), waits=[("V", vv)], inc="S")
        s.add("scalar", lambda e: e.activation(out=rinv, in_=tmp2[:, 960:976],
                                               func=AF.Exp, scale=-1.0), inc="S")
        sv = s.val("S")
        s.add("vector", lambda e: e.tensor_tensor(
            out=ew.ap().rearrange("p (t k) -> p t k", k=NE), in0=t3(esub),
            in1=b3(rinv), op=ALU.mult), waits=[("S", sv)], inc="V")
        if debug:
            s.add("sync", dma(dbg["dbg_sm"][:, :], mk[:, 0:48]),
                  waits=[("V", s.val("V"))], inc="L", inc_n=16)
            s.add("sync", dma(dbg["dbg_ew"][:, :], ew[:, :]), inc="L", inc_n=16)
            s.add("vector", lambda e: e.engine_nop(),
                  waits=[("L", s.val("L"))], inc="V")
        for tt in range(16):
            vv = s.val("V")
            s.add("tensor", lambda e, tt=tt: e.transpose(
                out=pG[0:8, 0:128], in_=ew[:, tt * 8:(tt + 1) * 8],
                identity=idf[:, :]), waits=[("V", vv)], inc="T")
            s.add("vector", lambda e, tt=tt: e.tensor_copy(
                out=ewT2[:, tt * 128:(tt + 1) * 128], in_=pG[0:8, 0:128]),
                waits=[("T", s.val("T"))], inc="V")

    blockrun(p0c)

    ssm_chunks = [(0, 512), (512, 1024), (1024, 1536), (1536, 2048), (2048, 2052)]

    def p1a(s):
        evac_done = {}
        cast_done = {}
        silu_done = {}
        pool_done = {}
        win_done = {}
        wstk_w = {}
        win_done[0] = 48
        wstk_w[0] = 64

        def emit_sdbc(ct):
            for q in range(4):
                qc = q * 512
                psd = [pC, pD, pE, pF][q]
                w = [("W", cast_done[(ct, q)])]
                if q == 0:
                    w.append(("W", wstk_w[ct]))
                s.add("tensor", lambda e, ct=ct, qc=qc, psd=psd: e.matmul(
                    out=psd[0:64, 0:512],
                    lhsT=wstk[:, (ct % 2) * 64:(ct % 2) * 64 + 64],
                    rhs=xconvT[:, ct * T + qc:ct * T + qc + 512],
                    start=(ct == 0), stop=(ct == NCT - 1)),
                    waits=w, inc="T")

        for ct in range(NCT):
            if ct + 1 < NCT:
                wt = s.val("T") if ct >= 1 else 0
                c0 = (ct + 1) * 128
                s.add("gpsimd", dma(
                    winst.ap()[:, ((ct + 1) % 2) * 1024:((ct + 1) % 2 + 1) * 1024].rearrange("p (k n) -> p k n", k=KD),
                    winT.ap()[:, c0:c0 + 128].rearrange("(k p) n -> p k n", p=128)),
                    waits=[("T", wt)], inc="W", inc_n=16)
                win_done[ct + 1] = s.val("W")
            for ci, (c0, c1) in enumerate(ssm_chunks):
                p = pAB[ci % 2]
                w = [("W", win_done[ct])]
                if ci >= 2:
                    prev = evac_done.get((ct, ci - 2))
                else:
                    prev = evac_done.get((ct - 1, 4 if ci == 0 else 3))
                if prev:
                    w.append(("V", prev))
                for k in range(KD):
                    s.add("tensor", lambda e, k=k, ct=ct, c0=c0, c1=c1, p=p: e.matmul(
                        out=p[:, 0:c1 - c0],
                        lhsT=winst[:, (ct % 2) * 1024 + k * 128:(ct % 2) * 1024 + (k + 1) * 128],
                        rhs=xnT[:, k * TH + c0:k * TH + c1],
                        start=(k == 0), stop=(k == KD - 1)),
                        waits=w if k == 0 else (), inc="T")
                tv = s.val("T")
                wev = [("T", tv)]
                if ci == 0 and (ct - 1, 3) in pool_done:
                    wev.append(("W", pool_done[(ct - 1, 3)]))
                s.add("vector", lambda e, c0=c0, c1=c1, p=p: e.tensor_copy(
                    out=xs[:, c0:c1], in_=p[:, 0:c1 - c0]),
                    waits=wev, inc="V")
                evac_done[(ct, ci)] = s.val("V")
            if ct >= 1:
                emit_sdbc(ct - 1)
            if ct + 1 < NCT:
                s.add("gpsimd", dma(
                    wstk[:, ((ct + 1) % 2) * 64:((ct + 1) % 2 + 1) * 64],
                    wstack[(ct + 1) * 128:(ct + 2) * 128, :]),
                    waits=[("T", s.val("T"))], inc="W", inc_n=16)
                wstk_w[ct + 1] = s.val("W")
            qc3 = 3 * 512
            w0 = [("V", evac_done[(ct, 4)])]
            wprev = silu_done.get((ct - 1, 3))
            if wprev:
                w0.append(("S", wprev))
            s.add("gpsimd", lambda e, ct=ct, qc=qc3: e.tensor_scalar(
                out=convA2[:, 0:512], in0=xs[:, qc + 1:qc + 513],
                scalar1=cwsb[:, ct * 4:ct * 4 + 1], scalar2=None,
                op0=ALU.mult), waits=w0, inc="W")
            for tap in (1, 2, 3):
                s.add("gpsimd", lambda e, ct=ct, qc=qc3, tap=tap: e.tensor_scalar(
                    out=convB2[:, 0:512], in0=xs[:, qc + 1 + tap:qc + 513 + tap],
                    scalar1=cwsb[:, ct * 4 + tap:ct * 4 + tap + 1], scalar2=None,
                    op0=ALU.mult), inc="W")
                s.add("gpsimd", lambda e: e.tensor_add(
                    out=convA2[:, 0:512], in0=convA2[:, 0:512],
                    in1=convB2[:, 0:512]), inc="W")
            s.add("gpsimd", lambda e, ct=ct: e.tensor_tensor(
                out=convB2[:, 0:512], in0=convA2[:, 0:512],
                in1=cbsb[:, ct:ct + 1].broadcast_to([128, 512]),
                op=ALU.add), inc="W")
            pool_done[(ct, 3)] = s.val("W")
            for q in range(4):
                qc = q * 512
                sa = saA if q % 2 == 0 else saB
                if q < 3:
                    w0 = [("V", evac_done[(ct, 4)])]
                    wprev = silu_done.get((ct, q - 1)) or silu_done.get((ct - 1, 2))
                    if wprev:
                        w0.append(("S", wprev))
                    s.add("vector", lambda e, ct=ct, qc=qc: e.tensor_scalar(
                        out=convA[:, 0:512], in0=xs[:, qc + 1:qc + 513],
                        scalar1=cwsb[:, ct * 4:ct * 4 + 1], scalar2=cbsb[:, ct:ct + 1],
                        op0=ALU.mult, op1=ALU.add), waits=w0, inc="V")
                    s.add("vector", lambda e, ct=ct, qc=qc: e.scalar_tensor_tensor(
                        out=convB[:, 0:512], in0=xs[:, qc + 2:qc + 514],
                        scalar=cwsb[:, ct * 4 + 1:ct * 4 + 2], in1=convA[:, 0:512],
                        op0=ALU.mult, op1=ALU.add), inc="V")
                    s.add("vector", lambda e, ct=ct, qc=qc: e.scalar_tensor_tensor(
                        out=convA[:, 0:512], in0=xs[:, qc + 3:qc + 515],
                        scalar=cwsb[:, ct * 4 + 2:ct * 4 + 3], in1=convB[:, 0:512],
                        op0=ALU.mult, op1=ALU.add), inc="V")
                    s.add("vector", lambda e, ct=ct, qc=qc: e.scalar_tensor_tensor(
                        out=convB[:, 0:512], in0=xs[:, qc + 4:qc + 516],
                        scalar=cwsb[:, ct * 4 + 3:ct * 4 + 4], in1=convA[:, 0:512],
                        op0=ALU.mult, op1=ALU.add), inc="V")
                    conv_wait = ("V", s.val("V"))
                    cBr = convB
                else:
                    conv_wait = ("W", pool_done[(ct, 3)])
                    cBr = convB2
                wsa = [conv_wait]
                pcq = {0: (ct - 1, 2), 1: (ct - 1, 3), 2: (ct, 0), 3: (ct, 1)}[q]
                pc = cast_done.get(pcq)
                if pc:
                    wsa.append(("W", pc))
                s.add("scalar", lambda e, sa=sa, cBr=cBr: e.activation(
                    out=sa[:, 0:512], in_=cBr[:, 0:512], func=AF.Silu),
                    waits=wsa, inc="S")
                silu_done[(ct, q)] = s.val("S")
                sv = s.val("S")
                s.add("gpsimd", lambda e, ct=ct, qc=qc, sa=sa: e.tensor_copy(
                    out=xconvT[:, ct * T + qc:ct * T + qc + 512], in_=sa[:, 0:512]),
                    waits=[("S", sv)], inc="W")
                cast_done[(ct, q)] = s.val("W")
        emit_sdbc(NCT - 1)
        if debug:
            s.add("gpsimd", dma(dbg["dbg_xconv"][:, :], xconvT[:, 0:T]),
                  waits=[("V", s.val("V"))], inc="W", inc_n=16)

    blockrun(p1a)

    def p1b(s):
        tv0 = s.val("T")
        for q, psd in enumerate((pC, pD, pE, pF)):
            s.add("vector", lambda e, q=q, psd=psd: e.tensor_copy(
                out=sdbc[0:64, q * 512:(q + 1) * 512], in_=psd[0:64, 0:512]),
                waits=[("T", tv0)], inc="V")
        marks["p1b_done"] = s.val("V")
        vv = s.val("V")
        for c in range(NCH):
            s.add("tensor", lambda e, c=c: e.transpose(
                out=pG[:, 0:64], in_=sdbc[0:64, c * 128:(c + 1) * 128],
                identity=idf[0:64, 0:64]), waits=[("V", s.val("V"))], inc="T")
            tv = s.val("T")
            s.add("vector", lambda e, c=c: e.tensor_copy(
                out=dpre[:, c * DS:(c + 1) * DS], in_=pG[:, 0:DS]),
                waits=[("T", tv)], inc="V")
            s.add("vector", lambda e, c=c: e.tensor_copy(
                out=sBs[:, c * DS:(c + 1) * DS], in_=pG[:, DS:2 * DS]), inc="V")
            s.add("vector", lambda e, c=c: e.tensor_copy(
                out=sCs[:, c * DS:(c + 1) * DS], in_=pG[:, 2 * DS:3 * DS]), inc="V")
            s.add("vector", lambda e, c=c: e.tensor_copy(
                out=xavg[:, c:c + 1], in_=pG[:, 3 * DS:3 * DS + 1]), inc="V")


    def p1c(s):
        V, S_, Tn = "V", "S", "T"
        for c in range(NCH):
            s.add("vector", lambda e, c=c: e.tensor_add(
                out=sT1[:, c * DS:(c + 1) * DS], in0=dpre[:, c * DS:(c + 1) * DS],
                in1=dbbc[:, :]), inc="V")
        vv = s.val("V")
        s.add("scalar", lambda e: e.activation(out=sT2[:, :], in_=sT1[:, :],
                                               func=AF.Abs), waits=[("V", vv)], inc="S")
        s.add("scalar", lambda e: e.activation(out=dpre[:, :], in_=sT2[:, :],
                                               func=AF.Exp, scale=-1.0), inc="S")
        s.add("scalar", lambda e: e.activation(out=sT2[:, :], in_=dpre[:, :],
                                               func=AF.Ln, bias=1.0), inc="S")
        sv = s.val("S")
        s.add("vector", lambda e: e.tensor_scalar(
            out=dpre[:, :], in0=sT1[:, :], scalar1=0.0, scalar2=None,
            op0=ALU.max), waits=[("S", sv)], inc="V")
        s.add("vector", lambda e: e.tensor_add(
            out=sT1[:, :], in0=dpre[:, :], in1=sT2[:, :]), inc="V")
        for c in range(NCH):
            s.add("vector", lambda e, c=c: e.tensor_mul(
                out=sG[:, c * DS:(c + 1) * DS], in0=sT1[:, c * DS:(c + 1) * DS],
                in1=abc[:, :]), inc="V")
        for c in range(NCH):
            s.add("vector", lambda e, c=c: e.tensor_mul(
                out=sT2[:, c * DS:(c + 1) * DS], in0=sT1[:, c * DS:(c + 1) * DS],
                in1=sBs[:, c * DS:(c + 1) * DS]), inc="V")
        for c in range(NCH):
            s.add("vector", lambda e, c=c: e.tensor_scalar(
                out=sBs[:, c * DS:(c + 1) * DS], in0=sT2[:, c * DS:(c + 1) * DS],
                scalar1=xavg[:, c:c + 1], scalar2=None, op0=ALU.mult), inc="V")
        dbg_l = None
        if debug:
            s.add("sync", dma(dbg["dbg_delta"][:, :], sT1[:, :]),
                  waits=[("V", s.val("V"))], inc="L", inc_n=16)
            s.add("sync", dma(dbg["dbg_g"][:, :], sG[:, :]),
                  waits=[("V", s.val("V"))], inc="L", inc_n=16)
            dbg_l = s.val("L")
        vv = s.val("V")
        s.add("tensor", lambda e: e.matmul(out=pG[0:1, 0:256], lhsT=trit[:, 127:128],
                                           rhs=sG[:, :], start=True, stop=True),
              waits=[("V", vv)], inc="T")
        s.add("vector", cp(glr[:, :], pG[0:1, 0:256]), waits=[("T", s.val("T"))], inc="V")
        for c in range(NCH):
            p = pAB[c % 2]
            wv = s.val("V")
            s.add("tensor", lambda e, c=c, p=p: e.matmul(
                out=p[:, 0:DS], lhsT=trit[:, :], rhs=sG[:, c * DS:(c + 1) * DS],
                start=True, stop=True), waits=[("V", wv)], inc="T")
            wcp = [("T", s.val("T"))]
            if c == 0 and dbg_l:
                wcp.append(("L", dbg_l))
            s.add("vector", lambda e, c=c, p=p: e.tensor_copy(
                out=sG[:, c * DS:(c + 1) * DS], in_=p[:, 0:DS]),
                waits=wcp, inc="V")
        vv = s.val("V")
        s.add("scalar", lambda e: e.activation(out=sT2[:, :], in_=sG[:, :],
                                               func=AF.Exp), waits=[("V", vv)], inc="S")
        sv = s.val("S")
        s.add("vector", lambda e: e.tensor_mul(out=sA_[:, :], in0=sCs[:, :],
                                               in1=sT2[:, :]), waits=[("S", sv)], inc="V")
        s.add("scalar", lambda e: e.activation(out=sT1[:, :], in_=sG[:, :],
                                               func=AF.Exp, scale=-1.0), inc="S")
        sv = s.val("S")
        s.add("vector", lambda e: e.tensor_mul(out=sG[:, :], in0=sT1[:, :],
                                               in1=sBs[:, :]), waits=[("S", sv)], inc="V")
        vv = s.val("V")
        s.add("tensor", lambda e: e.matmul(out=pG[0:1, 0:256], lhsT=trit[:, 127:128],
                                           rhs=sG[:, :], start=True, stop=True),
              waits=[("V", vv)], inc="T")
        s.add("vector", cp(blr[:, :], pG[0:1, 0:256]), waits=[("T", s.val("T"))], inc="V")
        for c in range(NCH):
            p = pAB[c % 2]
            wv = s.val("V")
            s.add("tensor", lambda e, c=c, p=p: e.matmul(
                out=p[:, 0:DS], lhsT=trit[:, :], rhs=sG[:, c * DS:(c + 1) * DS],
                start=True, stop=True), waits=[("V", wv)], inc="T")
            s.add("vector", lambda e, c=c, p=p: e.tensor_copy(
                out=sT1[:, c * DS:(c + 1) * DS], in_=p[:, 0:DS]),
                waits=[("T", s.val("T"))], inc="V")
        s.add("vector", lambda e: e.tensor_mul(out=sT2[:, :], in0=sA_[:, :],
                                               in1=sT1[:, :]), inc="V")
        for c in range(NCH):
            s.add("vector", lambda e, c=c: e.reduce_sum(
                out=y_col[:, c:c + 1], in_=sT2[:, c * DS:(c + 1) * DS], axis=AX), inc="V")
        if debug:
            s.add("sync", dma(dbg["dbg_a"][:, :], sA_[:, :]),
                  waits=[("V", s.val("V"))], inc="L", inc_n=16)
            s.add("sync", dma(dbg["dbg_bcum"][:, :], sT1[:, :]), inc="L", inc_n=16)
            s.add("sync", dma(dbg["dbg_yloc"][:, :], y_col[:, :]), inc="L", inc_n=16)
            dbg_p1c_l = s.val("L")
            s.add("vector", lambda e: e.engine_nop(),
                  waits=[("L", dbg_p1c_l)], inc="V")
        s.add("vector", lambda e: e.memset(gcum[:, 0:DS], 0.0), inc="V")
        s.add("vector", lambda e: e.memset(hrow[:, 0:DS], 0.0), inc="V")
        for c in range(NCH):
            s.add("vector", lambda e, c=c: e.tensor_add(
                out=gcum[:, (c + 1) * DS:(c + 2) * DS],
                in0=gcum[:, c * DS:(c + 1) * DS],
                in1=glr[:, c * DS:(c + 1) * DS]), inc="V")
            vv = s.val("V")
            s.add("scalar", lambda e, c=c: e.activation(
                out=erow[:, :], in_=glr[:, c * DS:(c + 1) * DS], func=AF.Exp),
                waits=[("V", vv)], inc="S")
            s.add("scalar", lambda e: e.activation(
                out=xld[:, 256:384], in_=xld[:, 128:256], func=AF.Identity),
                inc="S")
            sv = s.val("S")
            s.add("vector", lambda e, c=c: e.tensor_mul(
                out=wrow[:, :], in0=erow[:, :], in1=blr[:, c * DS:(c + 1) * DS]),
                waits=[("S", sv)], inc="V")
            s.add("vector", lambda e, c=c: e.tensor_mul(
                out=arow_r[:, :], in0=erow[:, :], in1=hrow[:, c * DS:(c + 1) * DS]), inc="V")
            s.add("vector", lambda e, c=c: e.tensor_add(
                out=hrow[:, (c + 1) * DS:(c + 2) * DS], in0=arow_r[:, :],
                in1=wrow[:, :]), inc="V")
        vv = s.val("V")
        s.add("tensor", lambda e: e.matmul(out=pG[0:8, 0:DS], lhsT=ohsb[:, :],
                                           rhs=gcum[:, NCH * DS:(NCH + 1) * DS],
                                           start=True, stop=True),
              waits=[("V", vv)], inc="T")
        s.add("tensor", lambda e: e.matmul(out=pG[0:8, DS:2 * DS], lhsT=ohsb[:, :],
                                           rhs=hrow[:, NCH * DS:(NCH + 1) * DS],
                                           start=True, stop=True), inc="T")
        s.add("vector", cp(gput[:, :], pG[0:8, 0:32]), waits=[("T", s.val("T"))], inc="V")
        vv = s.val("V")
        s.add("gpsimd", dma(cc_buf[:, :], gput[:, :]), waits=[("V", vv)], inc="W", inc_n=16)
        cw = s.val("W")
        s.add("gpsimd", lambda e: e.collective_compute(
            "AllReduce", ALU.add, replica_groups=[list(range(NCORE))],
            ins=[cc_buf.ap()], outs=[cc_buf.ap()]),
            waits=[("W", cw)], inc="C", inc_n=1)


    def _prefetch_gate(s, g, wait_t):
        ct = NCT + g
        c0g = ct * 128
        s.add("gpsimd", dma(
            winst.ap()[:, (ct % 2) * 1024:(ct % 2 + 1) * 1024].rearrange("p (k n) -> p k n", k=KD),
            winT.ap()[:, c0g:c0g + 128].rearrange("(k p) n -> p k n", p=128)),
            waits=[("T", wait_t)] if wait_t else (), inc="W", inc_n=16)
        marks.setdefault("win_done", {})[g] = s.val("W")

    def prefetch_gate_early(s):
        _prefetch_gate(s, 0, 0)
        _prefetch_gate(s, 1, 0)

    def p1d(s):
        win_done = marks["win_done"]
        silu_done = {}

        def prefetch_gate(g, wait_t):
            _prefetch_gate(s, g, wait_t)

        for g in range(NCT):
            for ci in range(4):
                p = (pC, pD, pE, pF)[ci]
                w = [("W", win_done[g])]
                prev = silu_done.get((g - 1, ci))
                if prev:
                    w.append(("S", prev))
                elif g == 0:
                    w.append(("V", marks["p1b_done"]))
                ct = NCT + g
                c0, c1 = HALO + ci * 512, HALO + (ci + 1) * 512
                for k in range(KD):
                    s.add("tensor", lambda e, k=k, ct=ct, c0=c0, c1=c1, p=p: e.matmul(
                        out=p[:, 0:512],
                        lhsT=winst[:, (ct % 2) * 1024 + k * 128:(ct % 2) * 1024 + (k + 1) * 128],
                        rhs=xnT[:, k * TH + c0:k * TH + c1],
                        start=(k == 0), stop=(k == KD - 1)),
                        waits=w if k == 0 else (), inc="T")
                tv = s.val("T")
                s.add("scalar", lambda e, g=g, ci=ci, p=p: e.activation(
                    out=sgT[:, g * T + ci * 512:g * T + (ci + 1) * 512],
                    in_=p[:, 0:512], func=AF.Silu), waits=[("T", tv)], inc="S")
                silu_done[(g, ci)] = s.val("S")
            if g + 2 < NCT:
                prefetch_gate(g + 2, s.val("T"))
        if debug:
            s.add("gpsimd", dma(dbg["dbg_sg"][:, :], sgT[:, 0:T]),
                  waits=[("S", s.val("S"))], inc="W", inc_n=16)


    def p1e(s):
        def vflush(tag=""):
            s.add("vector", lambda e: e.tensor_copy(out=xld[:, 128:256],
                                                    in_=xld[:, 0:128]), inc="V")
            return s.val("V")

        def sflush():
            s.add("scalar", lambda e: e.activation(out=xld[:, 256:384],
                                                   in_=xld[:, 128:256],
                                                   func=AF.Identity), inc="S")
            return s.val("S")

        cw = s.val("C")
        s.add("gpsimd", dma(su8[:, :], cc_buf[:, :]), waits=[("C", cw)],
              inc="W", inc_n=16)
        cw2 = s.val("W")
        if debug:
            s.add("sync", dma(dbg["dbg_su"][:, :], su8[:, :]), waits=[("W", cw2)],
                  inc="L", inc_n=16)
        s.add("tensor", lambda e: e.matmul(out=pA[0:8, 0:DS], lhsT=wmsb[:, :],
                                           rhs=su8[0:8, 0:DS], start=True, stop=True),
              waits=[("W", cw2)], inc="T")
        tv = s.val("T")
        s.add("scalar", lambda e: e.activation(out=e8[:, :], in_=pA[0:8, 0:DS],
                                               func=AF.Exp), waits=[("T", tv)], inc="S")
        sv = sflush()
        s.add("vector", lambda e: e.tensor_mul(out=ctrb[:, :], in0=e8[:, :],
                                               in1=su8[0:8, DS:2 * DS]),
              waits=[("S", sv)], inc="V")
        vv = vflush()
        s.add("tensor", lambda e: e.matmul(out=pB[0:1, 0:DS], lhsT=mvsb[:, :],
                                           rhs=ctrb[:, :], start=True, stop=True),
              waits=[("V", vv)], inc="T")
        s.add("vector", cp(hin[:, :], pB[0:1, 0:DS]), waits=[("T", s.val("T"))], inc="V")
        e_row = sdbc[0:1, 1024:1280]
        w_all = sdbc[0:1, 1280:1536]
        vv = s.val("V")
        s.add("scalar", lambda e: e.activation(out=e_row[:, :],
                                               in_=gcum[:, 0:NCH * DS],
                                               func=AF.Exp), waits=[("V", vv)], inc="S")
        sv = s.val("S")
        hin3 = hin.rearrange("p (o d) -> p o d", o=1).broadcast_to([1, NCH, DS])
        s.add("vector", lambda e: e.tensor_tensor(
            out=w_all.rearrange("p (t d) -> p t d", d=DS),
            in0=e_row.rearrange("p (t d) -> p t d", d=DS),
            in1=hin3, op=ALU.mult), waits=[("S", sv)], inc="V")
        vflush()
        s.add("vector", lambda e: e.tensor_add(out=w_all[:, :], in0=w_all[:, :],
                                               in1=hrow[:, 0:NCH * DS]), inc="V") \
            if False else None
        s.add("vector", lambda e: e.tensor_tensor(
            out=sdbc[0:1, 1536:1792], in0=w_all[:, :], in1=hrow[:, 0:NCH * DS],
            op=ALU.add), inc="V")
        vv = vflush()
        if debug:
            s.add("sync", dma(dbg["dbg_w2"][:, :], sdbc[0:1, 1536:1792]),
                  waits=[("V", vv)], inc="L", inc_n=16)
            s.add("sync", dma(dbg["dbg_er"][:, :], e_row[:, :]), inc="L", inc_n=16)
            s.add("sync", dma(dbg["dbg_hr"][:, :], hrow[:, 0:272]), inc="L", inc_n=16)
        s.add("tensor", lambda e: e.matmul(out=pA[:, 0:NCH * DS], lhsT=onesrow[:, :],
                                           rhs=sdbc[0:1, 1536:1792], start=True,
                                           stop=True), waits=[("V", vv)], inc="T")
        marks["sdbc_T"] = s.val("T")
        marks["sdbc_V"] = s.val("V")
        tv = s.val("T")
        s.add("vector", lambda e: e.tensor_mul(out=sT2[:, :], in0=sA_[:, :],
                                               in1=pA[:, 0:NCH * DS]),
              waits=[("T", tv)], inc="V")
        s.add("vector", lambda e: e.tensor_reduce(
            out=xld[:, 384:384 + NCH],
            in_=sT2.ap().rearrange("p (t d) -> p t d", d=DS), axis=AX,
            op=ALU.add), inc="V")
        vflush()
        s.add("vector", lambda e: e.tensor_add(out=yfin[:, :], in0=y_col[:, :],
                                               in1=xld[:, 384:384 + NCH]), inc="V")
        vv = vflush()
        if debug:
            s.add("sync", dma(dbg["dbg_ycol"][:, :], yfin[:, :]),
                  waits=[("V", vv)], inc="L", inc_n=16)
        for c in range(NCH):
            s.add("tensor", lambda e, c=c: e.transpose(
                out=pG[0:1, 0:128], in_=yfin[:, c:c + 1], identity=idf[:, :]),
                waits=[("V", s.val("V"))], inc="T")
            tv = s.val("T")
            s.add("vector", lambda e, c=c: e.tensor_copy(
                out=yrow_stage[:, :], in_=pG[0:1, 0:128]),
                waits=[("T", tv)], inc="V")
            vv = vflush()
            p = pAB[c % 2]
            s.add("tensor", lambda e, c=c, p=p: e.matmul(
                out=p[:, 0:128], lhsT=onesrow[:, :],
                rhs=yrow_stage[:, :], start=True, stop=True),
                waits=[("V", vv)], inc="T")
            tv = s.val("T")
            s.add("scalar", lambda e, c=c, p=p: e.activation(
                out=y_bc[:, c * 128:(c + 1) * 128], in_=p[:, 0:128],
                func=AF.Copy), waits=[("T", tv)], inc="S")
        if debug:
            s.add("gpsimd", dma(dbg["dbg_ysm"][:, :], y_bc[:, 0:NCH]),
                  waits=[("S", s.val("S"))], inc="W", inc_n=16)


    def p2(s):
        first_w = [("S", s.val("S")), ("T", s.val("T"))]
        if debug:
            first_w += [("W", s.val("W")), ("L", s.val("L"))]
        for ct in range(NCT):
            s.add("gpsimd", dma(
                woutst[:, ct * 512:(ct + 1) * 512],
                woutT.ap()[ct * 128:(ct + 1) * 128, 0:512]),
                waits=[("T", marks["sdbc_T"]), ("V", marks["sdbc_V"])] if ct == 0 else (),
                inc="W", inc_n=16)
        for ct in range(NCT):
            for h in range(2):
                c0, c1 = h * 1024, (h + 1) * 1024
                s.add("vector", lambda e, ct=ct, c0=c0, c1=c1: e.tensor_mul(
                    out=tmp2[:, :], in0=sgT[:, ct * T + c0:ct * T + c1],
                    in1=y_bc[:, c0:c1]), waits=first_w if (ct == 0 and h == 0) else (),
                    inc="V")
                s.add("vector", lambda e, ct=ct, c0=c0, c1=c1: e.scalar_tensor_tensor(
                    out=yskT[:, ct * T + c0:ct * T + c1],
                    in0=xconvT[:, ct * T + c0:ct * T + c1],
                    scalar=dsb[:, ct:ct + 1], in1=tmp2[:, :],
                    op0=ALU.mult, op1=ALU.add), inc="V")
        marks["p2_V"] = s.val("V")
        marks["p2_W"] = s.val("W")

    marks = {}

    def p_mid(s):
        prefetch_gate_early(s)
        p1b(s)
        p1c(s)
        p1d(s)
        p1e(s)
        p2(s)
    p_mid.__name__ = "p_mid"
    blockrun(p_mid)

    def p3(s):
        wout_ready = s.val("W")
        evac = {}
        for dc in range(2):
            if dc == 1:
                tv = s.val("T")
                for ct in range(NCT):
                    s.add("gpsimd", dma(
                        woutst[:, ct * 512:(ct + 1) * 512],
                        woutT.ap()[ct * 128:(ct + 1) * 128, 512:1024]),
                        waits=[("T", tv)], inc="W", inc_n=16)
                wout_ready = s.val("W")
            for tt in range(16):
                p = pAB[tt % 2]
                w = []
                pv = evac.get((dc, tt - 2)) or (evac.get((dc - 1, 14 + tt)) if dc else None)
                if pv:
                    w.append(("S", pv))
                s.add("tensor", lambda e, tt=tt, dc=dc, p=p: e.matmul(
                    out=p[:, 0:512],
                    lhsT=ewT2[:, tt * 128:(tt + 1) * 128],
                    rhs=bl2sb[:, dc * 512:(dc + 1) * 512],
                    start=True, stop=False), waits=w, inc="T")
                w2 = [("W", wout_ready)]
                if dc == 0 and tt == 0:
                    w2 += [("V", marks["p2_V"]), ("W", marks["p2_W"])]
                for ct in range(NCT):
                    s.add("tensor", lambda e, ct=ct, tt=tt, p=p: e.matmul(
                        out=p[:, 0:512],
                        lhsT=yskT[:, ct * T + tt * 128:ct * T + (tt + 1) * 128],
                        rhs=woutst[:, ct * 512:(ct + 1) * 512],
                        start=False, stop=(ct == NCT - 1)),
                        waits=w2 if ct == 0 else (), inc="T")
                tv = s.val("T")
                s.add("scalar", lambda e, tt=tt, dc=dc, p=p: e.activation(
                    out=y_acc[:, tt * D + dc * 512:tt * D + (dc + 1) * 512],
                    in_=p[:, 0:512], func=AF.Copy), waits=[("T", tv)], inc="S")
                evac[(dc, tt)] = s.val("S")
        if debug:
            s.add("sync", dma(dbg["dbg_yac"][:, :], y_acc[:, 0:D]),
                  waits=[("S", s.val("S"))], inc="L", inc_n=16)

    blockrun(p3)

    def p4(s):
        from collections import deque
        wl1_done = {}
        wl2_done = {}
        g_done = {}
        acc = {}
        hist = deque([None, None], maxlen=2)
        ew_done = {}
        p5_ready = {}

        def prefetch_ab(n, hp, wait_t):
            par = hp % 2
            for half, col0 in ((0, hp * 128), (1, HID + hp * 128)):
                slot = (2 * par + half) * D
                s.add("gpsimd", dma(
                    wabst.ap()[:, slot:slot + D].rearrange("p (k n) -> p k n", k=KD),
                    wl1.ap()[n, :, col0:col0 + 128].rearrange(
                        "(k p) n -> p k n", p=128)),
                    waits=[("T", wait_t)], inc="W", inc_n=16)
            wl1_done[(n, hp)] = s.val("W")

        def prefetch_wl2(n, hch, wait_t):
            for i in range(8):
                hc = hch * 8 + i
                s.add("gpsimd", dma(
                    wl2st[:, i * D:(i + 1) * D],
                    wl2.ap()[n, hc * 128:(hc + 1) * 128, :]),
                    waits=[("T", wait_t)], inc="W", inc_n=16)
            wl2_done[(n, hch)] = s.val("W")

        def emit_ewbc(n):
            prev_readers = g_done.get((n - 1, NCT - 1, 3))
            for tc in range(4):
                s.add("tensor", lambda e, n=n, tc=tc: e.matmul(
                    out=pG[:, 0:512], lhsT=selsb[:, n * 128:(n + 1) * 128],
                    rhs=ewT2[:, tc * 512:(tc + 1) * 512], start=True, stop=True),
                    waits=[("S", s.val("S"))], inc="T")
                tv = s.val("T")
                wsc = [("T", tv)]
                if prev_readers:
                    wsc.append(("V", prev_readers))
                s.add("scalar", lambda e, tc=tc: e.activation(
                    out=ew_bcT[:, tc * 512:(tc + 1) * 512], in_=pG[:, 0:512],
                    func=AF.Copy), waits=wsc, inc="S")
            ew_done[n] = s.val("S")

        def emit_mm1_half(n, half):
            for hp in range(half * 8, half * 8 + 8):
                par = hp % 2
                asl = (2 * par) * D
                bsl = (2 * par + 1) * D
                for tc in range(4):
                    pa = pCD[tc % 2]
                    pb = pEF[tc % 2]
                    w = [("W", wl1_done[(n, hp)])]
                    if hist[0] is not None:
                        w.append(("S", hist[0][0]))
                        w.append(("V", hist[0][1]))
                    for k in range(KD):
                        s.add("tensor", lambda e, k=k, tc=tc, asl=asl, pa=pa: e.matmul(
                            out=pa[:, 0:512],
                            lhsT=wabst[:, asl + k * 128:asl + (k + 1) * 128],
                            rhs=xnT[:, k * TH + HALO + tc * 512:k * TH + HALO + (tc + 1) * 512],
                            start=(k == 0), stop=(k == KD - 1)),
                            waits=w if k == 0 else (), inc="T")
                    for k in range(KD):
                        s.add("tensor", lambda e, k=k, tc=tc, bsl=bsl, pb=pb: e.matmul(
                            out=pb[:, 0:512],
                            lhsT=wabst[:, bsl + k * 128:bsl + (k + 1) * 128],
                            rhs=xnT[:, k * TH + HALO + tc * 512:k * TH + HALO + (tc + 1) * 512],
                            start=(k == 0), stop=(k == KD - 1)), inc="T")
                    tv = s.val("T")
                    sa = saA if tc % 2 == 0 else saB
                    sb = saB if tc % 2 == 0 else saA
                    wsl = [("T", tv), ("S", ew_done[n])]
                    pv2 = g_done.get((n, hp, tc - 1)) or g_done.get((n, hp - 1, 3)) \
                        or g_done.get((n - 1, NCT - 1, 3))
                    if pv2:
                        wsl.append(("V", pv2))
                    s.add("scalar", lambda e, hp=hp, n=n, pa=pa, sa=sa: e.activation(
                        out=sa[:, :], in_=pa[:, 0:512], func=AF.Silu,
                        bias=bl1sb[:, n * 32 + hp:n * 32 + hp + 1]),
                        waits=wsl, inc="S")
                    silu_s_val = s.val("S")
                    sv = s.val("S")
                    s.add("vector", lambda e, tc=tc, sa=sa, sb=sb: e.tensor_mul(
                        out=sb[:, :], in0=sa[:, :],
                        in1=ew_bcT[:, tc * 512:(tc + 1) * 512]),
                        waits=[("S", sv)], inc="V")
                    s.add("vector", lambda e, n=n, hp=hp, tc=tc, pb=pb, sb=sb:
                          e.scalar_tensor_tensor(
                              out=gT[:, hp * T + tc * 512:hp * T + (tc + 1) * 512],
                              in0=pb[:, 0:512],
                              scalar=bl1sb[:, n * 32 + 16 + hp:n * 32 + 16 + hp + 1],
                              in1=sb[:, :], op0=ALU.add, op1=ALU.mult), inc="V")
                    g_done[(n, hp, tc)] = s.val("V")
                    hist.append((silu_s_val, s.val("V")))
                nxt = (n, hp + 2)
                if hp + 2 >= NCT:
                    nxt = (n + 1, hp + 2 - NCT)
                if nxt[0] < NE:
                    prefetch_ab(nxt[0], nxt[1], s.val("T"))

        def emit_mm2_half(n, hch):
            wl2v = wl2_done[(n, hch)]
            gv = g_done[(n, hch * 8 + 7, 3)]
            for tt in range(16):
                for dc in range(2):
                    p = pAB[(tt * 2 + dc) % 2]
                    w = [("W", wl2v), ("V", gv)]
                    pv = acc.get((n, hch, tt - 1, dc)) or \
                        (acc.get((n, hch - 1, 15, 1)) if hch else
                         acc.get((n - 1, 1, 15, 1)))
                    if pv:
                        w.append(("V", pv))
                    for i in range(8):
                        hc = hch * 8 + i
                        s.add("tensor", lambda e, i=i, hc=hc, tt=tt, dc=dc, p=p:
                              e.matmul(
                                  out=p[:, 0:512],
                                  lhsT=gT[:, hc * T + tt * 128:hc * T + (tt + 1) * 128],
                                  rhs=wl2st[:, i * D + dc * 512:i * D + (dc + 1) * 512],
                                  start=(i == 0), stop=(i == 7)),
                              waits=w if i == 0 else (), inc="T")
                    tv = s.val("T")
                    s.add("vector", lambda e, tt=tt, dc=dc, p=p: e.tensor_add(
                        out=y_acc[:, tt * D + dc * 512:tt * D + (dc + 1) * 512],
                        in0=y_acc[:, tt * D + dc * 512:tt * D + (dc + 1) * 512],
                        in1=p[:, 0:512]), waits=[("T", tv)], inc="V")
                    acc[(n, hch, tt, dc)] = s.val("V")
                    if n == NE - 1 and hch == 1 and dc == 1:
                        p5_ready[tt] = s.val("V")

        prefetch_ab(0, 0, 0)
        prefetch_ab(0, 1, 0)
        emit_ewbc(0)
        emit_mm1_half(0, 0)
        prefetch_wl2(0, 0, 0)
        emit_mm1_half(0, 1)
        emit_mm2_half(0, 0)
        for n in range(1, NE):
            emit_ewbc(n)
            prefetch_wl2(n - 1, 1, s.val("T"))
            emit_mm1_half(n, 0)
            emit_mm2_half(n - 1, 1)
            prefetch_wl2(n, 0, s.val("T"))
            emit_mm1_half(n, 1)
            emit_mm2_half(n, 0)
        prefetch_wl2(NE - 1, 1, s.val("T"))
        emit_mm2_half(NE - 1, 1)
        t_d7 = s.val("T")
        tt_done = {}
        for tt in range(16):
            sl = (tt % 2) * D
            wx = [("T", t_d7)]
            if tt >= 2:
                wx.append(("V", tt_done[tt - 2]))
            s.add("sync", dma(xld[:, sl:sl + D], x_own[tt * 128:(tt + 1) * 128, :]),
                  waits=wx, inc="L", inc_n=16)
            lv = s.val("L")
            s.add("vector", lambda e, tt=tt, sl=sl: e.tensor_add(
                out=y_acc[:, tt * D:(tt + 1) * D],
                in0=y_acc[:, tt * D:(tt + 1) * D], in1=xld[:, sl:sl + D]),
                waits=[("L", lv), ("V", p5_ready[tt])], inc="V")
            tt_done[tt] = s.val("V")
            vv = s.val("V")
            s.add("sync", dma(out[tt * 128:(tt + 1) * 128, :],
                              y_acc[:, tt * D:(tt + 1) * D]),
                  waits=[("V", vv)], inc="L", inc_n=16)
        lv = s.val("L")
        s.add("sync", lambda e: e.nop(), waits=[("L", lv)])

    blockrun(p4)


    return nc, sbuf_used


def _prep_inputs(inputs):
    f32 = np.float32
    x = np.ascontiguousarray(inputs["L0_attn_x"], dtype=f32).reshape(2 * 8192, D)
    winT_h = np.ascontiguousarray(inputs["L0_attn_in_proj_w"].T, dtype=f32)
    wstack_h = np.zeros((DI, 64), dtype=f32)
    wstack_h[:, 0:16] = inputs["L0_attn_delta_w"].T
    wstack_h[:, 16:32] = inputs["L0_attn_B_w"].T
    wstack_h[:, 32:48] = inputs["L0_attn_C_w"].T
    wstack_h[:, 48] = 1.0 / DI
    cw = np.ascontiguousarray(inputs["L0_attn_conv_weight"], dtype=f32)
    conv_w_h = np.zeros((128, 64), dtype=f32)
    for ct in range(NCT):
        conv_w_h[:, ct * 4:(ct + 1) * 4] = cw[ct * 128:(ct + 1) * 128, :]
    conv_b_h = inputs["L0_attn_conv_bias"].astype(f32).reshape(NCT, 128).T.copy()
    d_col_h = inputs["L0_attn_D"].astype(f32).reshape(NCT, 128).T.copy()
    a_log_h = inputs["L0_attn_A_log"].astype(f32).reshape(1, DS)
    delta_b_h = inputs["L0_attn_delta_b"].astype(f32).reshape(1, DS)
    woutT_h = np.ascontiguousarray(inputs["L0_attn_out_proj_w"].T, dtype=f32)
    wrT_h = np.ascontiguousarray(inputs["L0_ffn_Wr"].T, dtype=f32)
    br_h = inputs["L0_ffn_br"].astype(f32).reshape(1, NE)
    bn_h = inputs["L0_ffn_biases_N"].astype(f32).reshape(1, NE)
    wl1_h = np.ascontiguousarray(inputs["L0_ffn_Wl1"], dtype=f32)
    bl1 = inputs["L0_ffn_bl1"].astype(f32)
    bl1c_h = np.zeros((128, 256), dtype=f32)
    for n in range(NE):
        bl1c_h[:, n * 32:(n + 1) * 32] = bl1[n].reshape(32, 128).T
    wl2_h = np.ascontiguousarray(inputs["L0_ffn_Wl2"], dtype=f32)
    bl2_h = np.ascontiguousarray(inputs["L0_ffn_bl2"], dtype=f32)

    maps = []
    for c in range(NCORE):
        t0 = c * T
        chain_pos = c % 4
        halo = (np.zeros((HALO, D), dtype=f32) if chain_pos == 0
                else x[t0 - HALO:t0].copy())
        oh = np.zeros((1, NCORE), dtype=f32)
        oh[0, c] = 1.0
        wmT = np.zeros((NCORE, NCORE), dtype=f32)
        mv = np.zeros((NCORE, 1), dtype=f32)
        chain = range((c // 4) * 4, (c // 4) * 4 + 4)
        for cp in chain:
            if cp < c:
                mv[cp, 0] = 1.0
                for j in chain:
                    if cp < j < c:
                        wmT[j, cp] = 1.0
        maps.append({
            "x_own": x[t0:t0 + T].copy(), "x_halo": halo,
            "winT": winT_h, "wstack": wstack_h, "conv_w": conv_w_h,
            "conv_b": conv_b_h, "d_col": d_col_h, "a_log": a_log_h,
            "delta_b": delta_b_h, "woutT": woutT_h, "wrT": wrT_h,
            "br_r": br_h, "bn_r": bn_h, "bn16": np.tile(bn_h, (1, 16)), "wl1": wl1_h, "bl1c": bl1c_h,
            "wl2": wl2_h, "bl2": bl2_h, "oh": oh, "wmaskT": wmT, "mvec": mv,
        })
    return maps


_NC_CACHE = {}


def kernel(**inputs):
    key = "main"
    if key not in _NC_CACHE:
        _NC_CACHE[key] = build(debug=False)
    nc, _ = _NC_CACHE[key]
    maps = _prep_inputs(inputs)
    res = run_bass_kernel_spmd(nc, maps, core_ids=list(range(NCORE)))
    outs = [res.results[c]["out"] for c in range(NCORE)]
    return np.concatenate(outs, axis=0).reshape(2, 8192, D).astype(np.float32)


def check_deadlock():
    sem = {k: 0 for k in ("L", "W", "T", "V", "S", "C")}
    for bname, ops in SCHEDS:
        ptr = {e: 0 for e in ENGS}
        stuck = False
        while True:
            progress = False
            for e in ENGS:
                while ptr[e] < len(ops[e]):
                    waits, inc, n = ops[e][ptr[e]]
                    if all(sem[s] >= v for (s, v) in waits):
                        if inc:
                            sem[inc] += n
                        ptr[e] += 1
                        progress = True
                    else:
                        break
            if all(ptr[e] == len(ops[e]) for e in ENGS):
                break
            if not progress:
                stuck = True
                print(f"DEADLOCK in block {bname}:")
                for e in ENGS:
                    if ptr[e] < len(ops[e]):
                        waits, inc, n = ops[e][ptr[e]]
                        unmet = [(s, v, sem[s]) for (s, v) in waits if sem[s] < v]
                        print(f"  {e} op#{ptr[e]}/{len(ops[e])} blocked on {unmet}")
                break
        if stuck:
            return False
    print("schedule: no deadlock")
    return True


if __name__ == "__main__":
    nc, used = build(debug=True)
    print(f"sbuf used: {used} bytes/partition")
    print(f"instructions: {nc.n_instructions()}")
    check_deadlock()


# version 10
# speedup vs baseline: 1.1720x; 1.1720x over previous
import numpy as np
import ml_dtypes

import concourse.bass as bass
import concourse.mybir as mybir
from concourse.bass_utils import run_bass_kernel_spmd

F32 = mybir.dt.float32
BF16 = mybir.dt.bfloat16
AF = mybir.ActivationFunctionType
ALU = mybir.AluOpType
AX = mybir.AxisListType.X
BF = ml_dtypes.bfloat16

T = 2048
HALO = 4
TH = T + HALO
D = 1024
KD = 8
DI = 2048
NCT = 16
DS = 16
Q = 128
NCH = 16
NE = 8
HID = 2048
NCORE = 8

ENGS = ("sync", "gpsimd", "tensor", "vector", "scalar")

SCHEDS = []


class Sched:

    def __init__(self, nc, sems):
        self.nc = nc
        self.ops = {e: [] for e in ENGS}
        self.sems = sems

    def add(self, eng, fn, waits=(), inc=None, inc_n=1):
        w = [(s, v) for (s, v) in waits]
        self.ops[eng].append((fn, w, inc, inc_n))
        if inc is not None:
            self.sems[inc][1] += inc_n
        return self.sems[inc][1] if inc is not None else None

    def val(self, sem):
        return self.sems[sem][1]

    def emit(self, block):
        for eng_name in ENGS:
            ops = self.ops[eng_name]
            if not ops:
                continue

            def body(eng, ops=ops, eng_name=eng_name):
                maxw = {}
                for fn, waits, inc, inc_n in ops:
                    for s, v in waits:
                        if v <= 0:
                            continue
                        if maxw.get(s, -1) >= v:
                            continue
                        eng.wait_ge(self.sems[s][0], v)
                        maxw[s] = v
                    instr = fn(eng)
                    if inc is not None:
                        instr.then_inc(self.sems[inc][0], inc_n)
            getattr(block, eng_name)(body)


def build(debug=False):
    nc = bass.Bass("TRN2", target_bir_lowering=False, num_devices=NCORE,
                   detect_race_conditions=False)

    def ein(name, shape):
        return nc.dram_tensor(name, list(shape), F32, kind="ExternalInput")

    x_own = ein("x_own", (T, D))
    x_halo = ein("x_halo", (HALO, D))
    winT = ein("winT", (D, 2 * DI))
    wstack = ein("wstack", (DI, 64))
    conv_w = ein("conv_w", (128, 64))
    conv_b = ein("conv_b", (128, NCT))
    d_col = ein("d_col", (128, NCT))
    a_log = ein("a_log", (1, DS))
    delta_b = ein("delta_b", (1, DS))
    woutT = ein("woutT", (DI, D))
    wrT = ein("wrT", (D, NE))
    br_r = ein("br_r", (1, NE))
    bn_r = ein("bn_r", (1, NE))
    bn16 = ein("bn16", (1, 128))
    wl1 = ein("wl1", (NE, D, 2 * HID))
    bl1c = ein("bl1c", (128, 256))
    wl2 = ein("wl2", (NE, HID, D))
    bl2 = ein("bl2", (NE, D))
    oh_in = ein("oh", (1, NCORE))
    wmaskT = ein("wmaskT", (NCORE, NCORE))
    mvec = ein("mvec", (NCORE, 1))

    out = nc.dram_tensor("out", [T, D], F32, kind="ExternalOutput")
    cc_buf = nc.dram_tensor("cc_buf", [NCORE, 32], F32)

    dbg = {}
    if debug:
        for name, shape in (
            ("dbg_xn", (128, D)), ("dbg_xconv", (128, T)), ("dbg_sg", (128, T)),
            ("dbg_delta", (128, 256)), ("dbg_ysm", (128, NCH)), ("dbg_ew", (128, 128)),
            ("dbg_yac", (128, D)), ("dbg_su", (8, 32)), ("dbg_g", (128, 256)),
            ("dbg_ycol", (128, NCH)), ("dbg_xnT", (128, 128)),
            ("dbg_sm", (128, 48)), ("dbg_a", (128, 256)), ("dbg_bcum", (128, 256)),
            ("dbg_yloc", (128, NCH)), ("dbg_sm2", (128, 48)), ("dbg_cr", (128, 48)),
            ("dbg_w2", (1, 256)), ("dbg_er", (1, 256)), ("dbg_hr", (1, 272)),
        ):
            dbg[name] = nc.dram_tensor(name, list(shape), F32, kind="ExternalOutput")

    ident_bf = nc.inline_tensor(np.eye(128, dtype=BF), "c_identbf")
    ident_f = nc.inline_tensor(np.eye(128, dtype=np.float32), "c_identf")
    tri = np.tril(np.ones((128, 128), dtype=np.float32))
    triT_c = nc.inline_tensor(np.ascontiguousarray(tri.T), "c_triT")
    ones_row_c = nc.inline_tensor(np.ones((1, 128), dtype=np.float32), "c_onesrow")
    sel_np = np.zeros((NE, NE * 128), dtype=BF)
    for n in range(NE):
        sel_np[n, n * 128:(n + 1) * 128] = 1
    sel_c = nc.inline_tensor(sel_np, "c_sel")

    BASE = 16512
    cur = [BASE]

    def at(name, shape, dtype, offset=None):
        if offset is None:
            offset = cur[0]
            sz = int(np.prod(shape[1:])) * mybir.dt.size(dtype)
            cur[0] = (offset + sz + 31) & ~31
        return nc.alloc_sbuf_tensor_at(name, list(shape), dtype, offset=offset)

    xnT = at("xnT", (128, KD * TH), BF16)
    B1 = cur[0]
    xconvT = at("xconvT", (128, NCT * T), BF16)
    y_acc = at("y_acc", (128, 16 * D), F32, offset=B1)
    B2 = cur[0]
    sgT = at("sgT", (128, NCT * T), BF16)
    yskT = at("yskT", (128, NCT * T), BF16, offset=B2)
    gT = at("gT", (128, NCT * T), BF16, offset=B2)
    S0 = cur[0]
    xs = at("xs", (128, TH), F32, offset=S0)
    convA = at("convA", (128, 512), F32, offset=S0 + 8224)
    convB = at("convB", (128, 512), F32, offset=S0 + 8224 + 2048)
    winst = at("winst", (128, 2048), BF16, offset=S0 + 8224 + 4096)
    xld = at("xld", (128, 2 * D), F32, offset=S0 + 16416)
    sdbc = at("sdbc", (64, T), F32, offset=S0)
    woutst = at("woutst", (128, NCT * 512), BF16, offset=S0)
    wabst = at("wabst", (128, 4 * D), BF16, offset=S0)
    wl2st = at("wl2st", (128, 8 * D), BF16, offset=S0 + 8192)
    cur[0] = S0 + 24640
    R0 = cur[0]
    dpre = at("dpre", (128, 256), F32, offset=R0)
    sBs = at("sBs", (128, 256), F32, offset=R0 + 1024)
    sCs = at("sCs", (128, 256), F32, offset=R0 + 2048)
    sA_ = at("sA_", (128, 256), F32, offset=R0 + 3072)
    sG = at("sG", (128, 256), F32, offset=R0 + 4096)
    sT1 = at("sT1", (128, 256), F32, offset=R0 + 5120)
    sT2 = at("sT2", (128, 256), F32, offset=R0 + 6144)
    tmp2 = at("tmp2", (128, 1024), F32, offset=R0)
    saA = at("saA", (128, 512), F32, offset=R0)
    saB = at("saB", (128, 512), F32, offset=R0 + 2048)
    convA2 = at("convA2", (128, 512), F32, offset=R0 + 4096)
    convB2 = at("convB2", (128, 512), F32, offset=R0 + 6144)
    cur[0] = R0 + 7168
    YB = cur[0]
    y_bc = at("y_bc", (128, T), BF16)
    ew_bcT = at("ew_bcT", (128, T), BF16, offset=YB)
    ew = at("ew", (128, 128), F32, offset=R0 + 4096)
    ewT2 = at("ewT2", (NE, T), BF16)
    xavg = at("xavg", (128, NCH), F32)
    y_col = at("y_col", (128, NCH), F32)
    yfin = xavg
    glr = dpre[0:1, 0:256]
    blr = sBs[0:1, 0:256]
    gcum = sdbc[0:1, 0:272]
    hrow = sdbc[0:1, 272:544]
    yrow_stage = dpre[0:1, 0:128]
    erow = sCs[0:1, 0:DS]
    wrow = sCs[0:1, DS:2 * DS]
    arow_r = sCs[0:1, 2 * DS:3 * DS]
    hin = sCs[0:1, 3 * DS:4 * DS]
    arow = sCs[0:1, 0:DS]
    abc = at("abc", (128, DS), F32)
    dbbc = at("dbbc", (128, DS), F32)
    brbc = at("brbc", (128, NE), F32)
    bnbc = at("bnbc", (128, NE), F32)
    su8 = at("su8", (NCORE, 32), F32)
    gput = at("gput", (NCORE, 32), F32)
    e8 = at("e8", (NCORE, DS), F32)
    ctrb = at("ctrb", (NCORE, DS), F32)
    xres = at("xres", (128, 2 * D), F32, offset=S0)
    wstk = at("wstk", (128, 2 * 64), BF16)
    cwsb = at("cwsb", (128, 64), F32)
    cbsb = at("cbsb", (128, NCT), F32)
    dsb = at("dsb", (128, NCT), F32)
    alsb = at("alsb", (1, DS), F32)
    dbsb = at("dbsb", (1, DS), F32)
    brsb = at("brsb", (1, NE), F32)
    bnsb = at("bnsb", (1, NE), F32)
    bl1sb = at("bl1sb", (128, 256), F32)
    bl2sb = at("bl2sb", (NE, D), BF16)
    wrsb = at("wrsb", (128, 64), BF16)
    ohsb = at("ohsb", (1, NCORE), F32)
    wmsb = at("wmsb", (NCORE, NCORE), F32)
    mvsb = at("mvsb", (NCORE, 1), F32)
    idbf = at("idbf", (128, 128), BF16)
    idf = at("idf", (128, 128), F32)
    trit = at("trit", (128, 128), F32)
    onesrow = at("onesrow", (1, 128), F32)
    selsb = at("selsb", (NE, NE * 128), BF16)

    sbuf_used = cur[0]

    pA = nc.alloc_psum_tensor("pA", [128, 512], F32)
    pB = nc.alloc_psum_tensor("pB", [128, 512], F32)
    pC = nc.alloc_psum_tensor("pC", [128, 512], F32)
    pD = nc.alloc_psum_tensor("pD", [128, 512], F32)
    pE = nc.alloc_psum_tensor("pE", [128, 512], F32)
    pF = nc.alloc_psum_tensor("pF", [128, 512], F32)
    pG = nc.alloc_psum_tensor("pG", [128, 512], F32)
    pH = nc.alloc_psum_tensor("pH", [128, 1024], BF16)
    pAB = [pA, pB]
    pCD = [pC, pD]
    pEF = [pE, pF]

    sems = {k: [nc.alloc_semaphore(k), 0] for k in ("L", "W", "T", "V", "S", "C")}

    def blockrun(fill):
        sch = Sched(nc, sems)
        fill(sch)
        SCHEDS.append((fill.__name__, {e: [(w, inc, n) for (f, w, inc, n) in sch.ops[e]]
                                       for e in ENGS}))
        with nc.Block() as blk:
            sch.emit(blk)

    def dma(dst, src):
        return lambda e: e.dma_start(out=dst, in_=src)

    def cp(dst, src):
        return lambda e: e.tensor_copy(out=dst, in_=src)

    def p0a(s):
        for dst, src in (
            (cwsb[:, :], conv_w[:, :]), (cbsb[:, :], conv_b[:, :]),
            (dsb[:, :], d_col[:, :]), (alsb[:, :], a_log[:, :]),
            (dbsb[:, :], delta_b[:, :]), (brsb[:, :], br_r[:, :]),
            (bnsb[:, :], bn_r[:, :]), (bl1sb[:, :], bl1c[:, :]),
            (ohsb[:, :], oh_in[:, :]), (wmsb[:, :], wmaskT[:, :]),
            (mvsb[:, :], mvec[:, :]), (idbf[:, :], ident_bf[:, :]),
            (idf[:, :], ident_f[:, :]), (trit[:, :], triT_c[:, :]),
            (onesrow[:, :], ones_row_c[:, :]),
            (selsb[:, :], sel_c[:, :]),
        ):
            s.add("sync", dma(dst, src), inc="L", inc_n=16)
        s.add("gpsimd", dma(bl2sb[:, :], bl2[:, :]), inc="W", inc_n=16)
        s.add("gpsimd",
              dma(wrsb.ap().rearrange("p (k n) -> p k n", k=KD),
                  wrT.ap().rearrange("(k p) n -> p k n", p=128)),
              inc="W", inc_n=16)
        s.add("gpsimd",
              dma(winst.ap()[:, 0:1024].rearrange("p (k n) -> p k n", k=KD),
                  winT.ap()[:, 0:128].rearrange("(k p) n -> p k n", p=128)),
              inc="W", inc_n=16)
        s.add("gpsimd", dma(wstk[:, 0:64], wstack[0:128, :]), inc="W", inc_n=16)
        lv = s.val("L")
        s.add("scalar", lambda e: e.activation(out=arow[:, :], in_=alsb[:, :],
                                               func=AF.Exp),
              waits=[("L", lv)], inc="S")
        s.add("vector", lambda e: e.tensor_scalar(out=arow[:, :], in0=arow[:, :],
                                                  scalar1=-1.0, scalar2=None,
                                                  op0=ALU.mult),
              waits=[("S", s.val("S"))], inc="V")
        vv = s.val("V")
        s.add("tensor", lambda e: e.matmul(out=pG[:, 0:DS], lhsT=onesrow[:, :],
                                           rhs=arow[:, :], start=True, stop=True),
              waits=[("V", vv), ("L", lv)], inc="T")
        s.add("vector", cp(abc[:, :], pG[:, 0:DS]), waits=[("T", s.val("T"))], inc="V")
        vv = s.val("V")
        s.add("tensor", lambda e: e.matmul(out=pG[:, 64:64 + DS], lhsT=onesrow[:, :],
                                           rhs=dbsb[:, :], start=True, stop=True),
              waits=[("V", vv)], inc="T")
        s.add("vector", cp(dbbc[:, :], pG[:, 64:64 + DS]), waits=[("T", s.val("T"))], inc="V")
        vv = s.val("V")
        s.add("tensor", lambda e: e.matmul(out=pG[:, 128:128 + NE], lhsT=onesrow[:, :],
                                           rhs=brsb[:, :], start=True, stop=True),
              waits=[("V", vv)], inc="T")
        s.add("vector", cp(brbc[:, :], pG[:, 128:128 + NE]), waits=[("T", s.val("T"))], inc="V")
        vv = s.val("V")
        s.add("tensor", lambda e: e.matmul(out=pG[:, 192:192 + NE], lhsT=onesrow[:, :],
                                           rhs=bnsb[:, :], start=True, stop=True),
              waits=[("V", vv)], inc="T")
        s.add("vector", cp(bnbc[:, :], pG[:, 192:192 + NE]), waits=[("T", s.val("T"))], inc="V")

    blockrun(p0a)

    def p0b(s):
        center_done = {}
        s.add("vector", lambda e: e.memset(convA[:, 6:7], 1e-5), inc="V")
        for ti in range(-1, 16):
            rows = HALO if ti < 0 else 128
            if ti < 0:
                ldst = xld[0:HALO, 0:D]
                lsrc = x_halo[:, :]
            else:
                sl = (ti % 2) * D
                ldst = xld[:, sl:sl + D]
                lsrc = x_own[ti * 128:(ti + 1) * 128, :]
            prev_user = center_done.get(ti - 2)
            if ti == 0:
                prev_user = center_done.get(-1)
            s.add("sync", dma(ldst, lsrc),
                  waits=[("V", prev_user)] if prev_user else (), inc="L", inc_n=16)
            lv = s.val("L")
            src = ldst
            s.add("vector", lambda e, src=src, rows=rows: e.reduce_sum(
                out=convA[0:rows, 0:1], in_=src, axis=AX), waits=[("L", lv)], inc="V")
            vv = s.val("V")
            s.add("scalar", lambda e, rows=rows: e.activation(
                out=convA[0:rows, 1:2], in_=convA[0:rows, 0:1], func=AF.Identity,
                scale=1.0 / D), waits=[("V", vv)], inc="S")
            sv = s.val("S")
            s.add("vector", lambda e, src=src, rows=rows: e.tensor_scalar(
                out=xs[0:rows, 0:D], in0=src, scalar1=convA[0:rows, 1:2],
                scalar2=None, op0=ALU.subtract), waits=[("S", sv)], inc="V")
            center_done[ti] = s.val("V")
            vv = s.val("V")
            s.add("scalar", lambda e, rows=rows: e.activation(
                out=tmp2[0:rows, 0:D], in_=xs[0:rows, 0:D], func=AF.Square),
                waits=[("V", vv)], inc="S")
            sv = s.val("S")
            s.add("vector", lambda e, rows=rows: e.reduce_sum(
                out=convA[0:rows, 2:3], in_=tmp2[0:rows, 0:D], axis=AX),
                waits=[("S", sv)], inc="V")
            vv = s.val("V")
            s.add("scalar", lambda e, rows=rows: e.activation(
                out=convA[0:rows, 4:5], in_=convA[0:rows, 2:3], func=AF.Ln,
                scale=1.0 / D, bias=convA[0:rows, 6:7]), waits=[("V", vv)], inc="S")
            s.add("scalar", lambda e, rows=rows: e.activation(
                out=convA[0:rows, 5:6], in_=convA[0:rows, 4:5], func=AF.Exp,
                scale=-0.5), inc="S")
            wyb = [("S", s.val("S"))]
            if debug and ti == 1:
                wyb.append(("W", dbg_xn_w))
            s.add("vector", lambda e, rows=rows: e.tensor_scalar(
                out=y_bc[0:rows, 0:D], in0=xs[0:rows, 0:D],
                scalar1=convA[0:rows, 5:6], scalar2=None, op0=ALU.mult),
                waits=wyb, inc="V")
            vv = s.val("V")
            dbg_gate = None
            if debug and ti == 0:
                s.add("gpsimd", dma(dbg["dbg_xn"][:, :], y_bc[:, 0:D]),
                      waits=[("V", s.val("V"))], inc="W", inc_n=16)
                dbg_xn_w = s.val("W")
            for k in range(KD):
                tv_prev = s.val("V")
                s.add("tensor", lambda e, k=k, rows=rows: e.transpose(
                    out=pH[:, k * 128:k * 128 + rows],
                    in_=y_bc[0:rows, k * 128:(k + 1) * 128],
                    identity=idbf[0:rows, 0:rows]),
                    waits=[("V", vv)], inc="T")
            tv = s.val("T")
            col = 0 if ti < 0 else HALO + ti * 128
            s.add("vector", lambda e, col=col, rows=rows: e.tensor_copy(
                out=xnT.ap().rearrange("p (k t) -> p k t", k=KD)[:, :, col:col + rows],
                in_=pH.ap().rearrange("p (k t) -> p k t", k=KD)[:, :, 0:rows]),
                waits=[("T", tv)], inc="V")

    blockrun(p0b)

    if debug:
        def p0b2(s):
            s.add("gpsimd", dma(dbg["dbg_xnT"][:, :],
                                xnT[:, HALO:HALO + 128]),
                  waits=[("V", s.val("V"))], inc="W", inc_n=16)
        blockrun(p0b2)

    def p0c(s):
        sc = tmp2[:, 0:128]
        sp_ = tmp2[:, 128:256]
        mk1 = tmp2[:, 256:384]
        sp2_ = tmp2[:, 384:512]
        mk = tmp2[:, 512:640]
        esub = tmp2[:, 640:768]
        eall = tmp2[:, 768:896]
        m1 = tmp2[:, 896:912]
        m2 = tmp2[:, 912:928]
        nm = tmp2[:, 928:944]
        den = tmp2[:, 944:960]
        rinv = tmp2[:, 976:992]

        def t3(ap2d):
            return ap2d.rearrange("p (t k) -> p t k", k=NE)

        def b3(ap16):
            return ap16.rearrange("p (t o) -> p t o", o=1).broadcast_to([128, 16, NE])

        s.add("sync", dma(convB[0:1, 0:128], bn16[:, :]), inc="L", inc_n=16)
        lv = s.val("L")
        s.add("tensor", lambda e: e.matmul(out=pG[:, 0:128], lhsT=onesrow[:, :],
                                           rhs=convB[0:1, 0:128], start=True,
                                           stop=True), waits=[("L", lv)], inc="T")
        s.add("vector", cp(convA[:, 0:128], pG[:, 0:128]),
              waits=[("T", s.val("T"))], inc="V")
        sc_done = {}
        for tt in range(16):
            p = pAB[tt % 2]
            w = [("W", 32)]
            pv = sc_done.get(tt - 2)
            if pv:
                w.append(("V", pv))
            for k in range(KD):
                s.add("tensor", lambda e, k=k, tt=tt, p=p: e.matmul(
                    out=p[:, 0:NE],
                    lhsT=xnT[:, k * TH + HALO + tt * 128: k * TH + HALO + (tt + 1) * 128],
                    rhs=wrsb[:, k * NE:(k + 1) * NE],
                    start=(k == 0), stop=(k == KD - 1)),
                    waits=w if k == 0 else (), inc="T")
            tv = s.val("T")
            s.add("vector", lambda e, p=p, tt=tt: e.tensor_add(
                out=sc[:, tt * 8:(tt + 1) * 8], in0=p[:, 0:NE], in1=brbc[:, :]),
                waits=[("T", tv)], inc="V")
            sc_done[tt] = s.val("V")
        s.add("vector", lambda e: e.tensor_add(out=sp_, in0=sc, in1=convA[:, 0:128]), inc="V")
        s.add("vector", lambda e: e.tensor_reduce(out=m1, in_=t3(sp_), axis=AX,
                                                  op=ALU.max), inc="V")
        s.add("vector", lambda e: e.tensor_tensor(out=t3(mk1), in0=t3(sp_),
                                                  in1=b3(m1), op=ALU.is_ge), inc="V")
        s.add("vector", lambda e: e.scalar_tensor_tensor(
            out=sp2_, in0=mk1, scalar=-1e30, in1=sp_, op0=ALU.mult, op1=ALU.add), inc="V")
        s.add("vector", lambda e: e.tensor_reduce(out=m2, in_=t3(sp2_), axis=AX,
                                                  op=ALU.max), inc="V")
        s.add("vector", lambda e: e.tensor_tensor(out=t3(esub), in0=t3(sp2_),
                                                  in1=b3(m2), op=ALU.is_ge), inc="V")
        s.add("vector", lambda e: e.tensor_add(out=mk, in0=mk1, in1=esub), inc="V")
        s.add("vector", lambda e: e.tensor_reduce(out=nm, in_=t3(sc), axis=AX,
                                                  op=ALU.max), inc="V")
        s.add("vector", lambda e: e.tensor_tensor(out=t3(esub), in0=t3(sc),
                                                  in1=b3(nm), op=ALU.subtract), inc="V")
        vv = s.val("V")
        s.add("scalar", lambda e: e.activation(out=eall, in_=esub, func=AF.Exp),
              waits=[("V", vv)], inc="S")
        sv = s.val("S")
        s.add("vector", lambda e: e.tensor_mul(out=esub, in0=eall, in1=mk),
              waits=[("S", sv)], inc="V")
        s.add("vector", lambda e: e.tensor_reduce(out=den, in_=t3(esub), axis=AX,
                                                  op=ALU.add), inc="V")
        vv = s.val("V")
        s.add("scalar", lambda e: e.activation(out=tmp2[:, 960:976], in_=den,
                                               func=AF.Ln), waits=[("V", vv)], inc="S")
        s.add("scalar", lambda e: e.activation(out=rinv, in_=tmp2[:, 960:976],
                                               func=AF.Exp, scale=-1.0), inc="S")
        sv = s.val("S")
        s.add("vector", lambda e: e.tensor_tensor(
            out=ew.ap().rearrange("p (t k) -> p t k", k=NE), in0=t3(esub),
            in1=b3(rinv), op=ALU.mult), waits=[("S", sv)], inc="V")
        if debug:
            s.add("sync", dma(dbg["dbg_sm"][:, :], mk[:, 0:48]),
                  waits=[("V", s.val("V"))], inc="L", inc_n=16)
            s.add("sync", dma(dbg["dbg_ew"][:, :], ew[:, :]), inc="L", inc_n=16)
            s.add("vector", lambda e: e.engine_nop(),
                  waits=[("L", s.val("L"))], inc="V")
        for tt in range(16):
            vv = s.val("V")
            s.add("tensor", lambda e, tt=tt: e.transpose(
                out=pG[0:8, 0:128], in_=ew[:, tt * 8:(tt + 1) * 8],
                identity=idf[:, :]), waits=[("V", vv)], inc="T")
            s.add("vector", lambda e, tt=tt: e.tensor_copy(
                out=ewT2[:, tt * 128:(tt + 1) * 128], in_=pG[0:8, 0:128]),
                waits=[("T", s.val("T"))], inc="V")

    blockrun(p0c)

    ssm_chunks = [(0, 512), (512, 1024), (1024, 1536), (1536, 2048), (2048, 2052)]

    def p1a(s):
        evac_done = {}
        cast_done = {}
        silu_done = {}
        pool_done = {}
        win_done = {}
        wstk_w = {}
        win_done[0] = 48
        wstk_w[0] = 64

        def emit_sdbc(ct):
            for q in range(4):
                qc = q * 512
                psd = [pC, pD, pE, pF][q]
                w = [("V", cast_done[(ct, q)])]
                if q == 0:
                    w.append(("W", wstk_w[ct]))
                s.add("tensor", lambda e, ct=ct, qc=qc, psd=psd: e.matmul(
                    out=psd[0:64, 0:512],
                    lhsT=wstk[:, (ct % 2) * 64:(ct % 2) * 64 + 64],
                    rhs=xconvT[:, ct * T + qc:ct * T + qc + 512],
                    start=(ct == 0), stop=(ct == NCT - 1)),
                    waits=w, inc="T")

        for ct in range(NCT):
            if ct + 1 < NCT:
                wt = s.val("T") if ct >= 1 else 0
                c0 = (ct + 1) * 128
                s.add("gpsimd", dma(
                    winst.ap()[:, ((ct + 1) % 2) * 1024:((ct + 1) % 2 + 1) * 1024].rearrange("p (k n) -> p k n", k=KD),
                    winT.ap()[:, c0:c0 + 128].rearrange("(k p) n -> p k n", p=128)),
                    waits=[("T", wt)], inc="W", inc_n=16)
                win_done[ct + 1] = s.val("W")
            for ci, (c0, c1) in enumerate(ssm_chunks):
                p = pAB[ci % 2]
                w = [("W", win_done[ct])]
                if ci >= 2:
                    prev = evac_done.get((ct, ci - 2))
                else:
                    prev = evac_done.get((ct - 1, 4 if ci == 0 else 3))
                if prev:
                    w.append(("V", prev))
                for k in range(KD):
                    s.add("tensor", lambda e, k=k, ct=ct, c0=c0, c1=c1, p=p: e.matmul(
                        out=p[:, 0:c1 - c0],
                        lhsT=winst[:, (ct % 2) * 1024 + k * 128:(ct % 2) * 1024 + (k + 1) * 128],
                        rhs=xnT[:, k * TH + c0:k * TH + c1],
                        start=(k == 0), stop=(k == KD - 1)),
                        waits=w if k == 0 else (), inc="T")
                tv = s.val("T")
                s.add("vector", lambda e, c0=c0, c1=c1, p=p: e.tensor_copy(
                    out=xs[:, c0:c1], in_=p[:, 0:c1 - c0]),
                    waits=[("T", tv)], inc="V")
                evac_done[(ct, ci)] = s.val("V")
            if ct >= 1:
                emit_sdbc(ct - 1)
            if ct + 1 < NCT:
                s.add("gpsimd", dma(
                    wstk[:, ((ct + 1) % 2) * 64:((ct + 1) % 2 + 1) * 64],
                    wstack[(ct + 1) * 128:(ct + 2) * 128, :]),
                    waits=[("T", s.val("T"))], inc="W", inc_n=16)
                wstk_w[ct + 1] = s.val("W")
            for q in range(4):
                qc = q * 512
                sa = saA if q % 2 == 0 else saB
                cB = convB if q % 2 == 0 else convB2
                w0 = [("V", evac_done[(ct, 4)])]
                wprev = silu_done.get((ct, q - 2)) or silu_done.get((ct - 1, 2 + (q % 2)))
                if wprev:
                    w0.append(("S", wprev))
                s.add("vector", lambda e, ct=ct, qc=qc: e.tensor_scalar(
                    out=convA[:, 0:512], in0=xs[:, qc + 1:qc + 513],
                    scalar1=cwsb[:, ct * 4:ct * 4 + 1], scalar2=cbsb[:, ct:ct + 1],
                    op0=ALU.mult, op1=ALU.add), waits=w0, inc="V")
                s.add("vector", lambda e, ct=ct, qc=qc, cB=cB: e.scalar_tensor_tensor(
                    out=cB[:, 0:512], in0=xs[:, qc + 2:qc + 514],
                    scalar=cwsb[:, ct * 4 + 1:ct * 4 + 2], in1=convA[:, 0:512],
                    op0=ALU.mult, op1=ALU.add), inc="V")
                s.add("vector", lambda e, ct=ct, qc=qc, cB=cB: e.scalar_tensor_tensor(
                    out=convA[:, 0:512], in0=xs[:, qc + 3:qc + 515],
                    scalar=cwsb[:, ct * 4 + 2:ct * 4 + 3], in1=cB[:, 0:512],
                    op0=ALU.mult, op1=ALU.add), inc="V")
                s.add("vector", lambda e, ct=ct, qc=qc, cB=cB: e.scalar_tensor_tensor(
                    out=cB[:, 0:512], in0=xs[:, qc + 4:qc + 516],
                    scalar=cwsb[:, ct * 4 + 3:ct * 4 + 4], in1=convA[:, 0:512],
                    op0=ALU.mult, op1=ALU.add), inc="V")
                vv = s.val("V")
                wsa = [("V", vv)]
                pcq = {0: (ct - 1, 2), 1: (ct - 1, 3), 2: (ct, 0), 3: (ct, 1)}[q]
                pc = cast_done.get(pcq)
                if pc:
                    wsa.append(("V", pc))
                s.add("scalar", lambda e, sa=sa, cB=cB: e.activation(
                    out=sa[:, 0:512], in_=cB[:, 0:512], func=AF.Silu),
                    waits=wsa, inc="S")
                silu_done[(ct, q)] = s.val("S")
                sv = s.val("S")
                s.add("vector", lambda e, ct=ct, qc=qc, sa=sa: e.tensor_copy(
                    out=xconvT[:, ct * T + qc:ct * T + qc + 512], in_=sa[:, 0:512]),
                    waits=[("S", sv)], inc="V")
                cast_done[(ct, q)] = s.val("V")
        emit_sdbc(NCT - 1)
        if debug:
            s.add("gpsimd", dma(dbg["dbg_xconv"][:, :], xconvT[:, 0:T]),
                  waits=[("V", s.val("V"))], inc="W", inc_n=16)

    blockrun(p1a)

    def p1b(s):
        tv0 = s.val("T")
        for q, psd in enumerate((pC, pD, pE, pF)):
            s.add("vector", lambda e, q=q, psd=psd: e.tensor_copy(
                out=sdbc[0:64, q * 512:(q + 1) * 512], in_=psd[0:64, 0:512]),
                waits=[("T", tv0)], inc="V")
        marks["p1b_done"] = s.val("V")
        vv = s.val("V")
        for c in range(NCH):
            s.add("tensor", lambda e, c=c: e.transpose(
                out=pG[:, 0:64], in_=sdbc[0:64, c * 128:(c + 1) * 128],
                identity=idf[0:64, 0:64]), waits=[("V", s.val("V"))], inc="T")
            tv = s.val("T")
            s.add("vector", lambda e, c=c: e.tensor_copy(
                out=dpre[:, c * DS:(c + 1) * DS], in_=pG[:, 0:DS]),
                waits=[("T", tv)], inc="V")
            s.add("vector", lambda e, c=c: e.tensor_copy(
                out=sBs[:, c * DS:(c + 1) * DS], in_=pG[:, DS:2 * DS]), inc="V")
            s.add("vector", lambda e, c=c: e.tensor_copy(
                out=sCs[:, c * DS:(c + 1) * DS], in_=pG[:, 2 * DS:3 * DS]), inc="V")
            s.add("vector", lambda e, c=c: e.tensor_copy(
                out=xavg[:, c:c + 1], in_=pG[:, 3 * DS:3 * DS + 1]), inc="V")


    def p1c(s):
        V, S_, Tn = "V", "S", "T"
        for c in range(NCH):
            s.add("vector", lambda e, c=c: e.tensor_add(
                out=sT1[:, c * DS:(c + 1) * DS], in0=dpre[:, c * DS:(c + 1) * DS],
                in1=dbbc[:, :]), inc="V")
        vv = s.val("V")
        s.add("scalar", lambda e: e.activation(out=sT2[:, :], in_=sT1[:, :],
                                               func=AF.Abs), waits=[("V", vv)], inc="S")
        s.add("scalar", lambda e: e.activation(out=dpre[:, :], in_=sT2[:, :],
                                               func=AF.Exp, scale=-1.0), inc="S")
        s.add("scalar", lambda e: e.activation(out=sT2[:, :], in_=dpre[:, :],
                                               func=AF.Ln, bias=1.0), inc="S")
        sv = s.val("S")
        s.add("vector", lambda e: e.tensor_scalar(
            out=dpre[:, :], in0=sT1[:, :], scalar1=0.0, scalar2=None,
            op0=ALU.max), waits=[("S", sv)], inc="V")
        s.add("vector", lambda e: e.tensor_add(
            out=sT1[:, :], in0=dpre[:, :], in1=sT2[:, :]), inc="V")
        for c in range(NCH):
            s.add("vector", lambda e, c=c: e.tensor_mul(
                out=sG[:, c * DS:(c + 1) * DS], in0=sT1[:, c * DS:(c + 1) * DS],
                in1=abc[:, :]), inc="V")
        for c in range(NCH):
            s.add("vector", lambda e, c=c: e.tensor_mul(
                out=sT2[:, c * DS:(c + 1) * DS], in0=sT1[:, c * DS:(c + 1) * DS],
                in1=sBs[:, c * DS:(c + 1) * DS]), inc="V")
        for c in range(NCH):
            s.add("vector", lambda e, c=c: e.tensor_scalar(
                out=sBs[:, c * DS:(c + 1) * DS], in0=sT2[:, c * DS:(c + 1) * DS],
                scalar1=xavg[:, c:c + 1], scalar2=None, op0=ALU.mult), inc="V")
        dbg_l = None
        if debug:
            s.add("sync", dma(dbg["dbg_delta"][:, :], sT1[:, :]),
                  waits=[("V", s.val("V"))], inc="L", inc_n=16)
            s.add("sync", dma(dbg["dbg_g"][:, :], sG[:, :]),
                  waits=[("V", s.val("V"))], inc="L", inc_n=16)
            dbg_l = s.val("L")
        vv = s.val("V")
        s.add("tensor", lambda e: e.matmul(out=pG[0:1, 0:256], lhsT=trit[:, 127:128],
                                           rhs=sG[:, :], start=True, stop=True),
              waits=[("V", vv)], inc="T")
        s.add("vector", cp(glr[:, :], pG[0:1, 0:256]), waits=[("T", s.val("T"))], inc="V")
        for c in range(NCH):
            p = pAB[c % 2]
            wv = s.val("V")
            s.add("tensor", lambda e, c=c, p=p: e.matmul(
                out=p[:, 0:DS], lhsT=trit[:, :], rhs=sG[:, c * DS:(c + 1) * DS],
                start=True, stop=True), waits=[("V", wv)], inc="T")
            wcp = [("T", s.val("T"))]
            if c == 0 and dbg_l:
                wcp.append(("L", dbg_l))
            s.add("vector", lambda e, c=c, p=p: e.tensor_copy(
                out=sG[:, c * DS:(c + 1) * DS], in_=p[:, 0:DS]),
                waits=wcp, inc="V")
        vv = s.val("V")
        s.add("scalar", lambda e: e.activation(out=sT2[:, :], in_=sG[:, :],
                                               func=AF.Exp), waits=[("V", vv)], inc="S")
        sv = s.val("S")
        s.add("vector", lambda e: e.tensor_mul(out=sA_[:, :], in0=sCs[:, :],
                                               in1=sT2[:, :]), waits=[("S", sv)], inc="V")
        s.add("scalar", lambda e: e.activation(out=sT1[:, :], in_=sG[:, :],
                                               func=AF.Exp, scale=-1.0), inc="S")
        sv = s.val("S")
        s.add("vector", lambda e: e.tensor_mul(out=sG[:, :], in0=sT1[:, :],
                                               in1=sBs[:, :]), waits=[("S", sv)], inc="V")
        vv = s.val("V")
        s.add("tensor", lambda e: e.matmul(out=pG[0:1, 0:256], lhsT=trit[:, 127:128],
                                           rhs=sG[:, :], start=True, stop=True),
              waits=[("V", vv)], inc="T")
        s.add("vector", cp(blr[:, :], pG[0:1, 0:256]), waits=[("T", s.val("T"))], inc="V")
        for c in range(NCH):
            p = pAB[c % 2]
            wv = s.val("V")
            s.add("tensor", lambda e, c=c, p=p: e.matmul(
                out=p[:, 0:DS], lhsT=trit[:, :], rhs=sG[:, c * DS:(c + 1) * DS],
                start=True, stop=True), waits=[("V", wv)], inc="T")
            s.add("vector", lambda e, c=c, p=p: e.tensor_copy(
                out=sT1[:, c * DS:(c + 1) * DS], in_=p[:, 0:DS]),
                waits=[("T", s.val("T"))], inc="V")
        s.add("vector", lambda e: e.tensor_mul(out=sT2[:, :], in0=sA_[:, :],
                                               in1=sT1[:, :]), inc="V")
        for c in range(NCH):
            s.add("vector", lambda e, c=c: e.reduce_sum(
                out=y_col[:, c:c + 1], in_=sT2[:, c * DS:(c + 1) * DS], axis=AX), inc="V")
        if debug:
            s.add("sync", dma(dbg["dbg_a"][:, :], sA_[:, :]),
                  waits=[("V", s.val("V"))], inc="L", inc_n=16)
            s.add("sync", dma(dbg["dbg_bcum"][:, :], sT1[:, :]), inc="L", inc_n=16)
            s.add("sync", dma(dbg["dbg_yloc"][:, :], y_col[:, :]), inc="L", inc_n=16)
            dbg_p1c_l = s.val("L")
            s.add("vector", lambda e: e.engine_nop(),
                  waits=[("L", dbg_p1c_l)], inc="V")
        s.add("vector", lambda e: e.memset(gcum[:, 0:DS], 0.0), inc="V")
        s.add("vector", lambda e: e.memset(hrow[:, 0:DS], 0.0), inc="V")
        for c in range(NCH):
            s.add("vector", lambda e, c=c: e.tensor_add(
                out=gcum[:, (c + 1) * DS:(c + 2) * DS],
                in0=gcum[:, c * DS:(c + 1) * DS],
                in1=glr[:, c * DS:(c + 1) * DS]), inc="V")
            vv = s.val("V")
            s.add("scalar", lambda e, c=c: e.activation(
                out=erow[:, :], in_=glr[:, c * DS:(c + 1) * DS], func=AF.Exp),
                waits=[("V", vv)], inc="S")
            s.add("scalar", lambda e: e.activation(
                out=xld[:, 256:384], in_=xld[:, 128:256], func=AF.Identity),
                inc="S")
            sv = s.val("S")
            s.add("vector", lambda e, c=c: e.tensor_mul(
                out=wrow[:, :], in0=erow[:, :], in1=blr[:, c * DS:(c + 1) * DS]),
                waits=[("S", sv)], inc="V")
            s.add("vector", lambda e, c=c: e.tensor_mul(
                out=arow_r[:, :], in0=erow[:, :], in1=hrow[:, c * DS:(c + 1) * DS]), inc="V")
            s.add("vector", lambda e, c=c: e.tensor_add(
                out=hrow[:, (c + 1) * DS:(c + 2) * DS], in0=arow_r[:, :],
                in1=wrow[:, :]), inc="V")
        vv = s.val("V")
        s.add("tensor", lambda e: e.matmul(out=pG[0:8, 0:DS], lhsT=ohsb[:, :],
                                           rhs=gcum[:, NCH * DS:(NCH + 1) * DS],
                                           start=True, stop=True),
              waits=[("V", vv)], inc="T")
        s.add("tensor", lambda e: e.matmul(out=pG[0:8, DS:2 * DS], lhsT=ohsb[:, :],
                                           rhs=hrow[:, NCH * DS:(NCH + 1) * DS],
                                           start=True, stop=True), inc="T")
        s.add("vector", cp(gput[:, :], pG[0:8, 0:32]), waits=[("T", s.val("T"))], inc="V")
        vv = s.val("V")
        s.add("gpsimd", dma(cc_buf[:, :], gput[:, :]), waits=[("V", vv)], inc="W", inc_n=16)
        cw = s.val("W")
        s.add("gpsimd", lambda e: e.collective_compute(
            "AllReduce", ALU.add, replica_groups=[list(range(NCORE))],
            ins=[cc_buf.ap()], outs=[cc_buf.ap()]),
            waits=[("W", cw)], inc="C", inc_n=1)


    def _prefetch_gate(s, g, wait_t):
        ct = NCT + g
        c0g = ct * 128
        s.add("gpsimd", dma(
            winst.ap()[:, (ct % 2) * 1024:(ct % 2 + 1) * 1024].rearrange("p (k n) -> p k n", k=KD),
            winT.ap()[:, c0g:c0g + 128].rearrange("(k p) n -> p k n", p=128)),
            waits=[("T", wait_t)] if wait_t else (), inc="W", inc_n=16)
        marks.setdefault("win_done", {})[g] = s.val("W")

    def prefetch_gate_early(s):
        _prefetch_gate(s, 0, 0)
        _prefetch_gate(s, 1, 0)

    def p1d(s, g_lo=0, g_hi=NCT):
        win_done = marks["win_done"]
        silu_done = marks.setdefault("gate_silu_chain", {})

        def prefetch_gate(g, wait_t):
            _prefetch_gate(s, g, wait_t)

        for g in range(g_lo, g_hi):
            for ci in range(4):
                p = (pC, pD, pE, pF)[ci]
                w = [("W", win_done[g])]
                prev = silu_done.get((g - 1, ci))
                if prev:
                    w.append(("S", prev))
                elif g == 0:
                    w.append(("V", marks["p1b_done"]))
                ct = NCT + g
                c0, c1 = HALO + ci * 512, HALO + (ci + 1) * 512
                for k in range(KD):
                    s.add("tensor", lambda e, k=k, ct=ct, c0=c0, c1=c1, p=p: e.matmul(
                        out=p[:, 0:512],
                        lhsT=winst[:, (ct % 2) * 1024 + k * 128:(ct % 2) * 1024 + (k + 1) * 128],
                        rhs=xnT[:, k * TH + c0:k * TH + c1],
                        start=(k == 0), stop=(k == KD - 1)),
                        waits=w if k == 0 else (), inc="T")
                tv = s.val("T")
                s.add("scalar", lambda e, g=g, ci=ci, p=p: e.activation(
                    out=sgT[:, g * T + ci * 512:g * T + (ci + 1) * 512],
                    in_=p[:, 0:512], func=AF.Silu), waits=[("T", tv)], inc="S")
                silu_done[(g, ci)] = s.val("S")
            marks.setdefault("gate_S", {})[g] = s.val("S")
            if g + 2 < NCT:
                prefetch_gate(g + 2, s.val("T"))
        if debug and g_hi == NCT:
            s.add("gpsimd", dma(dbg["dbg_sg"][:, :], sgT[:, 0:T]),
                  waits=[("S", s.val("S"))], inc="W", inc_n=16)


    def p1e(s):
        def vflush(tag=""):
            s.add("vector", lambda e: e.tensor_copy(out=xld[:, 128:256],
                                                    in_=xld[:, 0:128]), inc="V")
            return s.val("V")

        def sflush():
            s.add("scalar", lambda e: e.activation(out=xld[:, 256:384],
                                                   in_=xld[:, 128:256],
                                                   func=AF.Identity), inc="S")
            return s.val("S")

        cw = s.val("C")
        s.add("gpsimd", dma(su8[:, :], cc_buf[:, :]), waits=[("C", cw)],
              inc="W", inc_n=16)
        cw2 = s.val("W")
        if debug:
            s.add("sync", dma(dbg["dbg_su"][:, :], su8[:, :]), waits=[("W", cw2)],
                  inc="L", inc_n=16)
        s.add("tensor", lambda e: e.matmul(out=pA[0:8, 0:DS], lhsT=wmsb[:, :],
                                           rhs=su8[0:8, 0:DS], start=True, stop=True),
              waits=[("W", cw2)], inc="T")
        tv = s.val("T")
        s.add("scalar", lambda e: e.activation(out=e8[:, :], in_=pA[0:8, 0:DS],
                                               func=AF.Exp), waits=[("T", tv)], inc="S")
        sv = sflush()
        s.add("vector", lambda e: e.tensor_mul(out=ctrb[:, :], in0=e8[:, :],
                                               in1=su8[0:8, DS:2 * DS]),
              waits=[("S", sv)], inc="V")
        vv = vflush()
        s.add("tensor", lambda e: e.matmul(out=pB[0:1, 0:DS], lhsT=mvsb[:, :],
                                           rhs=ctrb[:, :], start=True, stop=True),
              waits=[("V", vv)], inc="T")
        s.add("vector", cp(hin[:, :], pB[0:1, 0:DS]), waits=[("T", s.val("T"))], inc="V")
        e_row = sdbc[0:1, 1024:1280]
        w_all = sdbc[0:1, 1280:1536]
        vv = s.val("V")
        s.add("scalar", lambda e: e.activation(out=e_row[:, :],
                                               in_=gcum[:, 0:NCH * DS],
                                               func=AF.Exp), waits=[("V", vv)], inc="S")
        sv = s.val("S")
        hin3 = hin.rearrange("p (o d) -> p o d", o=1).broadcast_to([1, NCH, DS])
        s.add("vector", lambda e: e.tensor_tensor(
            out=w_all.rearrange("p (t d) -> p t d", d=DS),
            in0=e_row.rearrange("p (t d) -> p t d", d=DS),
            in1=hin3, op=ALU.mult), waits=[("S", sv)], inc="V")
        vflush()
        s.add("vector", lambda e: e.tensor_add(out=w_all[:, :], in0=w_all[:, :],
                                               in1=hrow[:, 0:NCH * DS]), inc="V") \
            if False else None
        s.add("vector", lambda e: e.tensor_tensor(
            out=sdbc[0:1, 1536:1792], in0=w_all[:, :], in1=hrow[:, 0:NCH * DS],
            op=ALU.add), inc="V")
        vv = vflush()
        if debug:
            s.add("sync", dma(dbg["dbg_w2"][:, :], sdbc[0:1, 1536:1792]),
                  waits=[("V", vv)], inc="L", inc_n=16)
            s.add("sync", dma(dbg["dbg_er"][:, :], e_row[:, :]), inc="L", inc_n=16)
            s.add("sync", dma(dbg["dbg_hr"][:, :], hrow[:, 0:272]), inc="L", inc_n=16)
        s.add("tensor", lambda e: e.matmul(out=pA[:, 0:NCH * DS], lhsT=onesrow[:, :],
                                           rhs=sdbc[0:1, 1536:1792], start=True,
                                           stop=True), waits=[("V", vv)], inc="T")
        marks["sdbc_T"] = s.val("T")
        marks["sdbc_V"] = s.val("V")
        tv = s.val("T")
        s.add("vector", lambda e: e.tensor_mul(out=sT2[:, :], in0=sA_[:, :],
                                               in1=pA[:, 0:NCH * DS]),
              waits=[("T", tv)], inc="V")
        s.add("vector", lambda e: e.tensor_reduce(
            out=xld[:, 384:384 + NCH],
            in_=sT2.ap().rearrange("p (t d) -> p t d", d=DS), axis=AX,
            op=ALU.add), inc="V")
        vflush()
        s.add("vector", lambda e: e.tensor_add(out=yfin[:, :], in0=y_col[:, :],
                                               in1=xld[:, 384:384 + NCH]), inc="V")
        vv = vflush()
        if debug:
            s.add("sync", dma(dbg["dbg_ycol"][:, :], yfin[:, :]),
                  waits=[("V", vv)], inc="L", inc_n=16)
        for c in range(NCH):
            s.add("tensor", lambda e, c=c: e.transpose(
                out=pG[0:1, 0:128], in_=yfin[:, c:c + 1], identity=idf[:, :]),
                waits=[("V", s.val("V"))], inc="T")
            tv = s.val("T")
            s.add("vector", lambda e, c=c: e.tensor_copy(
                out=yrow_stage[:, :], in_=pG[0:1, 0:128]),
                waits=[("T", tv)], inc="V")
            vv = vflush()
            p = pAB[c % 2]
            s.add("tensor", lambda e, c=c, p=p: e.matmul(
                out=p[:, 0:128], lhsT=onesrow[:, :],
                rhs=yrow_stage[:, :], start=True, stop=True),
                waits=[("V", vv)], inc="T")
            tv = s.val("T")
            s.add("scalar", lambda e, c=c, p=p: e.activation(
                out=y_bc[:, c * 128:(c + 1) * 128], in_=p[:, 0:128],
                func=AF.Copy), waits=[("T", tv)], inc="S")
        if debug:
            s.add("gpsimd", dma(dbg["dbg_ysm"][:, :], y_bc[:, 0:NCH]),
                  waits=[("S", s.val("S"))], inc="W", inc_n=16)
        marks["ybc_S"] = s.val("S")
        marks["ybc_V"] = s.val("V")
        marks["ybc_T"] = s.val("T")


    def p2(s):
        for ct in range(NCT):
            if ct == 0:
                w = [("T", marks["sdbc_T"]), ("V", marks["sdbc_V"])]
            elif ct == 12:
                w = [("T", s.val("T"))]
            else:
                w = ()
            s.add("gpsimd", dma(
                woutst[:, ct * 512:(ct + 1) * 512],
                woutT.ap()[ct * 128:(ct + 1) * 128, 0:512]),
                waits=w, inc="W", inc_n=16)
        gate_S = marks["gate_S"]
        for ct in range(NCT):
            for h in range(2):
                c0, c1 = h * 1024, (h + 1) * 1024
                w = []
                if h == 0:
                    if ct == 0:
                        w = [("S", marks["ybc_S"]), ("T", marks["ybc_T"])]
                        if debug:
                            w += [("W", s.val("W")), ("L", s.val("L"))]
                    elif gate_S[ct] > marks["ybc_S"]:
                        w = [("S", gate_S[ct])]
                s.add("vector", lambda e, ct=ct, c0=c0, c1=c1: e.tensor_mul(
                    out=tmp2[:, :], in0=sgT[:, ct * T + c0:ct * T + c1],
                    in1=y_bc[:, c0:c1]), waits=w, inc="V")
                s.add("vector", lambda e, ct=ct, c0=c0, c1=c1: e.scalar_tensor_tensor(
                    out=yskT[:, ct * T + c0:ct * T + c1],
                    in0=xconvT[:, ct * T + c0:ct * T + c1],
                    scalar=dsb[:, ct:ct + 1], in1=tmp2[:, :],
                    op0=ALU.mult, op1=ALU.add), inc="V")
        marks["p2_V"] = s.val("V")
        marks["p2_W"] = s.val("W")

    marks = {}

    def p_mid(s):
        prefetch_gate_early(s)
        p1b(s)
        p1c(s)
        p1d(s, 0, 8)
        p1e(s)
        p1d(s, 8, NCT)
        p2(s)
    p_mid.__name__ = "p_mid"
    blockrun(p_mid)

    def p3(s):
        wout_ready = s.val("W")
        evac = {}
        for dc in range(2):
            if dc == 1:
                tv = s.val("T")
                for ct in range(NCT):
                    s.add("gpsimd", dma(
                        woutst[:, ct * 512:(ct + 1) * 512],
                        woutT.ap()[ct * 128:(ct + 1) * 128, 512:1024]),
                        waits=[("T", tv)], inc="W", inc_n=16)
                wout_ready = s.val("W")
            for tt in range(16):
                p = pAB[tt % 2]
                w = []
                pv = evac.get((dc, tt - 2)) or (evac.get((dc - 1, 14 + tt)) if dc else None)
                if pv:
                    w.append(("S", pv))
                s.add("tensor", lambda e, tt=tt, dc=dc, p=p: e.matmul(
                    out=p[:, 0:512],
                    lhsT=ewT2[:, tt * 128:(tt + 1) * 128],
                    rhs=bl2sb[:, dc * 512:(dc + 1) * 512],
                    start=True, stop=False), waits=w, inc="T")
                w2 = [("W", wout_ready)]
                if dc == 0 and tt == 0:
                    w2 += [("V", marks["p2_V"]), ("W", marks["p2_W"])]
                for ct in range(NCT):
                    s.add("tensor", lambda e, ct=ct, tt=tt, p=p: e.matmul(
                        out=p[:, 0:512],
                        lhsT=yskT[:, ct * T + tt * 128:ct * T + (tt + 1) * 128],
                        rhs=woutst[:, ct * 512:(ct + 1) * 512],
                        start=False, stop=(ct == NCT - 1)),
                        waits=w2 if ct == 0 else (), inc="T")
                tv = s.val("T")
                s.add("scalar", lambda e, tt=tt, dc=dc, p=p: e.activation(
                    out=y_acc[:, tt * D + dc * 512:tt * D + (dc + 1) * 512],
                    in_=p[:, 0:512], func=AF.Copy), waits=[("T", tv)], inc="S")
                evac[(dc, tt)] = s.val("S")
        if debug:
            s.add("sync", dma(dbg["dbg_yac"][:, :], y_acc[:, 0:D]),
                  waits=[("S", s.val("S"))], inc="L", inc_n=16)

    blockrun(p3)

    def p4(s):
        from collections import deque
        wl1_done = {}
        wl2_done = {}
        g_done = {}
        acc = {}
        hist = deque([None, None], maxlen=2)
        ew_done = {}
        p5_ready = {}

        def prefetch_ab(n, hp, wait_t):
            par = hp % 2
            for half, col0 in ((0, hp * 128), (1, HID + hp * 128)):
                slot = (2 * par + half) * D
                s.add("gpsimd", dma(
                    wabst.ap()[:, slot:slot + D].rearrange("p (k n) -> p k n", k=KD),
                    wl1.ap()[n, :, col0:col0 + 128].rearrange(
                        "(k p) n -> p k n", p=128)),
                    waits=[("T", wait_t)], inc="W", inc_n=16)
            wl1_done[(n, hp)] = s.val("W")

        def prefetch_wl2(n, hch, wait_t):
            for i in range(8):
                hc = hch * 8 + i
                s.add("gpsimd", dma(
                    wl2st[:, i * D:(i + 1) * D],
                    wl2.ap()[n, hc * 128:(hc + 1) * 128, :]),
                    waits=[("T", wait_t)], inc="W", inc_n=16)
            wl2_done[(n, hch)] = s.val("W")

        def emit_ewbc(n):
            prev_readers = g_done.get((n - 1, NCT - 1, 3))
            for tc in range(4):
                s.add("tensor", lambda e, n=n, tc=tc: e.matmul(
                    out=pG[:, 0:512], lhsT=selsb[:, n * 128:(n + 1) * 128],
                    rhs=ewT2[:, tc * 512:(tc + 1) * 512], start=True, stop=True),
                    waits=[("S", s.val("S"))], inc="T")
                tv = s.val("T")
                wsc = [("T", tv)]
                if prev_readers:
                    wsc.append(("V", prev_readers))
                s.add("scalar", lambda e, tc=tc: e.activation(
                    out=ew_bcT[:, tc * 512:(tc + 1) * 512], in_=pG[:, 0:512],
                    func=AF.Copy), waits=wsc, inc="S")
            ew_done[n] = s.val("S")

        def emit_mm1_half(n, half):
            for hp in range(half * 8, half * 8 + 8):
                par = hp % 2
                asl = (2 * par) * D
                bsl = (2 * par + 1) * D
                for tc in range(4):
                    pa = pCD[tc % 2]
                    pb = pEF[tc % 2]
                    w = [("W", wl1_done[(n, hp)])]
                    if hist[0] is not None:
                        w.append(("S", hist[0][0]))
                        w.append(("V", hist[0][1]))
                    for k in range(KD):
                        s.add("tensor", lambda e, k=k, tc=tc, asl=asl, pa=pa: e.matmul(
                            out=pa[:, 0:512],
                            lhsT=wabst[:, asl + k * 128:asl + (k + 1) * 128],
                            rhs=xnT[:, k * TH + HALO + tc * 512:k * TH + HALO + (tc + 1) * 512],
                            start=(k == 0), stop=(k == KD - 1)),
                            waits=w if k == 0 else (), inc="T")
                    for k in range(KD):
                        s.add("tensor", lambda e, k=k, tc=tc, bsl=bsl, pb=pb: e.matmul(
                            out=pb[:, 0:512],
                            lhsT=wabst[:, bsl + k * 128:bsl + (k + 1) * 128],
                            rhs=xnT[:, k * TH + HALO + tc * 512:k * TH + HALO + (tc + 1) * 512],
                            start=(k == 0), stop=(k == KD - 1)), inc="T")
                    tv = s.val("T")
                    sa = saA if tc % 2 == 0 else saB
                    sb = saB if tc % 2 == 0 else saA
                    wsl = [("T", tv), ("S", ew_done[n])]
                    pv2 = g_done.get((n, hp, tc - 1)) or g_done.get((n, hp - 1, 3)) \
                        or g_done.get((n - 1, NCT - 1, 3))
                    if pv2:
                        wsl.append(("V", pv2))
                    s.add("scalar", lambda e, hp=hp, n=n, pa=pa, sa=sa: e.activation(
                        out=sa[:, :], in_=pa[:, 0:512], func=AF.Silu,
                        bias=bl1sb[:, n * 32 + hp:n * 32 + hp + 1]),
                        waits=wsl, inc="S")
                    silu_s_val = s.val("S")
                    sv = s.val("S")
                    s.add("vector", lambda e, tc=tc, sa=sa, sb=sb: e.tensor_mul(
                        out=sb[:, :], in0=sa[:, :],
                        in1=ew_bcT[:, tc * 512:(tc + 1) * 512]),
                        waits=[("S", sv)], inc="V")
                    s.add("vector", lambda e, n=n, hp=hp, tc=tc, pb=pb, sb=sb:
                          e.scalar_tensor_tensor(
                              out=gT[:, hp * T + tc * 512:hp * T + (tc + 1) * 512],
                              in0=pb[:, 0:512],
                              scalar=bl1sb[:, n * 32 + 16 + hp:n * 32 + 16 + hp + 1],
                              in1=sb[:, :], op0=ALU.add, op1=ALU.mult), inc="V")
                    g_done[(n, hp, tc)] = s.val("V")
                    hist.append((silu_s_val, s.val("V")))
                nxt = (n, hp + 2)
                if hp + 2 >= NCT:
                    nxt = (n + 1, hp + 2 - NCT)
                if nxt[0] < NE:
                    prefetch_ab(nxt[0], nxt[1], s.val("T"))

        res_done = {}

        def emit_mm2_half(n, hch):
            wl2v = wl2_done[(n, hch)]
            gv = g_done[(n, hch * 8 + 7, 3)]
            final = (n == NE - 1 and hch == 1)
            if final:
                for tt0 in (0, 1):
                    s.add("sync", dma(xres[:, (tt0 % 2) * D:(tt0 % 2) * D + D],
                                      x_own[tt0 * 128:(tt0 + 1) * 128, :]),
                          waits=[("T", marks["mm1_done"])], inc="L", inc_n=16)
                    res_done[("L", tt0)] = s.val("L")
            for tt in range(16):
                for dc in range(2):
                    p = pAB[(tt * 2 + dc) % 2]
                    w = [("W", wl2v), ("V", gv)]
                    pv = acc.get((n, hch, tt - 1, dc)) or \
                        (acc.get((n, hch - 1, 15, 1)) if hch else
                         acc.get((n - 1, 1, 15, 1)))
                    if pv:
                        w.append(("V", pv))
                    for i in range(8):
                        hc = hch * 8 + i
                        s.add("tensor", lambda e, i=i, hc=hc, tt=tt, dc=dc, p=p:
                              e.matmul(
                                  out=p[:, 0:512],
                                  lhsT=gT[:, hc * T + tt * 128:hc * T + (tt + 1) * 128],
                                  rhs=wl2st[:, i * D + dc * 512:i * D + (dc + 1) * 512],
                                  start=(i == 0), stop=(i == 7)),
                              waits=w if i == 0 else (), inc="T")
                    tv = s.val("T")
                    s.add("vector", lambda e, tt=tt, dc=dc, p=p: e.tensor_add(
                        out=y_acc[:, tt * D + dc * 512:tt * D + (dc + 1) * 512],
                        in0=y_acc[:, tt * D + dc * 512:tt * D + (dc + 1) * 512],
                        in1=p[:, 0:512]), waits=[("T", tv)], inc="V")
                    acc[(n, hch, tt, dc)] = s.val("V")
                if final:
                    sl = (tt % 2) * D
                    s.add("vector", lambda e, tt=tt, sl=sl: e.tensor_add(
                        out=y_acc[:, tt * D:(tt + 1) * D],
                        in0=y_acc[:, tt * D:(tt + 1) * D], in1=xres[:, sl:sl + D]),
                        waits=[("L", res_done[("L", tt)])], inc="V")
                    res_done[tt] = s.val("V")
                    s.add("sync", dma(out[tt * 128:(tt + 1) * 128, :],
                                      y_acc[:, tt * D:(tt + 1) * D]),
                          waits=[("V", res_done[tt])], inc="L", inc_n=16)
                    if tt + 2 < 16:
                        s.add("sync", dma(xres[:, sl:sl + D],
                                          x_own[(tt + 2) * 128:(tt + 3) * 128, :]),
                              waits=[("V", res_done[tt])], inc="L", inc_n=16)
                        res_done[("L", tt + 2)] = s.val("L")

        prefetch_ab(0, 0, 0)
        prefetch_ab(0, 1, 0)
        emit_ewbc(0)
        emit_mm1_half(0, 0)
        prefetch_wl2(0, 0, 0)
        emit_mm1_half(0, 1)
        emit_mm2_half(0, 0)
        for n in range(1, NE):
            emit_ewbc(n)
            prefetch_wl2(n - 1, 1, s.val("T"))
            emit_mm1_half(n, 0)
            emit_mm2_half(n - 1, 1)
            prefetch_wl2(n, 0, s.val("T"))
            emit_mm1_half(n, 1)
            emit_mm2_half(n, 0)
        marks["mm1_done"] = s.val("T")
        prefetch_wl2(NE - 1, 1, s.val("T"))
        emit_mm2_half(NE - 1, 1)
        lv = s.val("L")
        s.add("sync", lambda e: e.nop(), waits=[("L", lv)])

    blockrun(p4)


    return nc, sbuf_used


def _prep_inputs(inputs):
    f32 = np.float32
    x = np.ascontiguousarray(inputs["L0_attn_x"], dtype=f32).reshape(2 * 8192, D)
    winT_h = np.ascontiguousarray(inputs["L0_attn_in_proj_w"].T, dtype=f32)
    wstack_h = np.zeros((DI, 64), dtype=f32)
    wstack_h[:, 0:16] = inputs["L0_attn_delta_w"].T
    wstack_h[:, 16:32] = inputs["L0_attn_B_w"].T
    wstack_h[:, 32:48] = inputs["L0_attn_C_w"].T
    wstack_h[:, 48] = 1.0 / DI
    cw = np.ascontiguousarray(inputs["L0_attn_conv_weight"], dtype=f32)
    conv_w_h = np.zeros((128, 64), dtype=f32)
    for ct in range(NCT):
        conv_w_h[:, ct * 4:(ct + 1) * 4] = cw[ct * 128:(ct + 1) * 128, :]
    conv_b_h = inputs["L0_attn_conv_bias"].astype(f32).reshape(NCT, 128).T.copy()
    d_col_h = inputs["L0_attn_D"].astype(f32).reshape(NCT, 128).T.copy()
    a_log_h = inputs["L0_attn_A_log"].astype(f32).reshape(1, DS)
    delta_b_h = inputs["L0_attn_delta_b"].astype(f32).reshape(1, DS)
    woutT_h = np.ascontiguousarray(inputs["L0_attn_out_proj_w"].T, dtype=f32)
    wrT_h = np.ascontiguousarray(inputs["L0_ffn_Wr"].T, dtype=f32)
    br_h = inputs["L0_ffn_br"].astype(f32).reshape(1, NE)
    bn_h = inputs["L0_ffn_biases_N"].astype(f32).reshape(1, NE)
    wl1_h = np.ascontiguousarray(inputs["L0_ffn_Wl1"], dtype=f32)
    bl1 = inputs["L0_ffn_bl1"].astype(f32)
    bl1c_h = np.zeros((128, 256), dtype=f32)
    for n in range(NE):
        bl1c_h[:, n * 32:(n + 1) * 32] = bl1[n].reshape(32, 128).T
    wl2_h = np.ascontiguousarray(inputs["L0_ffn_Wl2"], dtype=f32)
    bl2_h = np.ascontiguousarray(inputs["L0_ffn_bl2"], dtype=f32)

    maps = []
    for c in range(NCORE):
        t0 = c * T
        chain_pos = c % 4
        halo = (np.zeros((HALO, D), dtype=f32) if chain_pos == 0
                else x[t0 - HALO:t0].copy())
        oh = np.zeros((1, NCORE), dtype=f32)
        oh[0, c] = 1.0
        wmT = np.zeros((NCORE, NCORE), dtype=f32)
        mv = np.zeros((NCORE, 1), dtype=f32)
        chain = range((c // 4) * 4, (c // 4) * 4 + 4)
        for cp in chain:
            if cp < c:
                mv[cp, 0] = 1.0
                for j in chain:
                    if cp < j < c:
                        wmT[j, cp] = 1.0
        maps.append({
            "x_own": x[t0:t0 + T].copy(), "x_halo": halo,
            "winT": winT_h, "wstack": wstack_h, "conv_w": conv_w_h,
            "conv_b": conv_b_h, "d_col": d_col_h, "a_log": a_log_h,
            "delta_b": delta_b_h, "woutT": woutT_h, "wrT": wrT_h,
            "br_r": br_h, "bn_r": bn_h, "bn16": np.tile(bn_h, (1, 16)), "wl1": wl1_h, "bl1c": bl1c_h,
            "wl2": wl2_h, "bl2": bl2_h, "oh": oh, "wmaskT": wmT, "mvec": mv,
        })
    return maps


_NC_CACHE = {}


def kernel(**inputs):
    key = "main"
    if key not in _NC_CACHE:
        _NC_CACHE[key] = build(debug=False)
    nc, _ = _NC_CACHE[key]
    maps = _prep_inputs(inputs)
    res = run_bass_kernel_spmd(nc, maps, core_ids=list(range(NCORE)))
    outs = [res.results[c]["out"] for c in range(NCORE)]
    return np.concatenate(outs, axis=0).reshape(2, 8192, D).astype(np.float32)


def check_deadlock():
    sem = {k: 0 for k in ("L", "W", "T", "V", "S", "C")}
    for bname, ops in SCHEDS:
        ptr = {e: 0 for e in ENGS}
        stuck = False
        while True:
            progress = False
            for e in ENGS:
                while ptr[e] < len(ops[e]):
                    waits, inc, n = ops[e][ptr[e]]
                    if all(sem[s] >= v for (s, v) in waits):
                        if inc:
                            sem[inc] += n
                        ptr[e] += 1
                        progress = True
                    else:
                        break
            if all(ptr[e] == len(ops[e]) for e in ENGS):
                break
            if not progress:
                stuck = True
                print(f"DEADLOCK in block {bname}:")
                for e in ENGS:
                    if ptr[e] < len(ops[e]):
                        waits, inc, n = ops[e][ptr[e]]
                        unmet = [(s, v, sem[s]) for (s, v) in waits if sem[s] < v]
                        print(f"  {e} op#{ptr[e]}/{len(ops[e])} blocked on {unmet}")
                break
        if stuck:
            return False
    print("schedule: no deadlock")
    return True


if __name__ == "__main__":
    nc, used = build(debug=True)
    print(f"sbuf used: {used} bytes/partition")
    print(f"instructions: {nc.n_instructions()}")
    check_deadlock()


# version 11
# speedup vs baseline: 1.1982x; 1.0223x over previous
import numpy as np
import ml_dtypes

import concourse.bass as bass
import concourse.mybir as mybir
from concourse.bass_utils import run_bass_kernel_spmd

F32 = mybir.dt.float32
BF16 = mybir.dt.bfloat16
AF = mybir.ActivationFunctionType
ALU = mybir.AluOpType
AX = mybir.AxisListType.X
BF = ml_dtypes.bfloat16

T = 2048
HALO = 4
TH = T + HALO
D = 1024
KD = 8
DI = 2048
NCT = 16
DS = 16
Q = 128
NCH = 16
NE = 8
HID = 2048
NCORE = 8

ENGS = ("sync", "gpsimd", "tensor", "vector", "scalar")

SCHEDS = []


class Sched:

    def __init__(self, nc, sems):
        self.nc = nc
        self.ops = {e: [] for e in ENGS}
        self.sems = sems

    def add(self, eng, fn, waits=(), inc=None, inc_n=1):
        w = [(s, v) for (s, v) in waits]
        self.ops[eng].append((fn, w, inc, inc_n))
        if inc is not None:
            self.sems[inc][1] += inc_n
        return self.sems[inc][1] if inc is not None else None

    def val(self, sem):
        return self.sems[sem][1]

    def emit(self, block):
        for eng_name in ENGS:
            ops = self.ops[eng_name]
            if not ops:
                continue

            def body(eng, ops=ops, eng_name=eng_name):
                maxw = {}
                for fn, waits, inc, inc_n in ops:
                    for s, v in waits:
                        if v <= 0:
                            continue
                        if maxw.get(s, -1) >= v:
                            continue
                        eng.wait_ge(self.sems[s][0], v)
                        maxw[s] = v
                    instr = fn(eng)
                    if inc is not None:
                        instr.then_inc(self.sems[inc][0], inc_n)
            getattr(block, eng_name)(body)


def build(debug=False):
    nc = bass.Bass("TRN2", target_bir_lowering=False, num_devices=NCORE,
                   detect_race_conditions=False)

    def ein(name, shape):
        return nc.dram_tensor(name, list(shape), F32, kind="ExternalInput")

    x_own = ein("x_own", (T, D))
    x_halo = ein("x_halo", (HALO, D))
    winT = ein("winT", (D, 2 * DI))
    wstack = ein("wstack", (DI, 64))
    conv_w = ein("conv_w", (128, 64))
    conv_b = ein("conv_b", (128, NCT))
    d_col = ein("d_col", (128, NCT))
    a_log = ein("a_log", (1, DS))
    delta_b = ein("delta_b", (1, DS))
    woutT = ein("woutT", (DI, D))
    wrT = ein("wrT", (D, NE))
    br_r = ein("br_r", (1, NE))
    bn_r = ein("bn_r", (1, NE))
    bn16 = ein("bn16", (1, 128))
    wl1 = ein("wl1", (NE, D, 2 * HID))
    bl1c = ein("bl1c", (128, 256))
    wl2 = ein("wl2", (NE, HID, D))
    bl2 = ein("bl2", (NE, D))
    oh_in = ein("oh", (1, NCORE))
    wmaskT = ein("wmaskT", (NCORE, NCORE))
    mvec = ein("mvec", (NCORE, 1))

    out = nc.dram_tensor("out", [T, D], F32, kind="ExternalOutput")
    cc_buf = nc.dram_tensor("cc_buf", [NCORE, 32], F32)

    dbg = {}
    if debug:
        for name, shape in (
            ("dbg_xn", (128, D)), ("dbg_xconv", (128, T)), ("dbg_sg", (128, T)),
            ("dbg_delta", (128, 256)), ("dbg_ysm", (128, NCH)), ("dbg_ew", (128, 128)),
            ("dbg_yac", (128, D)), ("dbg_su", (8, 32)), ("dbg_g", (128, 256)),
            ("dbg_ycol", (128, NCH)), ("dbg_xnT", (128, 128)),
            ("dbg_sm", (128, 48)), ("dbg_a", (128, 256)), ("dbg_bcum", (128, 256)),
            ("dbg_yloc", (128, NCH)), ("dbg_sm2", (128, 48)), ("dbg_cr", (128, 48)),
            ("dbg_w2", (1, 256)), ("dbg_er", (1, 256)), ("dbg_hr", (1, 272)),
        ):
            dbg[name] = nc.dram_tensor(name, list(shape), F32, kind="ExternalOutput")

    ident_bf = nc.inline_tensor(np.eye(128, dtype=BF), "c_identbf")
    ident_f = nc.inline_tensor(np.eye(128, dtype=np.float32), "c_identf")
    tri = np.tril(np.ones((128, 128), dtype=np.float32))
    triT_c = nc.inline_tensor(np.ascontiguousarray(tri.T), "c_triT")
    ones_row_c = nc.inline_tensor(np.ones((1, 128), dtype=np.float32), "c_onesrow")
    sel_np = np.zeros((NE, NE * 128), dtype=BF)
    for n in range(NE):
        sel_np[n, n * 128:(n + 1) * 128] = 1
    sel_c = nc.inline_tensor(sel_np, "c_sel")

    BASE = 16512
    cur = [BASE]

    def at(name, shape, dtype, offset=None):
        if offset is None:
            offset = cur[0]
            sz = int(np.prod(shape[1:])) * mybir.dt.size(dtype)
            cur[0] = (offset + sz + 31) & ~31
        return nc.alloc_sbuf_tensor_at(name, list(shape), dtype, offset=offset)

    xnT = at("xnT", (128, KD * TH), BF16)
    B1 = cur[0]
    xconvT = at("xconvT", (128, NCT * T), BF16)
    y_acc = at("y_acc", (128, 16 * D), F32, offset=B1)
    B2 = cur[0]
    sgT = at("sgT", (128, NCT * T), BF16)
    yskT = at("yskT", (128, NCT * T), BF16, offset=B2)
    gT = at("gT", (128, NCT * T), BF16, offset=B2)
    S0 = cur[0]
    xs = at("xs", (128, TH), F32, offset=S0)
    convA = at("convA", (128, 512), F32, offset=S0 + 8224)
    convB = at("convB", (128, 512), F32, offset=S0 + 8224 + 2048)
    winst = at("winst", (128, 2048), BF16, offset=S0 + 8224 + 4096)
    xld = at("xld", (128, 2 * D), F32, offset=S0 + 16416)
    sdbc = at("sdbc", (64, T), F32, offset=S0)
    woutst = at("woutst", (128, NCT * 512), BF16, offset=S0)
    wabst = at("wabst", (128, 4 * D), BF16, offset=S0)
    wl2st = at("wl2st", (128, 8 * D), BF16, offset=S0 + 8192)
    cur[0] = S0 + 24640
    R0 = cur[0]
    dpre = at("dpre", (128, 256), F32, offset=R0)
    sBs = at("sBs", (128, 256), F32, offset=R0 + 1024)
    sCs = at("sCs", (128, 256), F32, offset=R0 + 2048)
    sA_ = at("sA_", (128, 256), F32, offset=R0 + 3072)
    sG = at("sG", (128, 256), F32, offset=R0 + 4096)
    sT1 = at("sT1", (128, 256), F32, offset=R0 + 5120)
    sT2 = at("sT2", (128, 256), F32, offset=R0 + 6144)
    tmp2 = at("tmp2", (128, 1024), F32, offset=R0)
    saA = at("saA", (128, 512), F32, offset=R0)
    saB = at("saB", (128, 512), F32, offset=R0 + 2048)
    convA2 = at("convA2", (128, 512), F32, offset=R0 + 4096)
    convB2 = at("convB2", (128, 512), F32, offset=R0 + 6144)
    cur[0] = R0 + 7168
    YB = cur[0]
    y_bc = at("y_bc", (128, T), BF16)
    ew_bcT = at("ew_bcT", (128, T), BF16, offset=YB)
    ew = at("ew", (128, 128), F32, offset=R0 + 4096)
    ewT2 = at("ewT2", (NE, T), BF16)
    xavg = at("xavg", (128, NCH), F32)
    y_col = at("y_col", (128, NCH), F32)
    yfin = xavg
    glr = dpre[0:1, 0:256]
    blr = sBs[0:1, 0:256]
    gcum = sdbc[0:1, 0:272]
    hrow = sdbc[0:1, 272:544]
    yrow_stage = dpre[0:1, 0:128]
    erow = sCs[0:1, 0:DS]
    wrow = sCs[0:1, DS:2 * DS]
    arow_r = sCs[0:1, 2 * DS:3 * DS]
    hin = sCs[0:1, 3 * DS:4 * DS]
    arow = sCs[0:1, 0:DS]
    abc = at("abc", (128, DS), F32)
    dbbc = at("dbbc", (128, DS), F32)
    brbc = at("brbc", (128, NE), F32)
    bnbc = at("bnbc", (128, NE), F32)
    su8 = at("su8", (NCORE, 32), F32)
    gput = at("gput", (NCORE, 32), F32)
    e8 = at("e8", (NCORE, DS), F32)
    ctrb = at("ctrb", (NCORE, DS), F32)
    xres = at("xres", (128, 2 * D), F32, offset=S0)
    wstk = at("wstk", (128, 2 * 64), BF16)
    cwsb = at("cwsb", (128, 64), F32)
    cbsb = at("cbsb", (128, NCT), F32)
    dsb = at("dsb", (128, NCT), F32)
    alsb = at("alsb", (1, DS), F32)
    dbsb = at("dbsb", (1, DS), F32)
    brsb = at("brsb", (1, NE), F32)
    bnsb = at("bnsb", (1, NE), F32)
    bl1sb = at("bl1sb", (128, 256), F32)
    bl2sb = at("bl2sb", (NE, D), BF16)
    wrsb = at("wrsb", (128, 64), BF16)
    ohsb = at("ohsb", (1, NCORE), F32)
    wmsb = at("wmsb", (NCORE, NCORE), F32)
    mvsb = at("mvsb", (NCORE, 1), F32)
    idbf = at("idbf", (128, 128), BF16)
    idf = at("idf", (128, 128), F32)
    trit = at("trit", (128, 128), F32)
    onesrow = at("onesrow", (1, 128), F32)
    selsb = at("selsb", (NE, NE * 128), BF16)

    sbuf_used = cur[0]

    pA = nc.alloc_psum_tensor("pA", [128, 512], F32)
    pB = nc.alloc_psum_tensor("pB", [128, 512], F32)
    pC = nc.alloc_psum_tensor("pC", [128, 512], F32)
    pD = nc.alloc_psum_tensor("pD", [128, 512], F32)
    pE = nc.alloc_psum_tensor("pE", [128, 512], F32)
    pF = nc.alloc_psum_tensor("pF", [128, 512], F32)
    pG = nc.alloc_psum_tensor("pG", [128, 512], F32)
    pH = nc.alloc_psum_tensor("pH", [128, 1024], BF16)
    pAB = [pA, pB]
    pCD = [pC, pD]
    pEF = [pE, pF]

    sems = {k: [nc.alloc_semaphore(k), 0] for k in ("L", "W", "T", "V", "S", "C")}

    def blockrun(fill):
        sch = Sched(nc, sems)
        fill(sch)
        SCHEDS.append((fill.__name__, {e: [(w, inc, n) for (f, w, inc, n) in sch.ops[e]]
                                       for e in ENGS}))
        with nc.Block() as blk:
            sch.emit(blk)

    def dma(dst, src):
        return lambda e: e.dma_start(out=dst, in_=src)

    def cp(dst, src):
        return lambda e: e.tensor_copy(out=dst, in_=src)

    def p0a(s):
        for dst, src in (
            (cwsb[:, :], conv_w[:, :]), (cbsb[:, :], conv_b[:, :]),
            (dsb[:, :], d_col[:, :]), (alsb[:, :], a_log[:, :]),
            (dbsb[:, :], delta_b[:, :]), (brsb[:, :], br_r[:, :]),
            (bnsb[:, :], bn_r[:, :]), (bl1sb[:, :], bl1c[:, :]),
            (ohsb[:, :], oh_in[:, :]), (wmsb[:, :], wmaskT[:, :]),
            (mvsb[:, :], mvec[:, :]), (idbf[:, :], ident_bf[:, :]),
            (idf[:, :], ident_f[:, :]), (trit[:, :], triT_c[:, :]),
            (onesrow[:, :], ones_row_c[:, :]),
            (selsb[:, :], sel_c[:, :]),
        ):
            s.add("sync", dma(dst, src), inc="L", inc_n=16)
        s.add("gpsimd", dma(bl2sb[:, :], bl2[:, :]), inc="W", inc_n=16)
        s.add("gpsimd",
              dma(wrsb.ap().rearrange("p (k n) -> p k n", k=KD),
                  wrT.ap().rearrange("(k p) n -> p k n", p=128)),
              inc="W", inc_n=16)
        s.add("gpsimd",
              dma(winst.ap()[:, 0:1024].rearrange("p (k n) -> p k n", k=KD),
                  winT.ap()[:, 0:128].rearrange("(k p) n -> p k n", p=128)),
              inc="W", inc_n=16)
        s.add("gpsimd", dma(wstk[:, 0:64], wstack[0:128, :]), inc="W", inc_n=16)
        lv = s.val("L")
        s.add("scalar", lambda e: e.activation(out=arow[:, :], in_=alsb[:, :],
                                               func=AF.Exp),
              waits=[("L", lv)], inc="S")
        s.add("vector", lambda e: e.tensor_scalar(out=arow[:, :], in0=arow[:, :],
                                                  scalar1=-1.0, scalar2=None,
                                                  op0=ALU.mult),
              waits=[("S", s.val("S"))], inc="V")
        vv = s.val("V")
        s.add("tensor", lambda e: e.matmul(out=pG[:, 0:DS], lhsT=onesrow[:, :],
                                           rhs=arow[:, :], start=True, stop=True),
              waits=[("V", vv), ("L", lv)], inc="T")
        s.add("vector", cp(abc[:, :], pG[:, 0:DS]), waits=[("T", s.val("T"))], inc="V")
        vv = s.val("V")
        s.add("tensor", lambda e: e.matmul(out=pG[:, 64:64 + DS], lhsT=onesrow[:, :],
                                           rhs=dbsb[:, :], start=True, stop=True),
              waits=[("V", vv)], inc="T")
        s.add("vector", cp(dbbc[:, :], pG[:, 64:64 + DS]), waits=[("T", s.val("T"))], inc="V")
        vv = s.val("V")
        s.add("tensor", lambda e: e.matmul(out=pG[:, 128:128 + NE], lhsT=onesrow[:, :],
                                           rhs=brsb[:, :], start=True, stop=True),
              waits=[("V", vv)], inc="T")
        s.add("vector", cp(brbc[:, :], pG[:, 128:128 + NE]), waits=[("T", s.val("T"))], inc="V")
        vv = s.val("V")
        s.add("tensor", lambda e: e.matmul(out=pG[:, 192:192 + NE], lhsT=onesrow[:, :],
                                           rhs=bnsb[:, :], start=True, stop=True),
              waits=[("V", vv)], inc="T")
        s.add("vector", cp(bnbc[:, :], pG[:, 192:192 + NE]), waits=[("T", s.val("T"))], inc="V")

    blockrun(p0a)

    def p0b(s):
        center_done = {}
        s.add("vector", lambda e: e.memset(convA[:, 7:8], 1e-5), inc="V")
        for ti in range(-1, 16):
            rows = HALO if ti < 0 else 128
            if ti < 0:
                ldst = xld[0:HALO, 0:D]
                lsrc = x_halo[:, :]
            else:
                sl = (ti % 2) * D
                ldst = xld[:, sl:sl + D]
                lsrc = x_own[ti * 128:(ti + 1) * 128, :]
            prev_user = center_done.get(ti - 2)
            if ti == 0:
                prev_user = center_done.get(-1)
            s.add("sync", dma(ldst, lsrc),
                  waits=[("V", prev_user)] if prev_user else (), inc="L", inc_n=16)
            lv = s.val("L")
            src = ldst
            wsq = [("L", lv)]
            if center_done.get("vq"):
                wsq.append(("V", center_done["vq"]))
            s.add("scalar", lambda e, src=src, rows=rows: e.activation(
                out=tmp2[0:rows, 0:D], in_=src, func=AF.Square),
                waits=wsq, inc="S")
            sq_s = s.val("S")
            s.add("vector", lambda e, src=src, rows=rows: e.reduce_sum(
                out=convA[0:rows, 0:1], in_=src, axis=AX), waits=[("L", lv)], inc="V")
            vv = s.val("V")
            s.add("vector", lambda e, rows=rows: e.reduce_sum(
                out=convA[0:rows, 2:3], in_=tmp2[0:rows, 0:D], axis=AX),
                waits=[("S", sq_s)], inc="V")
            vq = s.val("V")
            center_done["vq"] = vq
            s.add("scalar", lambda e, rows=rows: e.activation(
                out=convA[0:rows, 1:2], in_=convA[0:rows, 0:1], func=AF.Identity,
                scale=1.0 / D), waits=[("V", vv)], inc="S")
            s.add("scalar", lambda e, rows=rows: e.activation(
                out=convA[0:rows, 3:4], in_=convA[0:rows, 1:2], func=AF.Square), inc="S")
            s.add("scalar", lambda e, rows=rows: e.activation(
                out=convA[0:rows, 4:5], in_=convA[0:rows, 3:4], func=AF.Identity,
                scale=-1.0, bias=convA[0:rows, 7:8]), inc="S")
            s.add("scalar", lambda e, rows=rows: e.activation(
                out=convA[0:rows, 5:6], in_=convA[0:rows, 2:3], func=AF.Ln,
                scale=1.0 / D, bias=convA[0:rows, 4:5]), waits=[("V", vq)], inc="S")
            s.add("scalar", lambda e, rows=rows: e.activation(
                out=convA[0:rows, 6:7], in_=convA[0:rows, 5:6], func=AF.Exp,
                scale=-0.5), inc="S")
            wyb = [("S", s.val("S"))]
            if debug and ti == 1:
                wyb.append(("W", dbg_xn_w))
            s.add("vector", lambda e, src=src, rows=rows: e.tensor_scalar(
                out=y_bc[0:rows, 0:D], in0=src,
                scalar1=convA[0:rows, 1:2], scalar2=convA[0:rows, 6:7],
                op0=ALU.subtract, op1=ALU.mult),
                waits=wyb, inc="V")
            center_done[ti] = s.val("V")
            vv = s.val("V")
            dbg_gate = None
            if debug and ti == 0:
                s.add("gpsimd", dma(dbg["dbg_xn"][:, :], y_bc[:, 0:D]),
                      waits=[("V", s.val("V"))], inc="W", inc_n=16)
                dbg_xn_w = s.val("W")
            for k in range(KD):
                tv_prev = s.val("V")
                s.add("tensor", lambda e, k=k, rows=rows: e.transpose(
                    out=pH[:, k * 128:k * 128 + rows],
                    in_=y_bc[0:rows, k * 128:(k + 1) * 128],
                    identity=idbf[0:rows, 0:rows]),
                    waits=[("V", vv)], inc="T")
            tv = s.val("T")
            col = 0 if ti < 0 else HALO + ti * 128
            s.add("vector", lambda e, col=col, rows=rows: e.tensor_copy(
                out=xnT.ap().rearrange("p (k t) -> p k t", k=KD)[:, :, col:col + rows],
                in_=pH.ap().rearrange("p (k t) -> p k t", k=KD)[:, :, 0:rows]),
                waits=[("T", tv)], inc="V")

    blockrun(p0b)

    if debug:
        def p0b2(s):
            s.add("gpsimd", dma(dbg["dbg_xnT"][:, :],
                                xnT[:, HALO:HALO + 128]),
                  waits=[("V", s.val("V"))], inc="W", inc_n=16)
        blockrun(p0b2)

    def p0c(s):
        sc = tmp2[:, 0:128]
        sp_ = tmp2[:, 128:256]
        mk1 = tmp2[:, 256:384]
        sp2_ = tmp2[:, 384:512]
        mk = tmp2[:, 512:640]
        esub = tmp2[:, 640:768]
        eall = tmp2[:, 768:896]
        m1 = tmp2[:, 896:912]
        m2 = tmp2[:, 912:928]
        nm = tmp2[:, 928:944]
        den = tmp2[:, 944:960]
        rinv = tmp2[:, 976:992]

        def t3(ap2d):
            return ap2d.rearrange("p (t k) -> p t k", k=NE)

        def b3(ap16):
            return ap16.rearrange("p (t o) -> p t o", o=1).broadcast_to([128, 16, NE])

        s.add("sync", dma(convB[0:1, 0:128], bn16[:, :]), inc="L", inc_n=16)
        lv = s.val("L")
        s.add("tensor", lambda e: e.matmul(out=pG[:, 0:128], lhsT=onesrow[:, :],
                                           rhs=convB[0:1, 0:128], start=True,
                                           stop=True), waits=[("L", lv)], inc="T")
        s.add("vector", cp(convA[:, 0:128], pG[:, 0:128]),
              waits=[("T", s.val("T"))], inc="V")
        sc_done = {}
        for tt in range(16):
            p = pAB[tt % 2]
            w = [("W", 32)]
            pv = sc_done.get(tt - 2)
            if pv:
                w.append(("V", pv))
            for k in range(KD):
                s.add("tensor", lambda e, k=k, tt=tt, p=p: e.matmul(
                    out=p[:, 0:NE],
                    lhsT=xnT[:, k * TH + HALO + tt * 128: k * TH + HALO + (tt + 1) * 128],
                    rhs=wrsb[:, k * NE:(k + 1) * NE],
                    start=(k == 0), stop=(k == KD - 1)),
                    waits=w if k == 0 else (), inc="T")
            tv = s.val("T")
            s.add("vector", lambda e, p=p, tt=tt: e.tensor_add(
                out=sc[:, tt * 8:(tt + 1) * 8], in0=p[:, 0:NE], in1=brbc[:, :]),
                waits=[("T", tv)], inc="V")
            sc_done[tt] = s.val("V")
        s.add("vector", lambda e: e.tensor_add(out=sp_, in0=sc, in1=convA[:, 0:128]), inc="V")
        s.add("vector", lambda e: e.tensor_reduce(out=m1, in_=t3(sp_), axis=AX,
                                                  op=ALU.max), inc="V")
        s.add("vector", lambda e: e.tensor_tensor(out=t3(mk1), in0=t3(sp_),
                                                  in1=b3(m1), op=ALU.is_ge), inc="V")
        s.add("vector", lambda e: e.scalar_tensor_tensor(
            out=sp2_, in0=mk1, scalar=-1e30, in1=sp_, op0=ALU.mult, op1=ALU.add), inc="V")
        s.add("vector", lambda e: e.tensor_reduce(out=m2, in_=t3(sp2_), axis=AX,
                                                  op=ALU.max), inc="V")
        s.add("vector", lambda e: e.tensor_tensor(out=t3(esub), in0=t3(sp2_),
                                                  in1=b3(m2), op=ALU.is_ge), inc="V")
        s.add("vector", lambda e: e.tensor_add(out=mk, in0=mk1, in1=esub), inc="V")
        s.add("vector", lambda e: e.tensor_reduce(out=nm, in_=t3(sc), axis=AX,
                                                  op=ALU.max), inc="V")
        s.add("vector", lambda e: e.tensor_tensor(out=t3(esub), in0=t3(sc),
                                                  in1=b3(nm), op=ALU.subtract), inc="V")
        vv = s.val("V")
        s.add("scalar", lambda e: e.activation(out=eall, in_=esub, func=AF.Exp),
              waits=[("V", vv)], inc="S")
        sv = s.val("S")
        s.add("vector", lambda e: e.tensor_mul(out=esub, in0=eall, in1=mk),
              waits=[("S", sv)], inc="V")
        s.add("vector", lambda e: e.tensor_reduce(out=den, in_=t3(esub), axis=AX,
                                                  op=ALU.add), inc="V")
        vv = s.val("V")
        s.add("scalar", lambda e: e.activation(out=tmp2[:, 960:976], in_=den,
                                               func=AF.Ln), waits=[("V", vv)], inc="S")
        s.add("scalar", lambda e: e.activation(out=rinv, in_=tmp2[:, 960:976],
                                               func=AF.Exp, scale=-1.0), inc="S")
        sv = s.val("S")
        s.add("vector", lambda e: e.tensor_tensor(
            out=ew.ap().rearrange("p (t k) -> p t k", k=NE), in0=t3(esub),
            in1=b3(rinv), op=ALU.mult), waits=[("S", sv)], inc="V")
        if debug:
            s.add("sync", dma(dbg["dbg_sm"][:, :], mk[:, 0:48]),
                  waits=[("V", s.val("V"))], inc="L", inc_n=16)
            s.add("sync", dma(dbg["dbg_ew"][:, :], ew[:, :]), inc="L", inc_n=16)
            s.add("vector", lambda e: e.engine_nop(),
                  waits=[("L", s.val("L"))], inc="V")
        for tt in range(16):
            vv = s.val("V")
            s.add("tensor", lambda e, tt=tt: e.transpose(
                out=pG[0:8, 0:128], in_=ew[:, tt * 8:(tt + 1) * 8],
                identity=idf[:, :]), waits=[("V", vv)], inc="T")
            s.add("vector", lambda e, tt=tt: e.tensor_copy(
                out=ewT2[:, tt * 128:(tt + 1) * 128], in_=pG[0:8, 0:128]),
                waits=[("T", s.val("T"))], inc="V")

    blockrun(p0c)

    ssm_chunks = [(0, 512), (512, 1024), (1024, 1536), (1536, 2048), (2048, 2052)]

    def p1a(s):
        evac_done = {}
        cast_done = {}
        silu_done = {}
        pool_done = {}
        win_done = {}
        wstk_w = {}
        win_done[0] = 48
        wstk_w[0] = 64

        def emit_sdbc(ct):
            for q in range(4):
                qc = q * 512
                psd = [pC, pD, pE, pF][q]
                w = [("S", cast_done[(ct, q)])]
                if q == 0:
                    w.append(("W", wstk_w[ct]))
                s.add("tensor", lambda e, ct=ct, qc=qc, psd=psd: e.matmul(
                    out=psd[0:64, 0:512],
                    lhsT=wstk[:, (ct % 2) * 64:(ct % 2) * 64 + 64],
                    rhs=xconvT[:, ct * T + qc:ct * T + qc + 512],
                    start=(ct == 0), stop=(ct == NCT - 1)),
                    waits=w, inc="T")

        for ct in range(NCT):
            if ct + 1 < NCT:
                wt = s.val("T") if ct >= 1 else 0
                c0 = (ct + 1) * 128
                s.add("gpsimd", dma(
                    winst.ap()[:, ((ct + 1) % 2) * 1024:((ct + 1) % 2 + 1) * 1024].rearrange("p (k n) -> p k n", k=KD),
                    winT.ap()[:, c0:c0 + 128].rearrange("(k p) n -> p k n", p=128)),
                    waits=[("T", wt)], inc="W", inc_n=16)
                win_done[ct + 1] = s.val("W")
            for ci, (c0, c1) in enumerate(ssm_chunks):
                p = pAB[ci % 2]
                w = [("W", win_done[ct])]
                if ci >= 2:
                    prev = evac_done.get((ct, ci - 2))
                else:
                    prev = evac_done.get((ct - 1, 4 if ci == 0 else 3))
                if prev:
                    w.append(("V", prev))
                for k in range(KD):
                    s.add("tensor", lambda e, k=k, ct=ct, c0=c0, c1=c1, p=p: e.matmul(
                        out=p[:, 0:c1 - c0],
                        lhsT=winst[:, (ct % 2) * 1024 + k * 128:(ct % 2) * 1024 + (k + 1) * 128],
                        rhs=xnT[:, k * TH + c0:k * TH + c1],
                        start=(k == 0), stop=(k == KD - 1)),
                        waits=w if k == 0 else (), inc="T")
                tv = s.val("T")
                s.add("vector", lambda e, c0=c0, c1=c1, p=p: e.tensor_copy(
                    out=xs[:, c0:c1], in_=p[:, 0:c1 - c0]),
                    waits=[("T", tv)], inc="V")
                evac_done[(ct, ci)] = s.val("V")
            if ct >= 1:
                emit_sdbc(ct - 1)
            if ct + 1 < NCT:
                s.add("gpsimd", dma(
                    wstk[:, ((ct + 1) % 2) * 64:((ct + 1) % 2 + 1) * 64],
                    wstack[(ct + 1) * 128:(ct + 2) * 128, :]),
                    waits=[("T", s.val("T"))], inc="W", inc_n=16)
                wstk_w[ct + 1] = s.val("W")
            for q in range(4):
                qc = q * 512
                cB = (convB, convB2, convA2)[q % 3]
                w0 = [("V", evac_done[(ct, 4)])]
                wprev = (silu_done.get((ct, 0)) if q == 3 else
                         silu_done.get((ct - 1, {0: 3, 1: 1, 2: 2}[q])))
                if wprev:
                    w0.append(("S", wprev))
                s.add("vector", lambda e, ct=ct, qc=qc: e.tensor_scalar(
                    out=convA[:, 0:512], in0=xs[:, qc + 1:qc + 513],
                    scalar1=cwsb[:, ct * 4:ct * 4 + 1], scalar2=cbsb[:, ct:ct + 1],
                    op0=ALU.mult, op1=ALU.add), waits=w0, inc="V")
                s.add("vector", lambda e, ct=ct, qc=qc, cB=cB: e.scalar_tensor_tensor(
                    out=cB[:, 0:512], in0=xs[:, qc + 2:qc + 514],
                    scalar=cwsb[:, ct * 4 + 1:ct * 4 + 2], in1=convA[:, 0:512],
                    op0=ALU.mult, op1=ALU.add), inc="V")
                s.add("vector", lambda e, ct=ct, qc=qc, cB=cB: e.scalar_tensor_tensor(
                    out=convA[:, 0:512], in0=xs[:, qc + 3:qc + 515],
                    scalar=cwsb[:, ct * 4 + 2:ct * 4 + 3], in1=cB[:, 0:512],
                    op0=ALU.mult, op1=ALU.add), inc="V")
                s.add("vector", lambda e, ct=ct, qc=qc, cB=cB: e.scalar_tensor_tensor(
                    out=cB[:, 0:512], in0=xs[:, qc + 4:qc + 516],
                    scalar=cwsb[:, ct * 4 + 3:ct * 4 + 4], in1=convA[:, 0:512],
                    op0=ALU.mult, op1=ALU.add), inc="V")
                vv = s.val("V")
                s.add("scalar", lambda e, ct=ct, qc=qc, cB=cB: e.activation(
                    out=xconvT[:, ct * T + qc:ct * T + qc + 512],
                    in_=cB[:, 0:512], func=AF.Silu),
                    waits=[("V", vv)], inc="S")
                silu_done[(ct, q)] = s.val("S")
                cast_done[(ct, q)] = s.val("S")
        emit_sdbc(NCT - 1)
        if debug:
            s.add("gpsimd", dma(dbg["dbg_xconv"][:, :], xconvT[:, 0:T]),
                  waits=[("V", s.val("V"))], inc="W", inc_n=16)

    blockrun(p1a)

    def p1b(s):
        tv0 = s.val("T")
        for q, psd in enumerate((pC, pD, pE, pF)):
            s.add("vector", lambda e, q=q, psd=psd: e.tensor_copy(
                out=sdbc[0:64, q * 512:(q + 1) * 512], in_=psd[0:64, 0:512]),
                waits=[("T", tv0)], inc="V")
        marks["p1b_done"] = s.val("V")
        vv = s.val("V")
        for c in range(NCH):
            s.add("tensor", lambda e, c=c: e.transpose(
                out=pG[:, 0:64], in_=sdbc[0:64, c * 128:(c + 1) * 128],
                identity=idf[0:64, 0:64]), waits=[("V", s.val("V"))], inc="T")
            tv = s.val("T")
            s.add("vector", lambda e, c=c: e.tensor_copy(
                out=dpre[:, c * DS:(c + 1) * DS], in_=pG[:, 0:DS]),
                waits=[("T", tv)], inc="V")
            s.add("vector", lambda e, c=c: e.tensor_copy(
                out=sBs[:, c * DS:(c + 1) * DS], in_=pG[:, DS:2 * DS]), inc="V")
            s.add("vector", lambda e, c=c: e.tensor_copy(
                out=sCs[:, c * DS:(c + 1) * DS], in_=pG[:, 2 * DS:3 * DS]), inc="V")
            s.add("vector", lambda e, c=c: e.tensor_copy(
                out=xavg[:, c:c + 1], in_=pG[:, 3 * DS:3 * DS + 1]), inc="V")


    def p1c(s):
        V, S_, Tn = "V", "S", "T"
        for c in range(NCH):
            s.add("vector", lambda e, c=c: e.tensor_add(
                out=sT1[:, c * DS:(c + 1) * DS], in0=dpre[:, c * DS:(c + 1) * DS],
                in1=dbbc[:, :]), inc="V")
        vv = s.val("V")
        s.add("scalar", lambda e: e.activation(out=sT2[:, :], in_=sT1[:, :],
                                               func=AF.Abs), waits=[("V", vv)], inc="S")
        s.add("scalar", lambda e: e.activation(out=dpre[:, :], in_=sT2[:, :],
                                               func=AF.Exp, scale=-1.0), inc="S")
        s.add("scalar", lambda e: e.activation(out=sT2[:, :], in_=dpre[:, :],
                                               func=AF.Ln, bias=1.0), inc="S")
        sv = s.val("S")
        s.add("vector", lambda e: e.tensor_scalar(
            out=dpre[:, :], in0=sT1[:, :], scalar1=0.0, scalar2=None,
            op0=ALU.max), waits=[("S", sv)], inc="V")
        s.add("vector", lambda e: e.tensor_add(
            out=sT1[:, :], in0=dpre[:, :], in1=sT2[:, :]), inc="V")
        for c in range(NCH):
            s.add("vector", lambda e, c=c: e.tensor_mul(
                out=sG[:, c * DS:(c + 1) * DS], in0=sT1[:, c * DS:(c + 1) * DS],
                in1=abc[:, :]), inc="V")
        for c in range(NCH):
            s.add("vector", lambda e, c=c: e.tensor_mul(
                out=sT2[:, c * DS:(c + 1) * DS], in0=sT1[:, c * DS:(c + 1) * DS],
                in1=sBs[:, c * DS:(c + 1) * DS]), inc="V")
        for c in range(NCH):
            s.add("vector", lambda e, c=c: e.tensor_scalar(
                out=sBs[:, c * DS:(c + 1) * DS], in0=sT2[:, c * DS:(c + 1) * DS],
                scalar1=xavg[:, c:c + 1], scalar2=None, op0=ALU.mult), inc="V")
        dbg_l = None
        if debug:
            s.add("sync", dma(dbg["dbg_delta"][:, :], sT1[:, :]),
                  waits=[("V", s.val("V"))], inc="L", inc_n=16)
            s.add("sync", dma(dbg["dbg_g"][:, :], sG[:, :]),
                  waits=[("V", s.val("V"))], inc="L", inc_n=16)
            dbg_l = s.val("L")
        vv = s.val("V")
        s.add("tensor", lambda e: e.matmul(out=pG[0:1, 0:256], lhsT=trit[:, 127:128],
                                           rhs=sG[:, :], start=True, stop=True),
              waits=[("V", vv)], inc="T")
        s.add("vector", cp(glr[:, :], pG[0:1, 0:256]), waits=[("T", s.val("T"))], inc="V")
        for c in range(NCH):
            p = pAB[c % 2]
            wv = s.val("V")
            s.add("tensor", lambda e, c=c, p=p: e.matmul(
                out=p[:, 0:DS], lhsT=trit[:, :], rhs=sG[:, c * DS:(c + 1) * DS],
                start=True, stop=True), waits=[("V", wv)], inc="T")
            wcp = [("T", s.val("T"))]
            if c == 0 and dbg_l:
                wcp.append(("L", dbg_l))
            s.add("vector", lambda e, c=c, p=p: e.tensor_copy(
                out=sG[:, c * DS:(c + 1) * DS], in_=p[:, 0:DS]),
                waits=wcp, inc="V")
        vv = s.val("V")
        s.add("scalar", lambda e: e.activation(out=sT2[:, :], in_=sG[:, :],
                                               func=AF.Exp), waits=[("V", vv)], inc="S")
        sv = s.val("S")
        s.add("vector", lambda e: e.tensor_mul(out=sA_[:, :], in0=sCs[:, :],
                                               in1=sT2[:, :]), waits=[("S", sv)], inc="V")
        s.add("scalar", lambda e: e.activation(out=sT1[:, :], in_=sG[:, :],
                                               func=AF.Exp, scale=-1.0), inc="S")
        sv = s.val("S")
        s.add("vector", lambda e: e.tensor_mul(out=sG[:, :], in0=sT1[:, :],
                                               in1=sBs[:, :]), waits=[("S", sv)], inc="V")
        vv = s.val("V")
        s.add("tensor", lambda e: e.matmul(out=pG[0:1, 0:256], lhsT=trit[:, 127:128],
                                           rhs=sG[:, :], start=True, stop=True),
              waits=[("V", vv)], inc="T")
        s.add("vector", cp(blr[:, :], pG[0:1, 0:256]), waits=[("T", s.val("T"))], inc="V")
        for c in range(NCH):
            p = pAB[c % 2]
            wv = s.val("V")
            s.add("tensor", lambda e, c=c, p=p: e.matmul(
                out=p[:, 0:DS], lhsT=trit[:, :], rhs=sG[:, c * DS:(c + 1) * DS],
                start=True, stop=True), waits=[("V", wv)], inc="T")
            s.add("vector", lambda e, c=c, p=p: e.tensor_copy(
                out=sT1[:, c * DS:(c + 1) * DS], in_=p[:, 0:DS]),
                waits=[("T", s.val("T"))], inc="V")
        s.add("vector", lambda e: e.tensor_mul(out=sT2[:, :], in0=sA_[:, :],
                                               in1=sT1[:, :]), inc="V")
        for c in range(NCH):
            s.add("vector", lambda e, c=c: e.reduce_sum(
                out=y_col[:, c:c + 1], in_=sT2[:, c * DS:(c + 1) * DS], axis=AX), inc="V")
        if debug:
            s.add("sync", dma(dbg["dbg_a"][:, :], sA_[:, :]),
                  waits=[("V", s.val("V"))], inc="L", inc_n=16)
            s.add("sync", dma(dbg["dbg_bcum"][:, :], sT1[:, :]), inc="L", inc_n=16)
            s.add("sync", dma(dbg["dbg_yloc"][:, :], y_col[:, :]), inc="L", inc_n=16)
            dbg_p1c_l = s.val("L")
            s.add("vector", lambda e: e.engine_nop(),
                  waits=[("L", dbg_p1c_l)], inc="V")
        s.add("vector", lambda e: e.memset(gcum[:, 0:DS], 0.0), inc="V")
        s.add("vector", lambda e: e.memset(hrow[:, 0:DS], 0.0), inc="V")
        for c in range(NCH):
            s.add("vector", lambda e, c=c: e.tensor_add(
                out=gcum[:, (c + 1) * DS:(c + 2) * DS],
                in0=gcum[:, c * DS:(c + 1) * DS],
                in1=glr[:, c * DS:(c + 1) * DS]), inc="V")
            vv = s.val("V")
            s.add("scalar", lambda e, c=c: e.activation(
                out=erow[:, :], in_=glr[:, c * DS:(c + 1) * DS], func=AF.Exp),
                waits=[("V", vv)], inc="S")
            s.add("scalar", lambda e: e.activation(
                out=xld[:, 256:384], in_=xld[:, 128:256], func=AF.Identity),
                inc="S")
            sv = s.val("S")
            s.add("vector", lambda e, c=c: e.tensor_mul(
                out=wrow[:, :], in0=erow[:, :], in1=blr[:, c * DS:(c + 1) * DS]),
                waits=[("S", sv)], inc="V")
            s.add("vector", lambda e, c=c: e.tensor_mul(
                out=arow_r[:, :], in0=erow[:, :], in1=hrow[:, c * DS:(c + 1) * DS]), inc="V")
            s.add("vector", lambda e, c=c: e.tensor_add(
                out=hrow[:, (c + 1) * DS:(c + 2) * DS], in0=arow_r[:, :],
                in1=wrow[:, :]), inc="V")
        vv = s.val("V")
        s.add("tensor", lambda e: e.matmul(out=pG[0:8, 0:DS], lhsT=ohsb[:, :],
                                           rhs=gcum[:, NCH * DS:(NCH + 1) * DS],
                                           start=True, stop=True),
              waits=[("V", vv)], inc="T")
        s.add("tensor", lambda e: e.matmul(out=pG[0:8, DS:2 * DS], lhsT=ohsb[:, :],
                                           rhs=hrow[:, NCH * DS:(NCH + 1) * DS],
                                           start=True, stop=True), inc="T")
        s.add("vector", cp(gput[:, :], pG[0:8, 0:32]), waits=[("T", s.val("T"))], inc="V")
        vv = s.val("V")
        s.add("gpsimd", dma(cc_buf[:, :], gput[:, :]), waits=[("V", vv)], inc="W", inc_n=16)
        cw = s.val("W")
        s.add("gpsimd", lambda e: e.collective_compute(
            "AllReduce", ALU.add, replica_groups=[list(range(NCORE))],
            ins=[cc_buf.ap()], outs=[cc_buf.ap()]),
            waits=[("W", cw)], inc="C", inc_n=1)


    def _prefetch_gate(s, g, wait_t):
        ct = NCT + g
        c0g = ct * 128
        s.add("gpsimd", dma(
            winst.ap()[:, (ct % 2) * 1024:(ct % 2 + 1) * 1024].rearrange("p (k n) -> p k n", k=KD),
            winT.ap()[:, c0g:c0g + 128].rearrange("(k p) n -> p k n", p=128)),
            waits=[("T", wait_t)] if wait_t else (), inc="W", inc_n=16)
        marks.setdefault("win_done", {})[g] = s.val("W")

    def prefetch_gate_early(s):
        _prefetch_gate(s, 0, 0)
        _prefetch_gate(s, 1, 0)

    def p1d(s, g_lo=0, g_hi=NCT):
        win_done = marks["win_done"]
        silu_done = marks.setdefault("gate_silu_chain", {})

        def prefetch_gate(g, wait_t):
            _prefetch_gate(s, g, wait_t)

        for g in range(g_lo, g_hi):
            for ci in range(4):
                p = (pC, pD, pE, pF)[ci]
                w = [("W", win_done[g])]
                prev = silu_done.get((g - 1, ci))
                if prev:
                    w.append(("S", prev))
                elif g == 0:
                    w.append(("V", marks["p1b_done"]))
                ct = NCT + g
                c0, c1 = HALO + ci * 512, HALO + (ci + 1) * 512
                for k in range(KD):
                    s.add("tensor", lambda e, k=k, ct=ct, c0=c0, c1=c1, p=p: e.matmul(
                        out=p[:, 0:512],
                        lhsT=winst[:, (ct % 2) * 1024 + k * 128:(ct % 2) * 1024 + (k + 1) * 128],
                        rhs=xnT[:, k * TH + c0:k * TH + c1],
                        start=(k == 0), stop=(k == KD - 1)),
                        waits=w if k == 0 else (), inc="T")
                tv = s.val("T")
                s.add("scalar", lambda e, g=g, ci=ci, p=p: e.activation(
                    out=sgT[:, g * T + ci * 512:g * T + (ci + 1) * 512],
                    in_=p[:, 0:512], func=AF.Silu), waits=[("T", tv)], inc="S")
                silu_done[(g, ci)] = s.val("S")
            marks.setdefault("gate_S", {})[g] = s.val("S")
            if g + 2 < NCT:
                prefetch_gate(g + 2, s.val("T"))
        if debug and g_hi == NCT:
            s.add("gpsimd", dma(dbg["dbg_sg"][:, :], sgT[:, 0:T]),
                  waits=[("S", s.val("S"))], inc="W", inc_n=16)


    def p1e(s):
        def vflush(tag=""):
            s.add("vector", lambda e: e.tensor_copy(out=xld[:, 128:256],
                                                    in_=xld[:, 0:128]), inc="V")
            return s.val("V")

        def sflush():
            s.add("scalar", lambda e: e.activation(out=xld[:, 256:384],
                                                   in_=xld[:, 128:256],
                                                   func=AF.Identity), inc="S")
            return s.val("S")

        cw = s.val("C")
        s.add("gpsimd", dma(su8[:, :], cc_buf[:, :]), waits=[("C", cw)],
              inc="W", inc_n=16)
        cw2 = s.val("W")
        if debug:
            s.add("sync", dma(dbg["dbg_su"][:, :], su8[:, :]), waits=[("W", cw2)],
                  inc="L", inc_n=16)
        s.add("tensor", lambda e: e.matmul(out=pA[0:8, 0:DS], lhsT=wmsb[:, :],
                                           rhs=su8[0:8, 0:DS], start=True, stop=True),
              waits=[("W", cw2)], inc="T")
        tv = s.val("T")
        s.add("scalar", lambda e: e.activation(out=e8[:, :], in_=pA[0:8, 0:DS],
                                               func=AF.Exp), waits=[("T", tv)], inc="S")
        sv = sflush()
        s.add("vector", lambda e: e.tensor_mul(out=ctrb[:, :], in0=e8[:, :],
                                               in1=su8[0:8, DS:2 * DS]),
              waits=[("S", sv)], inc="V")
        vv = vflush()
        s.add("tensor", lambda e: e.matmul(out=pB[0:1, 0:DS], lhsT=mvsb[:, :],
                                           rhs=ctrb[:, :], start=True, stop=True),
              waits=[("V", vv)], inc="T")
        s.add("vector", cp(hin[:, :], pB[0:1, 0:DS]), waits=[("T", s.val("T"))], inc="V")
        e_row = sdbc[0:1, 1024:1280]
        w_all = sdbc[0:1, 1280:1536]
        vv = s.val("V")
        s.add("scalar", lambda e: e.activation(out=e_row[:, :],
                                               in_=gcum[:, 0:NCH * DS],
                                               func=AF.Exp), waits=[("V", vv)], inc="S")
        sv = s.val("S")
        hin3 = hin.rearrange("p (o d) -> p o d", o=1).broadcast_to([1, NCH, DS])
        s.add("vector", lambda e: e.tensor_tensor(
            out=w_all.rearrange("p (t d) -> p t d", d=DS),
            in0=e_row.rearrange("p (t d) -> p t d", d=DS),
            in1=hin3, op=ALU.mult), waits=[("S", sv)], inc="V")
        vflush()
        s.add("vector", lambda e: e.tensor_add(out=w_all[:, :], in0=w_all[:, :],
                                               in1=hrow[:, 0:NCH * DS]), inc="V") \
            if False else None
        s.add("vector", lambda e: e.tensor_tensor(
            out=sdbc[0:1, 1536:1792], in0=w_all[:, :], in1=hrow[:, 0:NCH * DS],
            op=ALU.add), inc="V")
        vv = vflush()
        if debug:
            s.add("sync", dma(dbg["dbg_w2"][:, :], sdbc[0:1, 1536:1792]),
                  waits=[("V", vv)], inc="L", inc_n=16)
            s.add("sync", dma(dbg["dbg_er"][:, :], e_row[:, :]), inc="L", inc_n=16)
            s.add("sync", dma(dbg["dbg_hr"][:, :], hrow[:, 0:272]), inc="L", inc_n=16)
        s.add("tensor", lambda e: e.matmul(out=pA[:, 0:NCH * DS], lhsT=onesrow[:, :],
                                           rhs=sdbc[0:1, 1536:1792], start=True,
                                           stop=True), waits=[("V", vv)], inc="T")
        marks["sdbc_T"] = s.val("T")
        marks["sdbc_V"] = s.val("V")
        tv = s.val("T")
        s.add("vector", lambda e: e.tensor_mul(out=sT2[:, :], in0=sA_[:, :],
                                               in1=pA[:, 0:NCH * DS]),
              waits=[("T", tv)], inc="V")
        s.add("vector", lambda e: e.tensor_reduce(
            out=xld[:, 384:384 + NCH],
            in_=sT2.ap().rearrange("p (t d) -> p t d", d=DS), axis=AX,
            op=ALU.add), inc="V")
        vflush()
        s.add("vector", lambda e: e.tensor_add(out=yfin[:, :], in0=y_col[:, :],
                                               in1=xld[:, 384:384 + NCH]), inc="V")
        vv = vflush()
        if debug:
            s.add("sync", dma(dbg["dbg_ycol"][:, :], yfin[:, :]),
                  waits=[("V", vv)], inc="L", inc_n=16)
        for c in range(NCH):
            s.add("tensor", lambda e, c=c: e.transpose(
                out=pG[0:1, 0:128], in_=yfin[:, c:c + 1], identity=idf[:, :]),
                waits=[("V", s.val("V"))], inc="T")
            tv = s.val("T")
            s.add("vector", lambda e, c=c: e.tensor_copy(
                out=yrow_stage[:, :], in_=pG[0:1, 0:128]),
                waits=[("T", tv)], inc="V")
            vv = vflush()
            p = pAB[c % 2]
            s.add("tensor", lambda e, c=c, p=p: e.matmul(
                out=p[:, 0:128], lhsT=onesrow[:, :],
                rhs=yrow_stage[:, :], start=True, stop=True),
                waits=[("V", vv)], inc="T")
            tv = s.val("T")
            s.add("scalar", lambda e, c=c, p=p: e.activation(
                out=y_bc[:, c * 128:(c + 1) * 128], in_=p[:, 0:128],
                func=AF.Copy), waits=[("T", tv)], inc="S")
        if debug:
            s.add("gpsimd", dma(dbg["dbg_ysm"][:, :], y_bc[:, 0:NCH]),
                  waits=[("S", s.val("S"))], inc="W", inc_n=16)
        marks["ybc_S"] = s.val("S")
        marks["ybc_V"] = s.val("V")
        marks["ybc_T"] = s.val("T")


    def p2(s):
        for ct in range(NCT):
            if ct == 0:
                w = [("T", marks["sdbc_T"]), ("V", marks["sdbc_V"])]
            elif ct == 12:
                w = [("T", s.val("T"))]
            else:
                w = ()
            s.add("gpsimd", dma(
                woutst[:, ct * 512:(ct + 1) * 512],
                woutT.ap()[ct * 128:(ct + 1) * 128, 0:512]),
                waits=w, inc="W", inc_n=16)
        gate_S = marks["gate_S"]
        for ct in range(NCT):
            for h in range(2):
                c0, c1 = h * 1024, (h + 1) * 1024
                w = []
                if h == 0:
                    if ct == 0:
                        w = [("S", marks["ybc_S"]), ("T", marks["ybc_T"])]
                        if debug:
                            w += [("W", s.val("W")), ("L", s.val("L"))]
                    elif gate_S[ct] > marks["ybc_S"]:
                        w = [("S", gate_S[ct])]
                s.add("vector", lambda e, ct=ct, c0=c0, c1=c1: e.tensor_mul(
                    out=tmp2[:, :], in0=sgT[:, ct * T + c0:ct * T + c1],
                    in1=y_bc[:, c0:c1]), waits=w, inc="V")
                s.add("vector", lambda e, ct=ct, c0=c0, c1=c1: e.scalar_tensor_tensor(
                    out=yskT[:, ct * T + c0:ct * T + c1],
                    in0=xconvT[:, ct * T + c0:ct * T + c1],
                    scalar=dsb[:, ct:ct + 1], in1=tmp2[:, :],
                    op0=ALU.mult, op1=ALU.add), inc="V")
        marks["p2_V"] = s.val("V")
        marks["p2_W"] = s.val("W")

    marks = {}

    def p_mid(s):
        prefetch_gate_early(s)
        p1b(s)
        p1c(s)
        p1d(s, 0, 8)
        p1e(s)
        p1d(s, 8, NCT)
        p2(s)
    p_mid.__name__ = "p_mid"
    blockrun(p_mid)

    def p3(s):
        wout_ready = s.val("W")
        evac = {}
        for dc in range(2):
            if dc == 1:
                tv = s.val("T")
                for ct in range(NCT):
                    s.add("gpsimd", dma(
                        woutst[:, ct * 512:(ct + 1) * 512],
                        woutT.ap()[ct * 128:(ct + 1) * 128, 512:1024]),
                        waits=[("T", tv)], inc="W", inc_n=16)
                wout_ready = s.val("W")
            for tt in range(16):
                p = pAB[tt % 2]
                w = []
                pv = evac.get((dc, tt - 2)) or (evac.get((dc - 1, 14 + tt)) if dc else None)
                if pv:
                    w.append(("S", pv))
                s.add("tensor", lambda e, tt=tt, dc=dc, p=p: e.matmul(
                    out=p[:, 0:512],
                    lhsT=ewT2[:, tt * 128:(tt + 1) * 128],
                    rhs=bl2sb[:, dc * 512:(dc + 1) * 512],
                    start=True, stop=False), waits=w, inc="T")
                w2 = [("W", wout_ready)]
                if dc == 0 and tt == 0:
                    w2 += [("V", marks["p2_V"]), ("W", marks["p2_W"])]
                for ct in range(NCT):
                    s.add("tensor", lambda e, ct=ct, tt=tt, p=p: e.matmul(
                        out=p[:, 0:512],
                        lhsT=yskT[:, ct * T + tt * 128:ct * T + (tt + 1) * 128],
                        rhs=woutst[:, ct * 512:(ct + 1) * 512],
                        start=False, stop=(ct == NCT - 1)),
                        waits=w2 if ct == 0 else (), inc="T")
                tv = s.val("T")
                s.add("scalar", lambda e, tt=tt, dc=dc, p=p: e.activation(
                    out=y_acc[:, tt * D + dc * 512:tt * D + (dc + 1) * 512],
                    in_=p[:, 0:512], func=AF.Copy), waits=[("T", tv)], inc="S")
                evac[(dc, tt)] = s.val("S")
        if debug:
            s.add("sync", dma(dbg["dbg_yac"][:, :], y_acc[:, 0:D]),
                  waits=[("S", s.val("S"))], inc="L", inc_n=16)

    blockrun(p3)

    def p4(s):
        from collections import deque
        wl1_done = {}
        wl2_done = {}
        g_done = {}
        acc = {}
        hist = deque([None, None], maxlen=2)
        ew_done = {}
        p5_ready = {}

        def prefetch_ab(n, hp, wait_t):
            par = hp % 2
            for half, col0 in ((0, hp * 128), (1, HID + hp * 128)):
                slot = (2 * par + half) * D
                s.add("gpsimd", dma(
                    wabst.ap()[:, slot:slot + D].rearrange("p (k n) -> p k n", k=KD),
                    wl1.ap()[n, :, col0:col0 + 128].rearrange(
                        "(k p) n -> p k n", p=128)),
                    waits=[("T", wait_t)], inc="W", inc_n=16)
            wl1_done[(n, hp)] = s.val("W")

        def prefetch_wl2(n, hch, wait_t):
            for i in range(8):
                hc = hch * 8 + i
                s.add("gpsimd", dma(
                    wl2st[:, i * D:(i + 1) * D],
                    wl2.ap()[n, hc * 128:(hc + 1) * 128, :]),
                    waits=[("T", wait_t)], inc="W", inc_n=16)
            wl2_done[(n, hch)] = s.val("W")

        def emit_ewbc(n):
            prev_readers = g_done.get((n - 1, NCT - 1, 3))
            for tc in range(4):
                s.add("tensor", lambda e, n=n, tc=tc: e.matmul(
                    out=pG[:, 0:512], lhsT=selsb[:, n * 128:(n + 1) * 128],
                    rhs=ewT2[:, tc * 512:(tc + 1) * 512], start=True, stop=True),
                    waits=[("S", s.val("S"))], inc="T")
                tv = s.val("T")
                wsc = [("T", tv)]
                if prev_readers:
                    wsc.append(("V", prev_readers))
                s.add("scalar", lambda e, tc=tc: e.activation(
                    out=ew_bcT[:, tc * 512:(tc + 1) * 512], in_=pG[:, 0:512],
                    func=AF.Copy), waits=wsc, inc="S")
            ew_done[n] = s.val("S")

        def emit_mm1_half(n, half):
            for hp in range(half * 8, half * 8 + 8):
                par = hp % 2
                asl = (2 * par) * D
                bsl = (2 * par + 1) * D
                for tc in range(4):
                    pa = pCD[tc % 2]
                    pb = pEF[tc % 2]
                    w = [("W", wl1_done[(n, hp)])]
                    if hist[0] is not None:
                        w.append(("S", hist[0][0]))
                        w.append(("V", hist[0][1]))
                    for k in range(KD):
                        s.add("tensor", lambda e, k=k, tc=tc, asl=asl, pa=pa: e.matmul(
                            out=pa[:, 0:512],
                            lhsT=wabst[:, asl + k * 128:asl + (k + 1) * 128],
                            rhs=xnT[:, k * TH + HALO + tc * 512:k * TH + HALO + (tc + 1) * 512],
                            start=(k == 0), stop=(k == KD - 1)),
                            waits=w if k == 0 else (), inc="T")
                    for k in range(KD):
                        s.add("tensor", lambda e, k=k, tc=tc, bsl=bsl, pb=pb: e.matmul(
                            out=pb[:, 0:512],
                            lhsT=wabst[:, bsl + k * 128:bsl + (k + 1) * 128],
                            rhs=xnT[:, k * TH + HALO + tc * 512:k * TH + HALO + (tc + 1) * 512],
                            start=(k == 0), stop=(k == KD - 1)), inc="T")
                    tv = s.val("T")
                    sa = saA if tc % 2 == 0 else saB
                    sb = saB if tc % 2 == 0 else saA
                    wsl = [("T", tv), ("S", ew_done[n])]
                    pv2 = g_done.get((n, hp, tc - 1)) or g_done.get((n, hp - 1, 3)) \
                        or g_done.get((n - 1, NCT - 1, 3))
                    if pv2:
                        wsl.append(("V", pv2))
                    s.add("scalar", lambda e, hp=hp, n=n, pa=pa, sa=sa: e.activation(
                        out=sa[:, :], in_=pa[:, 0:512], func=AF.Silu,
                        bias=bl1sb[:, n * 32 + hp:n * 32 + hp + 1]),
                        waits=wsl, inc="S")
                    silu_s_val = s.val("S")
                    sv = s.val("S")
                    s.add("vector", lambda e, tc=tc, sa=sa, sb=sb: e.tensor_mul(
                        out=sb[:, :], in0=sa[:, :],
                        in1=ew_bcT[:, tc * 512:(tc + 1) * 512]),
                        waits=[("S", sv)], inc="V")
                    s.add("vector", lambda e, n=n, hp=hp, tc=tc, pb=pb, sb=sb:
                          e.scalar_tensor_tensor(
                              out=gT[:, hp * T + tc * 512:hp * T + (tc + 1) * 512],
                              in0=pb[:, 0:512],
                              scalar=bl1sb[:, n * 32 + 16 + hp:n * 32 + 16 + hp + 1],
                              in1=sb[:, :], op0=ALU.add, op1=ALU.mult), inc="V")
                    g_done[(n, hp, tc)] = s.val("V")
                    hist.append((silu_s_val, s.val("V")))
                nxt = (n, hp + 2)
                if hp + 2 >= NCT:
                    nxt = (n + 1, hp + 2 - NCT)
                if nxt[0] < NE:
                    prefetch_ab(nxt[0], nxt[1], s.val("T"))

        res_done = {}

        def emit_mm2_half(n, hch):
            wl2v = wl2_done[(n, hch)]
            gv = g_done[(n, hch * 8 + 7, 3)]
            final = (n == NE - 1 and hch == 1)
            if final:
                for tt0 in (0, 1):
                    s.add("sync", dma(xres[:, (tt0 % 2) * D:(tt0 % 2) * D + D],
                                      x_own[tt0 * 128:(tt0 + 1) * 128, :]),
                          waits=[("T", marks["mm1_done"])], inc="L", inc_n=16)
                    res_done[("L", tt0)] = s.val("L")
            for tt in range(16):
                for dc in range(2):
                    p = pAB[(tt * 2 + dc) % 2]
                    w = [("W", wl2v), ("V", gv)]
                    pv = acc.get((n, hch, tt - 1, dc)) or \
                        (acc.get((n, hch - 1, 15, 1)) if hch else
                         acc.get((n - 1, 1, 15, 1)))
                    if pv:
                        w.append(("V", pv))
                    for i in range(8):
                        hc = hch * 8 + i
                        s.add("tensor", lambda e, i=i, hc=hc, tt=tt, dc=dc, p=p:
                              e.matmul(
                                  out=p[:, 0:512],
                                  lhsT=gT[:, hc * T + tt * 128:hc * T + (tt + 1) * 128],
                                  rhs=wl2st[:, i * D + dc * 512:i * D + (dc + 1) * 512],
                                  start=(i == 0), stop=(i == 7)),
                              waits=w if i == 0 else (), inc="T")
                    tv = s.val("T")
                    s.add("vector", lambda e, tt=tt, dc=dc, p=p: e.tensor_add(
                        out=y_acc[:, tt * D + dc * 512:tt * D + (dc + 1) * 512],
                        in0=y_acc[:, tt * D + dc * 512:tt * D + (dc + 1) * 512],
                        in1=p[:, 0:512]), waits=[("T", tv)], inc="V")
                    acc[(n, hch, tt, dc)] = s.val("V")
                if final:
                    sl = (tt % 2) * D
                    s.add("vector", lambda e, tt=tt, sl=sl: e.tensor_add(
                        out=y_acc[:, tt * D:(tt + 1) * D],
                        in0=y_acc[:, tt * D:(tt + 1) * D], in1=xres[:, sl:sl + D]),
                        waits=[("L", res_done[("L", tt)])], inc="V")
                    res_done[tt] = s.val("V")
                    s.add("sync", dma(out[tt * 128:(tt + 1) * 128, :],
                                      y_acc[:, tt * D:(tt + 1) * D]),
                          waits=[("V", res_done[tt])], inc="L", inc_n=16)
                    if tt + 2 < 16:
                        s.add("sync", dma(xres[:, sl:sl + D],
                                          x_own[(tt + 2) * 128:(tt + 3) * 128, :]),
                              waits=[("V", res_done[tt])], inc="L", inc_n=16)
                        res_done[("L", tt + 2)] = s.val("L")

        prefetch_ab(0, 0, 0)
        prefetch_ab(0, 1, 0)
        emit_ewbc(0)
        emit_mm1_half(0, 0)
        prefetch_wl2(0, 0, 0)
        emit_mm1_half(0, 1)
        emit_mm2_half(0, 0)
        for n in range(1, NE):
            emit_ewbc(n)
            prefetch_wl2(n - 1, 1, s.val("T"))
            emit_mm1_half(n, 0)
            emit_mm2_half(n - 1, 1)
            prefetch_wl2(n, 0, s.val("T"))
            emit_mm1_half(n, 1)
            emit_mm2_half(n, 0)
        marks["mm1_done"] = s.val("T")
        prefetch_wl2(NE - 1, 1, s.val("T"))
        emit_mm2_half(NE - 1, 1)
        lv = s.val("L")
        s.add("sync", lambda e: e.nop(), waits=[("L", lv)])

    blockrun(p4)


    return nc, sbuf_used


def _prep_inputs(inputs):
    f32 = np.float32
    x = np.ascontiguousarray(inputs["L0_attn_x"], dtype=f32).reshape(2 * 8192, D)
    winT_h = np.ascontiguousarray(inputs["L0_attn_in_proj_w"].T, dtype=f32)
    wstack_h = np.zeros((DI, 64), dtype=f32)
    wstack_h[:, 0:16] = inputs["L0_attn_delta_w"].T
    wstack_h[:, 16:32] = inputs["L0_attn_B_w"].T
    wstack_h[:, 32:48] = inputs["L0_attn_C_w"].T
    wstack_h[:, 48] = 1.0 / DI
    cw = np.ascontiguousarray(inputs["L0_attn_conv_weight"], dtype=f32)
    conv_w_h = np.zeros((128, 64), dtype=f32)
    for ct in range(NCT):
        conv_w_h[:, ct * 4:(ct + 1) * 4] = cw[ct * 128:(ct + 1) * 128, :]
    conv_b_h = inputs["L0_attn_conv_bias"].astype(f32).reshape(NCT, 128).T.copy()
    d_col_h = inputs["L0_attn_D"].astype(f32).reshape(NCT, 128).T.copy()
    a_log_h = inputs["L0_attn_A_log"].astype(f32).reshape(1, DS)
    delta_b_h = inputs["L0_attn_delta_b"].astype(f32).reshape(1, DS)
    woutT_h = np.ascontiguousarray(inputs["L0_attn_out_proj_w"].T, dtype=f32)
    wrT_h = np.ascontiguousarray(inputs["L0_ffn_Wr"].T, dtype=f32)
    br_h = inputs["L0_ffn_br"].astype(f32).reshape(1, NE)
    bn_h = inputs["L0_ffn_biases_N"].astype(f32).reshape(1, NE)
    wl1_h = np.ascontiguousarray(inputs["L0_ffn_Wl1"], dtype=f32)
    bl1 = inputs["L0_ffn_bl1"].astype(f32)
    bl1c_h = np.zeros((128, 256), dtype=f32)
    for n in range(NE):
        bl1c_h[:, n * 32:(n + 1) * 32] = bl1[n].reshape(32, 128).T
    wl2_h = np.ascontiguousarray(inputs["L0_ffn_Wl2"], dtype=f32)
    bl2_h = np.ascontiguousarray(inputs["L0_ffn_bl2"], dtype=f32)

    maps = []
    for c in range(NCORE):
        t0 = c * T
        chain_pos = c % 4
        halo = (np.zeros((HALO, D), dtype=f32) if chain_pos == 0
                else x[t0 - HALO:t0].copy())
        oh = np.zeros((1, NCORE), dtype=f32)
        oh[0, c] = 1.0
        wmT = np.zeros((NCORE, NCORE), dtype=f32)
        mv = np.zeros((NCORE, 1), dtype=f32)
        chain = range((c // 4) * 4, (c // 4) * 4 + 4)
        for cp in chain:
            if cp < c:
                mv[cp, 0] = 1.0
                for j in chain:
                    if cp < j < c:
                        wmT[j, cp] = 1.0
        maps.append({
            "x_own": x[t0:t0 + T].copy(), "x_halo": halo,
            "winT": winT_h, "wstack": wstack_h, "conv_w": conv_w_h,
            "conv_b": conv_b_h, "d_col": d_col_h, "a_log": a_log_h,
            "delta_b": delta_b_h, "woutT": woutT_h, "wrT": wrT_h,
            "br_r": br_h, "bn_r": bn_h, "bn16": np.tile(bn_h, (1, 16)), "wl1": wl1_h, "bl1c": bl1c_h,
            "wl2": wl2_h, "bl2": bl2_h, "oh": oh, "wmaskT": wmT, "mvec": mv,
        })
    return maps


_NC_CACHE = {}


def kernel(**inputs):
    key = "main"
    if key not in _NC_CACHE:
        _NC_CACHE[key] = build(debug=False)
    nc, _ = _NC_CACHE[key]
    maps = _prep_inputs(inputs)
    res = run_bass_kernel_spmd(nc, maps, core_ids=list(range(NCORE)))
    outs = [res.results[c]["out"] for c in range(NCORE)]
    return np.concatenate(outs, axis=0).reshape(2, 8192, D).astype(np.float32)


def check_deadlock():
    sem = {k: 0 for k in ("L", "W", "T", "V", "S", "C")}
    for bname, ops in SCHEDS:
        ptr = {e: 0 for e in ENGS}
        stuck = False
        while True:
            progress = False
            for e in ENGS:
                while ptr[e] < len(ops[e]):
                    waits, inc, n = ops[e][ptr[e]]
                    if all(sem[s] >= v for (s, v) in waits):
                        if inc:
                            sem[inc] += n
                        ptr[e] += 1
                        progress = True
                    else:
                        break
            if all(ptr[e] == len(ops[e]) for e in ENGS):
                break
            if not progress:
                stuck = True
                print(f"DEADLOCK in block {bname}:")
                for e in ENGS:
                    if ptr[e] < len(ops[e]):
                        waits, inc, n = ops[e][ptr[e]]
                        unmet = [(s, v, sem[s]) for (s, v) in waits if sem[s] < v]
                        print(f"  {e} op#{ptr[e]}/{len(ops[e])} blocked on {unmet}")
                break
        if stuck:
            return False
    print("schedule: no deadlock")
    return True


if __name__ == "__main__":
    nc, used = build(debug=True)
    print(f"sbuf used: {used} bytes/partition")
    print(f"instructions: {nc.n_instructions()}")
    check_deadlock()


# version 12
# speedup vs baseline: 1.2172x; 1.0159x over previous
import numpy as np
import ml_dtypes

import concourse.bass as bass
import concourse.mybir as mybir
from concourse.bass_utils import run_bass_kernel_spmd

F32 = mybir.dt.float32
BF16 = mybir.dt.bfloat16
AF = mybir.ActivationFunctionType
ALU = mybir.AluOpType
AX = mybir.AxisListType.X
BF = ml_dtypes.bfloat16

T = 2048
HALO = 4
TH = T + HALO
D = 1024
KD = 8
DI = 2048
NCT = 16
DS = 16
Q = 128
NCH = 16
NE = 8
HID = 2048
NCORE = 8

ENGS = ("sync", "gpsimd", "tensor", "vector", "scalar")

SCHEDS = []


class Sched:

    def __init__(self, nc, sems):
        self.nc = nc
        self.ops = {e: [] for e in ENGS}
        self.sems = sems

    def add(self, eng, fn, waits=(), inc=None, inc_n=1):
        w = [(s, v) for (s, v) in waits]
        self.ops[eng].append((fn, w, inc, inc_n))
        if inc is not None:
            self.sems[inc][1] += inc_n
        return self.sems[inc][1] if inc is not None else None

    def val(self, sem):
        return self.sems[sem][1]

    def emit(self, block):
        for eng_name in ENGS:
            ops = self.ops[eng_name]
            if not ops:
                continue

            def body(eng, ops=ops, eng_name=eng_name):
                maxw = {}
                for fn, waits, inc, inc_n in ops:
                    for s, v in waits:
                        if v <= 0:
                            continue
                        if maxw.get(s, -1) >= v:
                            continue
                        eng.wait_ge(self.sems[s][0], v)
                        maxw[s] = v
                    instr = fn(eng)
                    if inc is not None:
                        instr.then_inc(self.sems[inc][0], inc_n)
            getattr(block, eng_name)(body)


def build(debug=False):
    nc = bass.Bass("TRN2", target_bir_lowering=False, num_devices=NCORE,
                   detect_race_conditions=False)

    def ein(name, shape):
        return nc.dram_tensor(name, list(shape), F32, kind="ExternalInput")

    x_own = ein("x_own", (T, D))
    x_halo = ein("x_halo", (HALO, D))
    winT = ein("winT", (D, 2 * DI))
    wstack = ein("wstack", (DI, 64))
    conv_w = ein("conv_w", (128, 64))
    conv_b = ein("conv_b", (128, NCT))
    d_col = ein("d_col", (128, NCT))
    a_log = ein("a_log", (1, DS))
    delta_b = ein("delta_b", (1, DS))
    woutT = ein("woutT", (DI, D))
    wrT = ein("wrT", (D, NE))
    br_r = ein("br_r", (1, NE))
    bn_r = ein("bn_r", (1, NE))
    bn16 = ein("bn16", (1, 128))
    wl1 = ein("wl1", (NE, D, 2 * HID))
    bl1c = ein("bl1c", (128, 256))
    wl2 = ein("wl2", (NE, HID, D))
    bl2 = ein("bl2", (NE, D))
    oh_in = ein("oh", (1, NCORE))
    wmaskT = ein("wmaskT", (NCORE, NCORE))
    mvec = ein("mvec", (NCORE, 1))

    out = nc.dram_tensor("out", [T, D], F32, kind="ExternalOutput")
    cc_buf = nc.dram_tensor("cc_buf", [NCORE, 32], F32)

    dbg = {}
    if debug:
        for name, shape in (
            ("dbg_xn", (128, D)), ("dbg_xconv", (128, T)), ("dbg_sg", (128, T)),
            ("dbg_delta", (128, 256)), ("dbg_ysm", (128, NCH)), ("dbg_ew", (128, 128)),
            ("dbg_yac", (128, D)), ("dbg_su", (8, 32)), ("dbg_g", (128, 256)),
            ("dbg_ycol", (128, NCH)), ("dbg_xnT", (128, 128)),
            ("dbg_sm", (128, 48)), ("dbg_a", (128, 256)), ("dbg_bcum", (128, 256)),
            ("dbg_yloc", (128, NCH)), ("dbg_sm2", (128, 48)), ("dbg_cr", (128, 48)),
            ("dbg_w2", (1, 256)), ("dbg_er", (1, 256)), ("dbg_hr", (1, 272)),
        ):
            dbg[name] = nc.dram_tensor(name, list(shape), F32, kind="ExternalOutput")

    ident_bf = nc.inline_tensor(np.eye(128, dtype=BF), "c_identbf")
    ident_f = nc.inline_tensor(np.eye(128, dtype=np.float32), "c_identf")
    tri = np.tril(np.ones((128, 128), dtype=np.float32))
    triT_c = nc.inline_tensor(np.ascontiguousarray(tri.T), "c_triT")
    ones_row_c = nc.inline_tensor(np.ones((1, 128), dtype=np.float32), "c_onesrow")
    sel_np = np.zeros((NE, NE * 128), dtype=BF)
    for n in range(NE):
        sel_np[n, n * 128:(n + 1) * 128] = 1
    sel_c = nc.inline_tensor(sel_np, "c_sel")

    BASE = 16512
    cur = [BASE]

    def at(name, shape, dtype, offset=None):
        if offset is None:
            offset = cur[0]
            sz = int(np.prod(shape[1:])) * mybir.dt.size(dtype)
            cur[0] = (offset + sz + 31) & ~31
        return nc.alloc_sbuf_tensor_at(name, list(shape), dtype, offset=offset)

    xnT = at("xnT", (128, KD * TH), BF16)
    B1 = cur[0]
    xconvT = at("xconvT", (128, NCT * T), BF16)
    y_acc = at("y_acc", (128, 16 * D), F32, offset=B1)
    B2 = cur[0]
    sgT = at("sgT", (128, NCT * T), BF16)
    yskT = at("yskT", (128, NCT * T), BF16, offset=B2)
    gT = at("gT", (128, NCT * T), BF16, offset=B2)
    S0 = cur[0]
    xs = at("xs", (128, TH), F32, offset=S0)
    convA = at("convA", (128, 512), F32, offset=S0 + 8224)
    convB = at("convB", (128, 512), F32, offset=S0 + 8224 + 2048)
    winst = at("winst", (128, 2048), BF16, offset=S0 + 8224 + 4096)
    xld = at("xld", (128, 2 * D), F32, offset=S0 + 16416)
    sdbc = at("sdbc", (64, T), F32, offset=S0)
    woutst = at("woutst", (128, NCT * 512), BF16, offset=S0)
    wabst = at("wabst", (128, 4 * D), BF16, offset=S0)
    wl2st = at("wl2st", (128, 8 * D), BF16, offset=S0 + 8192)
    cur[0] = S0 + 24640
    R0 = cur[0]
    dpre = at("dpre", (128, 256), F32, offset=R0)
    sBs = at("sBs", (128, 256), F32, offset=R0 + 1024)
    sCs = at("sCs", (128, 256), F32, offset=R0 + 2048)
    sA_ = at("sA_", (128, 256), F32, offset=R0 + 3072)
    sG = at("sG", (128, 256), F32, offset=R0 + 4096)
    sT1 = at("sT1", (128, 256), F32, offset=R0 + 5120)
    sT2 = at("sT2", (128, 256), F32, offset=R0 + 6144)
    tmp2 = at("tmp2", (128, 1024), F32, offset=R0)
    saA = at("saA", (128, 512), F32, offset=R0)
    saB = at("saB", (128, 512), F32, offset=R0 + 2048)
    convA2 = at("convA2", (128, 512), F32, offset=R0 + 4096)
    convB2 = at("convB2", (128, 512), F32, offset=R0 + 6144)
    cur[0] = R0 + 7168
    YB = cur[0]
    y_bc = at("y_bc", (128, T), BF16)
    ew_bcT = at("ew_bcT", (128, T), BF16, offset=YB)
    ew = at("ew", (128, 128), F32, offset=R0 + 4096)
    ewT2 = at("ewT2", (NE, T), BF16)
    xavg = at("xavg", (128, NCH), F32)
    y_col = at("y_col", (128, NCH), F32)
    yfin = xavg
    glr = dpre[0:1, 0:256]
    blr = sBs[0:1, 0:256]
    gcum = sdbc[0:1, 0:272]
    hrow = sdbc[0:1, 272:544]
    yrow_stage = dpre[0:1, 0:128]
    erow = sCs[0:1, 0:DS]
    wrow = sCs[0:1, DS:2 * DS]
    arow_r = sCs[0:1, 2 * DS:3 * DS]
    hin = sCs[0:1, 3 * DS:4 * DS]
    arow = sCs[0:1, 0:DS]
    abc = at("abc", (128, DS), F32)
    dbbc = at("dbbc", (128, DS), F32)
    brbc = at("brbc", (128, NE), F32)
    bnbc = at("bnbc", (128, NE), F32)
    su8 = at("su8", (NCORE, 32), F32)
    gput = at("gput", (NCORE, 32), F32)
    e8 = at("e8", (NCORE, DS), F32)
    ctrb = at("ctrb", (NCORE, DS), F32)
    xres = at("xres", (128, 2 * D), F32, offset=S0)
    ystg = at("ystg", (8, 256), BF16, offset=S0 + 16416 + 1536)
    wstk = at("wstk", (128, 2 * 64), BF16)
    cwsb = at("cwsb", (128, 64), F32)
    cbsb = at("cbsb", (128, NCT), F32)
    dsb = at("dsb", (128, NCT), F32)
    alsb = at("alsb", (1, DS), F32)
    dbsb = at("dbsb", (1, DS), F32)
    brsb = at("brsb", (1, NE), F32)
    bnsb = at("bnsb", (1, NE), F32)
    bl1sb = at("bl1sb", (128, 256), F32)
    bl2sb = at("bl2sb", (NE, D), BF16)
    wrsb = at("wrsb", (128, 64), BF16)
    ohsb = at("ohsb", (1, NCORE), F32)
    wmsb = at("wmsb", (NCORE, NCORE), F32)
    mvsb = at("mvsb", (NCORE, 1), F32)
    idbf = at("idbf", (128, 128), BF16)
    idf = at("idf", (128, 128), F32)
    trit = at("trit", (128, 128), F32)
    onesrow = at("onesrow", (1, 128), F32)
    selsb = at("selsb", (NE, NE * 128), BF16)

    sbuf_used = cur[0]

    pA = nc.alloc_psum_tensor("pA", [128, 512], F32)
    pB = nc.alloc_psum_tensor("pB", [128, 512], F32)
    pC = nc.alloc_psum_tensor("pC", [128, 512], F32)
    pD = nc.alloc_psum_tensor("pD", [128, 512], F32)
    pE = nc.alloc_psum_tensor("pE", [128, 512], F32)
    pF = nc.alloc_psum_tensor("pF", [128, 512], F32)
    pG = nc.alloc_psum_tensor("pG", [128, 512], F32)
    pH = nc.alloc_psum_tensor("pH", [128, 1024], BF16)
    pAB = [pA, pB]
    pCD = [pC, pD]
    pEF = [pE, pF]

    sems = {k: [nc.alloc_semaphore(k), 0] for k in ("L", "W", "T", "V", "S", "C")}

    def blockrun(fill):
        sch = Sched(nc, sems)
        fill(sch)
        SCHEDS.append((fill.__name__, {e: [(w, inc, n) for (f, w, inc, n) in sch.ops[e]]
                                       for e in ENGS}))
        with nc.Block() as blk:
            sch.emit(blk)

    def dma(dst, src):
        return lambda e: e.dma_start(out=dst, in_=src)

    def cp(dst, src):
        return lambda e: e.tensor_copy(out=dst, in_=src)

    def p0a(s):
        for dst, src in (
            (cwsb[:, :], conv_w[:, :]), (cbsb[:, :], conv_b[:, :]),
            (dsb[:, :], d_col[:, :]), (alsb[:, :], a_log[:, :]),
            (dbsb[:, :], delta_b[:, :]), (brsb[:, :], br_r[:, :]),
            (bnsb[:, :], bn_r[:, :]), (bl1sb[:, :], bl1c[:, :]),
            (ohsb[:, :], oh_in[:, :]), (wmsb[:, :], wmaskT[:, :]),
            (mvsb[:, :], mvec[:, :]), (idbf[:, :], ident_bf[:, :]),
            (idf[:, :], ident_f[:, :]), (trit[:, :], triT_c[:, :]),
            (onesrow[:, :], ones_row_c[:, :]),
            (selsb[:, :], sel_c[:, :]),
        ):
            s.add("sync", dma(dst, src), inc="L", inc_n=16)
        s.add("gpsimd", dma(bl2sb[:, :], bl2[:, :]), inc="W", inc_n=16)
        s.add("gpsimd",
              dma(wrsb.ap().rearrange("p (k n) -> p k n", k=KD),
                  wrT.ap().rearrange("(k p) n -> p k n", p=128)),
              inc="W", inc_n=16)
        s.add("gpsimd",
              dma(winst.ap()[:, 0:1024].rearrange("p (k n) -> p k n", k=KD),
                  winT.ap()[:, 0:128].rearrange("(k p) n -> p k n", p=128)),
              inc="W", inc_n=16)
        s.add("gpsimd", dma(wstk[:, 0:64], wstack[0:128, :]), inc="W", inc_n=16)
        lv = s.val("L")
        s.add("scalar", lambda e: e.activation(out=arow[:, :], in_=alsb[:, :],
                                               func=AF.Exp),
              waits=[("L", lv)], inc="S")
        s.add("vector", lambda e: e.tensor_scalar(out=arow[:, :], in0=arow[:, :],
                                                  scalar1=-1.0, scalar2=None,
                                                  op0=ALU.mult),
              waits=[("S", s.val("S"))], inc="V")
        vv = s.val("V")
        s.add("tensor", lambda e: e.matmul(out=pG[:, 0:DS], lhsT=onesrow[:, :],
                                           rhs=arow[:, :], start=True, stop=True),
              waits=[("V", vv), ("L", lv)], inc="T")
        s.add("vector", cp(abc[:, :], pG[:, 0:DS]), waits=[("T", s.val("T"))], inc="V")
        vv = s.val("V")
        s.add("tensor", lambda e: e.matmul(out=pG[:, 64:64 + DS], lhsT=onesrow[:, :],
                                           rhs=dbsb[:, :], start=True, stop=True),
              waits=[("V", vv)], inc="T")
        s.add("vector", cp(dbbc[:, :], pG[:, 64:64 + DS]), waits=[("T", s.val("T"))], inc="V")
        vv = s.val("V")
        s.add("tensor", lambda e: e.matmul(out=pG[:, 128:128 + NE], lhsT=onesrow[:, :],
                                           rhs=brsb[:, :], start=True, stop=True),
              waits=[("V", vv)], inc="T")
        s.add("vector", cp(brbc[:, :], pG[:, 128:128 + NE]), waits=[("T", s.val("T"))], inc="V")
        vv = s.val("V")
        s.add("tensor", lambda e: e.matmul(out=pG[:, 192:192 + NE], lhsT=onesrow[:, :],
                                           rhs=bnsb[:, :], start=True, stop=True),
              waits=[("V", vv)], inc="T")
        s.add("vector", cp(bnbc[:, :], pG[:, 192:192 + NE]), waits=[("T", s.val("T"))], inc="V")

    blockrun(p0a)

    def p0b(s):
        center_done = {}
        s.add("vector", lambda e: e.memset(convA[:, 7:8], 1e-5), inc="V")
        for ti in range(-1, 16):
            rows = HALO if ti < 0 else 128
            if ti < 0:
                ldst = xld[0:HALO, 0:D]
                lsrc = x_halo[:, :]
            else:
                sl = (ti % 2) * D
                ldst = xld[:, sl:sl + D]
                lsrc = x_own[ti * 128:(ti + 1) * 128, :]
            prev_user = center_done.get(ti - 2)
            if ti == 0:
                prev_user = center_done.get(-1)
            s.add("sync", dma(ldst, lsrc),
                  waits=[("V", prev_user)] if prev_user else (), inc="L", inc_n=16)
            lv = s.val("L")
            src = ldst
            wsq = [("L", lv)]
            if center_done.get("vq"):
                wsq.append(("V", center_done["vq"]))
            s.add("scalar", lambda e, src=src, rows=rows: e.activation(
                out=tmp2[0:rows, 0:D], in_=src, func=AF.Square),
                waits=wsq, inc="S")
            sq_s = s.val("S")
            s.add("vector", lambda e, src=src, rows=rows: e.reduce_sum(
                out=convA[0:rows, 0:1], in_=src, axis=AX), waits=[("L", lv)], inc="V")
            vv = s.val("V")
            s.add("vector", lambda e, rows=rows: e.reduce_sum(
                out=convA[0:rows, 2:3], in_=tmp2[0:rows, 0:D], axis=AX),
                waits=[("S", sq_s)], inc="V")
            vq = s.val("V")
            center_done["vq"] = vq
            s.add("scalar", lambda e, rows=rows: e.activation(
                out=convA[0:rows, 1:2], in_=convA[0:rows, 0:1], func=AF.Identity,
                scale=1.0 / D), waits=[("V", vv)], inc="S")
            s.add("scalar", lambda e, rows=rows: e.activation(
                out=convA[0:rows, 3:4], in_=convA[0:rows, 1:2], func=AF.Square), inc="S")
            s.add("scalar", lambda e, rows=rows: e.activation(
                out=convA[0:rows, 4:5], in_=convA[0:rows, 3:4], func=AF.Identity,
                scale=-1.0, bias=convA[0:rows, 7:8]), inc="S")
            s.add("scalar", lambda e, rows=rows: e.activation(
                out=convA[0:rows, 5:6], in_=convA[0:rows, 2:3], func=AF.Ln,
                scale=1.0 / D, bias=convA[0:rows, 4:5]), waits=[("V", vq)], inc="S")
            s.add("scalar", lambda e, rows=rows: e.activation(
                out=convA[0:rows, 6:7], in_=convA[0:rows, 5:6], func=AF.Exp,
                scale=-0.5), inc="S")
            wyb = [("S", s.val("S"))]
            if debug and ti == 1:
                wyb.append(("W", dbg_xn_w))
            s.add("vector", lambda e, src=src, rows=rows: e.tensor_scalar(
                out=y_bc[0:rows, 0:D], in0=src,
                scalar1=convA[0:rows, 1:2], scalar2=convA[0:rows, 6:7],
                op0=ALU.subtract, op1=ALU.mult),
                waits=wyb, inc="V")
            center_done[ti] = s.val("V")
            vv = s.val("V")
            dbg_gate = None
            if debug and ti == 0:
                s.add("gpsimd", dma(dbg["dbg_xn"][:, :], y_bc[:, 0:D]),
                      waits=[("V", s.val("V"))], inc="W", inc_n=16)
                dbg_xn_w = s.val("W")
            for k in range(KD):
                tv_prev = s.val("V")
                s.add("tensor", lambda e, k=k, rows=rows: e.transpose(
                    out=pH[:, k * 128:k * 128 + rows],
                    in_=y_bc[0:rows, k * 128:(k + 1) * 128],
                    identity=idbf[0:rows, 0:rows]),
                    waits=[("V", vv)], inc="T")
            tv = s.val("T")
            col = 0 if ti < 0 else HALO + ti * 128
            s.add("vector", lambda e, col=col, rows=rows: e.tensor_copy(
                out=xnT.ap().rearrange("p (k t) -> p k t", k=KD)[:, :, col:col + rows],
                in_=pH.ap().rearrange("p (k t) -> p k t", k=KD)[:, :, 0:rows]),
                waits=[("T", tv)], inc="V")

    blockrun(p0b)

    if debug:
        def p0b2(s):
            s.add("gpsimd", dma(dbg["dbg_xnT"][:, :],
                                xnT[:, HALO:HALO + 128]),
                  waits=[("V", s.val("V"))], inc="W", inc_n=16)
        blockrun(p0b2)

    def p0c(s):
        sc = tmp2[:, 0:128]
        sp_ = tmp2[:, 128:256]
        mk1 = tmp2[:, 256:384]
        sp2_ = tmp2[:, 384:512]
        mk = tmp2[:, 512:640]
        esub = tmp2[:, 640:768]
        eall = tmp2[:, 768:896]
        m1 = tmp2[:, 896:912]
        m2 = tmp2[:, 912:928]
        nm = tmp2[:, 928:944]
        den = tmp2[:, 944:960]
        rinv = tmp2[:, 976:992]

        def t3(ap2d):
            return ap2d.rearrange("p (t k) -> p t k", k=NE)

        def b3(ap16):
            return ap16.rearrange("p (t o) -> p t o", o=1).broadcast_to([128, 16, NE])

        s.add("sync", dma(convB[0:1, 0:128], bn16[:, :]), inc="L", inc_n=16)
        lv = s.val("L")
        s.add("tensor", lambda e: e.matmul(out=pG[:, 0:128], lhsT=onesrow[:, :],
                                           rhs=convB[0:1, 0:128], start=True,
                                           stop=True), waits=[("L", lv)], inc="T")
        s.add("vector", cp(convA[:, 0:128], pG[:, 0:128]),
              waits=[("T", s.val("T"))], inc="V")
        sc_done = {}
        for tt in range(16):
            p = pAB[tt % 2]
            w = [("W", 32)]
            pv = sc_done.get(tt - 2)
            if pv:
                w.append(("V", pv))
            for k in range(KD):
                s.add("tensor", lambda e, k=k, tt=tt, p=p: e.matmul(
                    out=p[:, 0:NE],
                    lhsT=xnT[:, k * TH + HALO + tt * 128: k * TH + HALO + (tt + 1) * 128],
                    rhs=wrsb[:, k * NE:(k + 1) * NE],
                    start=(k == 0), stop=(k == KD - 1)),
                    waits=w if k == 0 else (), inc="T")
            tv = s.val("T")
            s.add("vector", lambda e, p=p, tt=tt: e.tensor_add(
                out=sc[:, tt * 8:(tt + 1) * 8], in0=p[:, 0:NE], in1=brbc[:, :]),
                waits=[("T", tv)], inc="V")
            sc_done[tt] = s.val("V")
        s.add("vector", lambda e: e.tensor_add(out=sp_, in0=sc, in1=convA[:, 0:128]), inc="V")
        s.add("vector", lambda e: e.tensor_reduce(out=m1, in_=t3(sp_), axis=AX,
                                                  op=ALU.max), inc="V")
        s.add("vector", lambda e: e.tensor_tensor(out=t3(mk1), in0=t3(sp_),
                                                  in1=b3(m1), op=ALU.is_ge), inc="V")
        s.add("vector", lambda e: e.scalar_tensor_tensor(
            out=sp2_, in0=mk1, scalar=-1e30, in1=sp_, op0=ALU.mult, op1=ALU.add), inc="V")
        s.add("vector", lambda e: e.tensor_reduce(out=m2, in_=t3(sp2_), axis=AX,
                                                  op=ALU.max), inc="V")
        s.add("vector", lambda e: e.tensor_tensor(out=t3(esub), in0=t3(sp2_),
                                                  in1=b3(m2), op=ALU.is_ge), inc="V")
        s.add("vector", lambda e: e.tensor_add(out=mk, in0=mk1, in1=esub), inc="V")
        s.add("vector", lambda e: e.tensor_reduce(out=nm, in_=t3(sc), axis=AX,
                                                  op=ALU.max), inc="V")
        s.add("vector", lambda e: e.tensor_tensor(out=t3(esub), in0=t3(sc),
                                                  in1=b3(nm), op=ALU.subtract), inc="V")
        vv = s.val("V")
        s.add("scalar", lambda e: e.activation(out=eall, in_=esub, func=AF.Exp),
              waits=[("V", vv)], inc="S")
        sv = s.val("S")
        s.add("vector", lambda e: e.tensor_mul(out=esub, in0=eall, in1=mk),
              waits=[("S", sv)], inc="V")
        s.add("vector", lambda e: e.tensor_reduce(out=den, in_=t3(esub), axis=AX,
                                                  op=ALU.add), inc="V")
        vv = s.val("V")
        s.add("scalar", lambda e: e.activation(out=tmp2[:, 960:976], in_=den,
                                               func=AF.Ln), waits=[("V", vv)], inc="S")
        s.add("scalar", lambda e: e.activation(out=rinv, in_=tmp2[:, 960:976],
                                               func=AF.Exp, scale=-1.0), inc="S")
        sv = s.val("S")
        s.add("vector", lambda e: e.tensor_tensor(
            out=ew.ap().rearrange("p (t k) -> p t k", k=NE), in0=t3(esub),
            in1=b3(rinv), op=ALU.mult), waits=[("S", sv)], inc="V")
        if debug:
            s.add("sync", dma(dbg["dbg_sm"][:, :], mk[:, 0:48]),
                  waits=[("V", s.val("V"))], inc="L", inc_n=16)
            s.add("sync", dma(dbg["dbg_ew"][:, :], ew[:, :]), inc="L", inc_n=16)
            s.add("vector", lambda e: e.engine_nop(),
                  waits=[("L", s.val("L"))], inc="V")
        for tt in range(16):
            vv = s.val("V")
            s.add("tensor", lambda e, tt=tt: e.transpose(
                out=pG[0:8, 0:128], in_=ew[:, tt * 8:(tt + 1) * 8],
                identity=idf[:, :]), waits=[("V", vv)], inc="T")
            s.add("vector", lambda e, tt=tt: e.tensor_copy(
                out=ewT2[:, tt * 128:(tt + 1) * 128], in_=pG[0:8, 0:128]),
                waits=[("T", s.val("T"))], inc="V")

    blockrun(p0c)

    ssm_chunks = [(0, 512), (512, 1024), (1024, 1536), (1536, 2048), (2048, 2052)]

    def p1a(s):
        evac_done = {}
        cast_done = {}
        silu_done = {}
        pool_done = {}
        win_done = {}
        wstk_w = {}
        win_done[0] = 48
        wstk_w[0] = 64

        def emit_sdbc(ct):
            for q in range(4):
                qc = q * 512
                psd = [pC, pD, pE, pF][q]
                w = [("S", cast_done[(ct, q)])]
                if q == 0:
                    w.append(("W", wstk_w[ct]))
                s.add("tensor", lambda e, ct=ct, qc=qc, psd=psd: e.matmul(
                    out=psd[0:64, 0:512],
                    lhsT=wstk[:, (ct % 2) * 64:(ct % 2) * 64 + 64],
                    rhs=xconvT[:, ct * T + qc:ct * T + qc + 512],
                    start=(ct == 0), stop=(ct == NCT - 1)),
                    waits=w, inc="T")

        for ct in range(NCT):
            if ct + 1 < NCT:
                wt = s.val("T") if ct >= 1 else 0
                c0 = (ct + 1) * 128
                s.add("gpsimd", dma(
                    winst.ap()[:, ((ct + 1) % 2) * 1024:((ct + 1) % 2 + 1) * 1024].rearrange("p (k n) -> p k n", k=KD),
                    winT.ap()[:, c0:c0 + 128].rearrange("(k p) n -> p k n", p=128)),
                    waits=[("T", wt)], inc="W", inc_n=16)
                win_done[ct + 1] = s.val("W")
            for ci, (c0, c1) in enumerate(ssm_chunks):
                p = pAB[ci % 2]
                w = [("W", win_done[ct])]
                if ci >= 2:
                    prev = evac_done.get((ct, ci - 2))
                else:
                    prev = evac_done.get((ct - 1, 4 if ci == 0 else 3))
                if prev:
                    w.append(("V", prev))
                for k in range(KD):
                    s.add("tensor", lambda e, k=k, ct=ct, c0=c0, c1=c1, p=p: e.matmul(
                        out=p[:, 0:c1 - c0],
                        lhsT=winst[:, (ct % 2) * 1024 + k * 128:(ct % 2) * 1024 + (k + 1) * 128],
                        rhs=xnT[:, k * TH + c0:k * TH + c1],
                        start=(k == 0), stop=(k == KD - 1)),
                        waits=w if k == 0 else (), inc="T")
                tv = s.val("T")
                s.add("vector", lambda e, c0=c0, c1=c1, p=p: e.tensor_copy(
                    out=xs[:, c0:c1], in_=p[:, 0:c1 - c0]),
                    waits=[("T", tv)], inc="V")
                evac_done[(ct, ci)] = s.val("V")
            if ct >= 1:
                emit_sdbc(ct - 1)
            if ct + 1 < NCT:
                s.add("gpsimd", dma(
                    wstk[:, ((ct + 1) % 2) * 64:((ct + 1) % 2 + 1) * 64],
                    wstack[(ct + 1) * 128:(ct + 2) * 128, :]),
                    waits=[("T", s.val("T"))], inc="W", inc_n=16)
                wstk_w[ct + 1] = s.val("W")
            for q in range(4):
                qc = q * 512
                cB = (convB, convB2, convA2)[q % 3]
                w0 = [("V", evac_done[(ct, 4)])]
                wprev = (silu_done.get((ct, 0)) if q == 3 else
                         silu_done.get((ct - 1, {0: 3, 1: 1, 2: 2}[q])))
                if wprev:
                    w0.append(("S", wprev))
                s.add("vector", lambda e, ct=ct, qc=qc: e.tensor_scalar(
                    out=convA[:, 0:512], in0=xs[:, qc + 1:qc + 513],
                    scalar1=cwsb[:, ct * 4:ct * 4 + 1], scalar2=cbsb[:, ct:ct + 1],
                    op0=ALU.mult, op1=ALU.add), waits=w0, inc="V")
                s.add("vector", lambda e, ct=ct, qc=qc, cB=cB: e.scalar_tensor_tensor(
                    out=cB[:, 0:512], in0=xs[:, qc + 2:qc + 514],
                    scalar=cwsb[:, ct * 4 + 1:ct * 4 + 2], in1=convA[:, 0:512],
                    op0=ALU.mult, op1=ALU.add), inc="V")
                s.add("vector", lambda e, ct=ct, qc=qc, cB=cB: e.scalar_tensor_tensor(
                    out=convA[:, 0:512], in0=xs[:, qc + 3:qc + 515],
                    scalar=cwsb[:, ct * 4 + 2:ct * 4 + 3], in1=cB[:, 0:512],
                    op0=ALU.mult, op1=ALU.add), inc="V")
                s.add("vector", lambda e, ct=ct, qc=qc, cB=cB: e.scalar_tensor_tensor(
                    out=cB[:, 0:512], in0=xs[:, qc + 4:qc + 516],
                    scalar=cwsb[:, ct * 4 + 3:ct * 4 + 4], in1=convA[:, 0:512],
                    op0=ALU.mult, op1=ALU.add), inc="V")
                vv = s.val("V")
                s.add("scalar", lambda e, ct=ct, qc=qc, cB=cB: e.activation(
                    out=xconvT[:, ct * T + qc:ct * T + qc + 512],
                    in_=cB[:, 0:512], func=AF.Silu),
                    waits=[("V", vv)], inc="S")
                silu_done[(ct, q)] = s.val("S")
                cast_done[(ct, q)] = s.val("S")
        emit_sdbc(NCT - 1)
        if debug:
            s.add("gpsimd", dma(dbg["dbg_xconv"][:, :], xconvT[:, 0:T]),
                  waits=[("V", s.val("V"))], inc="W", inc_n=16)

    blockrun(p1a)

    def p1b(s):
        tv0 = s.val("T")
        for q, psd in enumerate((pC, pD, pE, pF)):
            s.add("vector", lambda e, q=q, psd=psd: e.tensor_copy(
                out=sdbc[0:64, q * 512:(q + 1) * 512], in_=psd[0:64, 0:512]),
                waits=[("T", tv0)], inc="V")
        marks["p1b_done"] = s.val("V")
        vv = s.val("V")
        for c in range(NCH):
            s.add("tensor", lambda e, c=c: e.transpose(
                out=pG[:, 0:64], in_=sdbc[0:64, c * 128:(c + 1) * 128],
                identity=idf[0:64, 0:64]), waits=[("V", s.val("V"))], inc="T")
            tv = s.val("T")
            s.add("vector", lambda e, c=c: e.tensor_copy(
                out=dpre[:, c * DS:(c + 1) * DS], in_=pG[:, 0:DS]),
                waits=[("T", tv)], inc="V")
            s.add("vector", lambda e, c=c: e.tensor_copy(
                out=sBs[:, c * DS:(c + 1) * DS], in_=pG[:, DS:2 * DS]), inc="V")
            s.add("vector", lambda e, c=c: e.tensor_copy(
                out=sCs[:, c * DS:(c + 1) * DS], in_=pG[:, 2 * DS:3 * DS]), inc="V")
            s.add("vector", lambda e, c=c: e.tensor_copy(
                out=xavg[:, c:c + 1], in_=pG[:, 3 * DS:3 * DS + 1]), inc="V")


    def p1c(s):
        V, S_, Tn = "V", "S", "T"
        for c in range(NCH):
            s.add("vector", lambda e, c=c: e.tensor_add(
                out=sT1[:, c * DS:(c + 1) * DS], in0=dpre[:, c * DS:(c + 1) * DS],
                in1=dbbc[:, :]), inc="V")
        vv = s.val("V")
        s.add("scalar", lambda e: e.activation(out=sT2[:, :], in_=sT1[:, :],
                                               func=AF.Abs), waits=[("V", vv)], inc="S")
        s.add("scalar", lambda e: e.activation(out=dpre[:, :], in_=sT2[:, :],
                                               func=AF.Exp, scale=-1.0), inc="S")
        s.add("scalar", lambda e: e.activation(out=sT2[:, :], in_=dpre[:, :],
                                               func=AF.Ln, bias=1.0), inc="S")
        sv = s.val("S")
        s.add("vector", lambda e: e.tensor_scalar(
            out=dpre[:, :], in0=sT1[:, :], scalar1=0.0, scalar2=None,
            op0=ALU.max), waits=[("S", sv)], inc="V")
        s.add("vector", lambda e: e.tensor_add(
            out=sT1[:, :], in0=dpre[:, :], in1=sT2[:, :]), inc="V")
        for c in range(NCH):
            s.add("vector", lambda e, c=c: e.tensor_mul(
                out=sG[:, c * DS:(c + 1) * DS], in0=sT1[:, c * DS:(c + 1) * DS],
                in1=abc[:, :]), inc="V")
        for c in range(NCH):
            s.add("vector", lambda e, c=c: e.tensor_mul(
                out=sT2[:, c * DS:(c + 1) * DS], in0=sT1[:, c * DS:(c + 1) * DS],
                in1=sBs[:, c * DS:(c + 1) * DS]), inc="V")
        for c in range(NCH):
            s.add("vector", lambda e, c=c: e.tensor_scalar(
                out=sBs[:, c * DS:(c + 1) * DS], in0=sT2[:, c * DS:(c + 1) * DS],
                scalar1=xavg[:, c:c + 1], scalar2=None, op0=ALU.mult), inc="V")
        dbg_l = None
        if debug:
            s.add("sync", dma(dbg["dbg_delta"][:, :], sT1[:, :]),
                  waits=[("V", s.val("V"))], inc="L", inc_n=16)
            s.add("sync", dma(dbg["dbg_g"][:, :], sG[:, :]),
                  waits=[("V", s.val("V"))], inc="L", inc_n=16)
            dbg_l = s.val("L")
        vv = s.val("V")
        s.add("tensor", lambda e: e.matmul(out=pG[0:1, 0:256], lhsT=trit[:, 127:128],
                                           rhs=sG[:, :], start=True, stop=True),
              waits=[("V", vv)], inc="T")
        s.add("vector", cp(glr[:, :], pG[0:1, 0:256]), waits=[("T", s.val("T"))], inc="V")
        for c in range(NCH):
            p = pAB[c % 2]
            wv = s.val("V")
            s.add("tensor", lambda e, c=c, p=p: e.matmul(
                out=p[:, 0:DS], lhsT=trit[:, :], rhs=sG[:, c * DS:(c + 1) * DS],
                start=True, stop=True), waits=[("V", wv)], inc="T")
            wcp = [("T", s.val("T"))]
            if c == 0 and dbg_l:
                wcp.append(("L", dbg_l))
            s.add("vector", lambda e, c=c, p=p: e.tensor_copy(
                out=sG[:, c * DS:(c + 1) * DS], in_=p[:, 0:DS]),
                waits=wcp, inc="V")
        vv = s.val("V")
        s.add("scalar", lambda e: e.activation(out=sT2[:, :], in_=sG[:, :],
                                               func=AF.Exp), waits=[("V", vv)], inc="S")
        sv = s.val("S")
        s.add("vector", lambda e: e.tensor_mul(out=sA_[:, :], in0=sCs[:, :],
                                               in1=sT2[:, :]), waits=[("S", sv)], inc="V")
        s.add("scalar", lambda e: e.activation(out=sT1[:, :], in_=sG[:, :],
                                               func=AF.Exp, scale=-1.0), inc="S")
        sv = s.val("S")
        s.add("vector", lambda e: e.tensor_mul(out=sG[:, :], in0=sT1[:, :],
                                               in1=sBs[:, :]), waits=[("S", sv)], inc="V")
        vv = s.val("V")
        s.add("tensor", lambda e: e.matmul(out=pG[0:1, 0:256], lhsT=trit[:, 127:128],
                                           rhs=sG[:, :], start=True, stop=True),
              waits=[("V", vv)], inc="T")
        s.add("vector", cp(blr[:, :], pG[0:1, 0:256]), waits=[("T", s.val("T"))], inc="V")
        for c in range(NCH):
            p = pAB[c % 2]
            wv = s.val("V")
            s.add("tensor", lambda e, c=c, p=p: e.matmul(
                out=p[:, 0:DS], lhsT=trit[:, :], rhs=sG[:, c * DS:(c + 1) * DS],
                start=True, stop=True), waits=[("V", wv)], inc="T")
            s.add("vector", lambda e, c=c, p=p: e.tensor_copy(
                out=sT1[:, c * DS:(c + 1) * DS], in_=p[:, 0:DS]),
                waits=[("T", s.val("T"))], inc="V")
        s.add("vector", lambda e: e.tensor_mul(out=sT2[:, :], in0=sA_[:, :],
                                               in1=sT1[:, :]), inc="V")
        for c in range(NCH):
            s.add("vector", lambda e, c=c: e.reduce_sum(
                out=y_col[:, c:c + 1], in_=sT2[:, c * DS:(c + 1) * DS], axis=AX), inc="V")
        if debug:
            s.add("sync", dma(dbg["dbg_a"][:, :], sA_[:, :]),
                  waits=[("V", s.val("V"))], inc="L", inc_n=16)
            s.add("sync", dma(dbg["dbg_bcum"][:, :], sT1[:, :]), inc="L", inc_n=16)
            s.add("sync", dma(dbg["dbg_yloc"][:, :], y_col[:, :]), inc="L", inc_n=16)
            dbg_p1c_l = s.val("L")
            s.add("vector", lambda e: e.engine_nop(),
                  waits=[("L", dbg_p1c_l)], inc="V")
        s.add("vector", lambda e: e.memset(gcum[:, 0:DS], 0.0), inc="V")
        s.add("vector", lambda e: e.memset(hrow[:, 0:DS], 0.0), inc="V")
        for c in range(NCH):
            s.add("vector", lambda e, c=c: e.tensor_add(
                out=gcum[:, (c + 1) * DS:(c + 2) * DS],
                in0=gcum[:, c * DS:(c + 1) * DS],
                in1=glr[:, c * DS:(c + 1) * DS]), inc="V")
            vv = s.val("V")
            s.add("scalar", lambda e, c=c: e.activation(
                out=erow[:, :], in_=glr[:, c * DS:(c + 1) * DS], func=AF.Exp),
                waits=[("V", vv)], inc="S")
            s.add("scalar", lambda e: e.activation(
                out=xld[:, 256:384], in_=xld[:, 128:256], func=AF.Identity),
                inc="S")
            sv = s.val("S")
            s.add("vector", lambda e, c=c: e.tensor_mul(
                out=wrow[:, :], in0=erow[:, :], in1=blr[:, c * DS:(c + 1) * DS]),
                waits=[("S", sv)], inc="V")
            s.add("vector", lambda e, c=c: e.tensor_mul(
                out=arow_r[:, :], in0=erow[:, :], in1=hrow[:, c * DS:(c + 1) * DS]), inc="V")
            s.add("vector", lambda e, c=c: e.tensor_add(
                out=hrow[:, (c + 1) * DS:(c + 2) * DS], in0=arow_r[:, :],
                in1=wrow[:, :]), inc="V")
        vv = s.val("V")
        s.add("tensor", lambda e: e.matmul(out=pG[0:8, 0:DS], lhsT=ohsb[:, :],
                                           rhs=gcum[:, NCH * DS:(NCH + 1) * DS],
                                           start=True, stop=True),
              waits=[("V", vv)], inc="T")
        s.add("tensor", lambda e: e.matmul(out=pG[0:8, DS:2 * DS], lhsT=ohsb[:, :],
                                           rhs=hrow[:, NCH * DS:(NCH + 1) * DS],
                                           start=True, stop=True), inc="T")
        s.add("vector", cp(gput[:, :], pG[0:8, 0:32]), waits=[("T", s.val("T"))], inc="V")
        vv = s.val("V")
        s.add("gpsimd", dma(cc_buf[:, :], gput[:, :]), waits=[("V", vv)], inc="W", inc_n=16)
        cw = s.val("W")
        s.add("gpsimd", lambda e: e.collective_compute(
            "AllReduce", ALU.add, replica_groups=[list(range(NCORE))],
            ins=[cc_buf.ap()], outs=[cc_buf.ap()]),
            waits=[("W", cw)], inc="C", inc_n=1)


    def _prefetch_gate(s, g, wait_t):
        ct = NCT + g
        c0g = ct * 128
        s.add("gpsimd", dma(
            winst.ap()[:, (ct % 2) * 1024:(ct % 2 + 1) * 1024].rearrange("p (k n) -> p k n", k=KD),
            winT.ap()[:, c0g:c0g + 128].rearrange("(k p) n -> p k n", p=128)),
            waits=[("T", wait_t)] if wait_t else (), inc="W", inc_n=16)
        marks.setdefault("win_done", {})[g] = s.val("W")

    def prefetch_gate_early(s):
        _prefetch_gate(s, 0, 0)
        _prefetch_gate(s, 1, 0)

    def p1d(s, g_lo=0, g_hi=NCT):
        win_done = marks["win_done"]
        silu_done = marks.setdefault("gate_silu_chain", {})

        def prefetch_gate(g, wait_t):
            _prefetch_gate(s, g, wait_t)

        for g in range(g_lo, g_hi):
            for ci in range(4):
                p = (pC, pD, pE, pF)[ci]
                w = [("W", win_done[g])]
                prev = silu_done.get((g - 1, ci))
                if prev:
                    w.append(("S", prev))
                elif g == 0:
                    w.append(("V", marks["p1b_done"]))
                ct = NCT + g
                c0, c1 = HALO + ci * 512, HALO + (ci + 1) * 512
                for k in range(KD):
                    s.add("tensor", lambda e, k=k, ct=ct, c0=c0, c1=c1, p=p: e.matmul(
                        out=p[:, 0:512],
                        lhsT=winst[:, (ct % 2) * 1024 + k * 128:(ct % 2) * 1024 + (k + 1) * 128],
                        rhs=xnT[:, k * TH + c0:k * TH + c1],
                        start=(k == 0), stop=(k == KD - 1)),
                        waits=w if k == 0 else (), inc="T")
                tv = s.val("T")
                s.add("scalar", lambda e, g=g, ci=ci, p=p: e.activation(
                    out=sgT[:, g * T + ci * 512:g * T + (ci + 1) * 512],
                    in_=p[:, 0:512], func=AF.Silu), waits=[("T", tv)], inc="S")
                silu_done[(g, ci)] = s.val("S")
            marks.setdefault("gate_S", {})[g] = s.val("S")
            if g + 2 < NCT:
                prefetch_gate(g + 2, s.val("T"))
        if debug and g_hi == NCT:
            s.add("gpsimd", dma(dbg["dbg_sg"][:, :], sgT[:, 0:T]),
                  waits=[("S", s.val("S"))], inc="W", inc_n=16)


    def p1e(s):
        def vflush(tag=""):
            s.add("vector", lambda e: e.tensor_copy(out=xld[:, 128:256],
                                                    in_=xld[:, 0:128]), inc="V")
            return s.val("V")

        def sflush():
            s.add("scalar", lambda e: e.activation(out=xld[:, 256:384],
                                                   in_=xld[:, 128:256],
                                                   func=AF.Identity), inc="S")
            return s.val("S")

        cw = s.val("C")
        s.add("gpsimd", dma(su8[:, :], cc_buf[:, :]), waits=[("C", cw)],
              inc="W", inc_n=16)
        cw2 = s.val("W")
        if debug:
            s.add("sync", dma(dbg["dbg_su"][:, :], su8[:, :]), waits=[("W", cw2)],
                  inc="L", inc_n=16)
        s.add("tensor", lambda e: e.matmul(out=pA[0:8, 0:DS], lhsT=wmsb[:, :],
                                           rhs=su8[0:8, 0:DS], start=True, stop=True),
              waits=[("W", cw2)], inc="T")
        tv = s.val("T")
        s.add("scalar", lambda e: e.activation(out=e8[:, :], in_=pA[0:8, 0:DS],
                                               func=AF.Exp), waits=[("T", tv)], inc="S")
        sv = sflush()
        s.add("vector", lambda e: e.tensor_mul(out=ctrb[:, :], in0=e8[:, :],
                                               in1=su8[0:8, DS:2 * DS]),
              waits=[("S", sv)], inc="V")
        vv = vflush()
        s.add("tensor", lambda e: e.matmul(out=pB[0:1, 0:DS], lhsT=mvsb[:, :],
                                           rhs=ctrb[:, :], start=True, stop=True),
              waits=[("V", vv)], inc="T")
        s.add("vector", cp(hin[:, :], pB[0:1, 0:DS]), waits=[("T", s.val("T"))], inc="V")
        e_row = sdbc[0:1, 1024:1280]
        w_all = sdbc[0:1, 1280:1536]
        vv = s.val("V")
        s.add("scalar", lambda e: e.activation(out=e_row[:, :],
                                               in_=gcum[:, 0:NCH * DS],
                                               func=AF.Exp), waits=[("V", vv)], inc="S")
        sv = s.val("S")
        hin3 = hin.rearrange("p (o d) -> p o d", o=1).broadcast_to([1, NCH, DS])
        s.add("vector", lambda e: e.tensor_tensor(
            out=w_all.rearrange("p (t d) -> p t d", d=DS),
            in0=e_row.rearrange("p (t d) -> p t d", d=DS),
            in1=hin3, op=ALU.mult), waits=[("S", sv)], inc="V")
        vflush()
        s.add("vector", lambda e: e.tensor_add(out=w_all[:, :], in0=w_all[:, :],
                                               in1=hrow[:, 0:NCH * DS]), inc="V") \
            if False else None
        s.add("vector", lambda e: e.tensor_tensor(
            out=sdbc[0:1, 1536:1792], in0=w_all[:, :], in1=hrow[:, 0:NCH * DS],
            op=ALU.add), inc="V")
        vv = vflush()
        if debug:
            s.add("sync", dma(dbg["dbg_w2"][:, :], sdbc[0:1, 1536:1792]),
                  waits=[("V", vv)], inc="L", inc_n=16)
            s.add("sync", dma(dbg["dbg_er"][:, :], e_row[:, :]), inc="L", inc_n=16)
            s.add("sync", dma(dbg["dbg_hr"][:, :], hrow[:, 0:272]), inc="L", inc_n=16)
        s.add("tensor", lambda e: e.matmul(out=pA[:, 0:NCH * DS], lhsT=onesrow[:, :],
                                           rhs=sdbc[0:1, 1536:1792], start=True,
                                           stop=True), waits=[("V", vv)], inc="T")
        marks["sdbc_T"] = s.val("T")
        marks["sdbc_V"] = s.val("V")
        tv = s.val("T")
        s.add("vector", lambda e: e.tensor_mul(out=sT2[:, :], in0=sA_[:, :],
                                               in1=pA[:, 0:NCH * DS]),
              waits=[("T", tv)], inc="V")
        s.add("vector", lambda e: e.tensor_reduce(
            out=xld[:, 384:384 + NCH],
            in_=sT2.ap().rearrange("p (t d) -> p t d", d=DS), axis=AX,
            op=ALU.add), inc="V")
        vflush()
        s.add("vector", lambda e: e.tensor_add(out=yfin[:, :], in0=y_col[:, :],
                                               in1=xld[:, 384:384 + NCH]), inc="V")
        vv = vflush()
        if debug:
            s.add("sync", dma(dbg["dbg_ycol"][:, :], yfin[:, :]),
                  waits=[("V", vv)], inc="L", inc_n=16)
        vv = s.val("V")
        s.add("tensor", lambda e: e.transpose(
            out=pG[0:8, 0:128], in_=yfin[:, 0:8], identity=idf[:, :]),
            waits=[("V", vv)], inc="T")
        s.add("tensor", lambda e: e.transpose(
            out=pG[0:8, 128:256], in_=yfin[:, 8:16], identity=idf[:, :]), inc="T")
        tv = s.val("T")
        s.add("vector", lambda e: e.tensor_copy(
            out=ystg[:, :], in_=pG[0:8, 0:256]), waits=[("T", tv)], inc="V")
        vv2 = vflush()
        scopy = {}
        for c in range(NCH):
            p = pAB[(c // 4) % 2]
            w = ()
            if c == 0:
                w = [("V", vv2)]
            elif c in (8, 12):
                w = [("S", scopy[c // 4 - 2])]
            s.add("tensor", lambda e, c=c, p=p: e.matmul(
                out=p[:, (c % 4) * 128:(c % 4 + 1) * 128],
                lhsT=selsb[:, (c % 8) * 128:(c % 8 + 1) * 128],
                rhs=ystg[0:8, (c // 8) * 128:(c // 8) * 128 + 128],
                start=True, stop=True), waits=w, inc="T")
            if c % 4 == 3:
                tv = s.val("T")
                s.add("scalar", lambda e, c=c, p=p: e.activation(
                    out=y_bc[:, (c - 3) * 128:(c + 1) * 128], in_=p[:, 0:512],
                    func=AF.Copy), waits=[("T", tv)], inc="S")
                scopy[c // 4] = s.val("S")
        if debug:
            s.add("gpsimd", dma(dbg["dbg_ysm"][:, :], y_bc[:, 0:NCH]),
                  waits=[("S", s.val("S"))], inc="W", inc_n=16)
        marks["ybc_S"] = s.val("S")
        marks["ybc_V"] = s.val("V")
        marks["ybc_T"] = s.val("T")


    def p2(s):
        for ct in range(NCT):
            if ct == 0:
                w = [("T", marks["sdbc_T"]), ("V", marks["sdbc_V"])]
            elif ct == 12:
                w = [("T", s.val("T"))]
            else:
                w = ()
            s.add("gpsimd", dma(
                woutst[:, ct * 512:(ct + 1) * 512],
                woutT.ap()[ct * 128:(ct + 1) * 128, 0:512]),
                waits=w, inc="W", inc_n=16)
        gate_S = marks["gate_S"]
        for ct in range(NCT):
            for h in range(2):
                c0, c1 = h * 1024, (h + 1) * 1024
                w = []
                if h == 0:
                    if ct == 0:
                        w = [("S", marks["ybc_S"]), ("T", marks["ybc_T"])]
                        if debug:
                            w += [("W", s.val("W")), ("L", s.val("L"))]
                    elif gate_S[ct] > marks["ybc_S"]:
                        w = [("S", gate_S[ct])]
                s.add("vector", lambda e, ct=ct, c0=c0, c1=c1: e.tensor_mul(
                    out=tmp2[:, :], in0=sgT[:, ct * T + c0:ct * T + c1],
                    in1=y_bc[:, c0:c1]), waits=w, inc="V")
                s.add("vector", lambda e, ct=ct, c0=c0, c1=c1: e.scalar_tensor_tensor(
                    out=yskT[:, ct * T + c0:ct * T + c1],
                    in0=xconvT[:, ct * T + c0:ct * T + c1],
                    scalar=dsb[:, ct:ct + 1], in1=tmp2[:, :],
                    op0=ALU.mult, op1=ALU.add), inc="V")
        marks["p2_V"] = s.val("V")
        marks["p2_W"] = s.val("W")

    marks = {}

    def p_mid(s):
        prefetch_gate_early(s)
        p1b(s)
        p1c(s)
        p1d(s, 0, 8)
        p1e(s)
        p1d(s, 8, NCT)
        p2(s)
    p_mid.__name__ = "p_mid"
    blockrun(p_mid)

    def p3(s):
        wout_ready = s.val("W")
        evac = {}
        for dc in range(2):
            if dc == 1:
                tv = s.val("T")
                for ct in range(NCT):
                    s.add("gpsimd", dma(
                        woutst[:, ct * 512:(ct + 1) * 512],
                        woutT.ap()[ct * 128:(ct + 1) * 128, 512:1024]),
                        waits=[("T", tv)], inc="W", inc_n=16)
                wout_ready = s.val("W")
            for tt in range(16):
                p = pAB[tt % 2]
                w = []
                pv = evac.get((dc, tt - 2)) or (evac.get((dc - 1, 14 + tt)) if dc else None)
                if pv:
                    w.append(("S", pv))
                s.add("tensor", lambda e, tt=tt, dc=dc, p=p: e.matmul(
                    out=p[:, 0:512],
                    lhsT=ewT2[:, tt * 128:(tt + 1) * 128],
                    rhs=bl2sb[:, dc * 512:(dc + 1) * 512],
                    start=True, stop=False), waits=w, inc="T")
                w2 = [("W", wout_ready)]
                if dc == 0 and tt == 0:
                    w2 += [("V", marks["p2_V"]), ("W", marks["p2_W"])]
                for ct in range(NCT):
                    s.add("tensor", lambda e, ct=ct, tt=tt, p=p: e.matmul(
                        out=p[:, 0:512],
                        lhsT=yskT[:, ct * T + tt * 128:ct * T + (tt + 1) * 128],
                        rhs=woutst[:, ct * 512:(ct + 1) * 512],
                        start=False, stop=(ct == NCT - 1)),
                        waits=w2 if ct == 0 else (), inc="T")
                tv = s.val("T")
                s.add("scalar", lambda e, tt=tt, dc=dc, p=p: e.activation(
                    out=y_acc[:, tt * D + dc * 512:tt * D + (dc + 1) * 512],
                    in_=p[:, 0:512], func=AF.Copy), waits=[("T", tv)], inc="S")
                evac[(dc, tt)] = s.val("S")
        if debug:
            s.add("sync", dma(dbg["dbg_yac"][:, :], y_acc[:, 0:D]),
                  waits=[("S", s.val("S"))], inc="L", inc_n=16)

    blockrun(p3)

    def p4(s):
        from collections import deque
        wl1_done = {}
        wl2_done = {}
        g_done = {}
        acc = {}
        hist = deque([None, None], maxlen=2)
        ew_done = {}
        p5_ready = {}

        def prefetch_ab(n, hp, wait_t):
            par = hp % 2
            for half, col0 in ((0, hp * 128), (1, HID + hp * 128)):
                slot = (2 * par + half) * D
                s.add("gpsimd", dma(
                    wabst.ap()[:, slot:slot + D].rearrange("p (k n) -> p k n", k=KD),
                    wl1.ap()[n, :, col0:col0 + 128].rearrange(
                        "(k p) n -> p k n", p=128)),
                    waits=[("T", wait_t)], inc="W", inc_n=16)
            wl1_done[(n, hp)] = s.val("W")

        def prefetch_wl2(n, hch, wait_t):
            for i in range(8):
                hc = hch * 8 + i
                s.add("gpsimd", dma(
                    wl2st[:, i * D:(i + 1) * D],
                    wl2.ap()[n, hc * 128:(hc + 1) * 128, :]),
                    waits=[("T", wait_t)], inc="W", inc_n=16)
            wl2_done[(n, hch)] = s.val("W")

        def emit_ewbc(n):
            prev_readers = g_done.get((n - 1, NCT - 1, 3))
            for tc in range(4):
                s.add("tensor", lambda e, n=n, tc=tc: e.matmul(
                    out=pG[:, 0:512], lhsT=selsb[:, n * 128:(n + 1) * 128],
                    rhs=ewT2[:, tc * 512:(tc + 1) * 512], start=True, stop=True),
                    waits=[("S", s.val("S"))], inc="T")
                tv = s.val("T")
                wsc = [("T", tv)]
                if prev_readers:
                    wsc.append(("V", prev_readers))
                s.add("scalar", lambda e, tc=tc: e.activation(
                    out=ew_bcT[:, tc * 512:(tc + 1) * 512], in_=pG[:, 0:512],
                    func=AF.Copy), waits=wsc, inc="S")
            ew_done[n] = s.val("S")

        def emit_mm1_half(n, half):
            for hp in range(half * 8, half * 8 + 8):
                par = hp % 2
                asl = (2 * par) * D
                bsl = (2 * par + 1) * D
                for tc in range(4):
                    pa = pCD[tc % 2]
                    pb = pEF[tc % 2]
                    w = [("W", wl1_done[(n, hp)])]
                    if hist[0] is not None:
                        w.append(("S", hist[0][0]))
                        w.append(("V", hist[0][1]))
                    for k in range(KD):
                        s.add("tensor", lambda e, k=k, tc=tc, asl=asl, pa=pa: e.matmul(
                            out=pa[:, 0:512],
                            lhsT=wabst[:, asl + k * 128:asl + (k + 1) * 128],
                            rhs=xnT[:, k * TH + HALO + tc * 512:k * TH + HALO + (tc + 1) * 512],
                            start=(k == 0), stop=(k == KD - 1)),
                            waits=w if k == 0 else (), inc="T")
                    for k in range(KD):
                        s.add("tensor", lambda e, k=k, tc=tc, bsl=bsl, pb=pb: e.matmul(
                            out=pb[:, 0:512],
                            lhsT=wabst[:, bsl + k * 128:bsl + (k + 1) * 128],
                            rhs=xnT[:, k * TH + HALO + tc * 512:k * TH + HALO + (tc + 1) * 512],
                            start=(k == 0), stop=(k == KD - 1)), inc="T")
                    tv = s.val("T")
                    sa = saA if tc % 2 == 0 else saB
                    sb = saB if tc % 2 == 0 else saA
                    wsl = [("T", tv), ("S", ew_done[n])]
                    pv2 = g_done.get((n, hp, tc - 1)) or g_done.get((n, hp - 1, 3)) \
                        or g_done.get((n - 1, NCT - 1, 3))
                    if pv2:
                        wsl.append(("V", pv2))
                    s.add("scalar", lambda e, hp=hp, n=n, pa=pa, sa=sa: e.activation(
                        out=sa[:, :], in_=pa[:, 0:512], func=AF.Silu,
                        bias=bl1sb[:, n * 32 + hp:n * 32 + hp + 1]),
                        waits=wsl, inc="S")
                    silu_s_val = s.val("S")
                    sv = s.val("S")
                    s.add("vector", lambda e, tc=tc, sa=sa, sb=sb: e.tensor_mul(
                        out=sb[:, :], in0=sa[:, :],
                        in1=ew_bcT[:, tc * 512:(tc + 1) * 512]),
                        waits=[("S", sv)], inc="V")
                    s.add("vector", lambda e, n=n, hp=hp, tc=tc, pb=pb, sb=sb:
                          e.scalar_tensor_tensor(
                              out=gT[:, hp * T + tc * 512:hp * T + (tc + 1) * 512],
                              in0=pb[:, 0:512],
                              scalar=bl1sb[:, n * 32 + 16 + hp:n * 32 + 16 + hp + 1],
                              in1=sb[:, :], op0=ALU.add, op1=ALU.mult), inc="V")
                    g_done[(n, hp, tc)] = s.val("V")
                    hist.append((silu_s_val, s.val("V")))
                nxt = (n, hp + 2)
                if hp + 2 >= NCT:
                    nxt = (n + 1, hp + 2 - NCT)
                if nxt[0] < NE:
                    prefetch_ab(nxt[0], nxt[1], s.val("T"))

        res_done = {}

        def emit_mm2_half(n, hch):
            wl2v = wl2_done[(n, hch)]
            gv = g_done[(n, hch * 8 + 7, 3)]
            final = (n == NE - 1 and hch == 1)
            if final:
                for tt0 in (0, 1):
                    s.add("sync", dma(xres[:, (tt0 % 2) * D:(tt0 % 2) * D + D],
                                      x_own[tt0 * 128:(tt0 + 1) * 128, :]),
                          waits=[("T", marks["mm1_done"])], inc="L", inc_n=16)
                    res_done[("L", tt0)] = s.val("L")
            for tt in range(16):
                for dc in range(2):
                    p = pAB[(tt * 2 + dc) % 2]
                    w = [("W", wl2v), ("V", gv)]
                    pv = acc.get((n, hch, tt - 1, dc)) or \
                        (acc.get((n, hch - 1, 15, 1)) if hch else
                         acc.get((n - 1, 1, 15, 1)))
                    if pv:
                        w.append(("V", pv))
                    for i in range(8):
                        hc = hch * 8 + i
                        s.add("tensor", lambda e, i=i, hc=hc, tt=tt, dc=dc, p=p:
                              e.matmul(
                                  out=p[:, 0:512],
                                  lhsT=gT[:, hc * T + tt * 128:hc * T + (tt + 1) * 128],
                                  rhs=wl2st[:, i * D + dc * 512:i * D + (dc + 1) * 512],
                                  start=(i == 0), stop=(i == 7)),
                              waits=w if i == 0 else (), inc="T")
                    tv = s.val("T")
                    s.add("vector", lambda e, tt=tt, dc=dc, p=p: e.tensor_add(
                        out=y_acc[:, tt * D + dc * 512:tt * D + (dc + 1) * 512],
                        in0=y_acc[:, tt * D + dc * 512:tt * D + (dc + 1) * 512],
                        in1=p[:, 0:512]), waits=[("T", tv)], inc="V")
                    acc[(n, hch, tt, dc)] = s.val("V")
                if final:
                    sl = (tt % 2) * D
                    s.add("vector", lambda e, tt=tt, sl=sl: e.tensor_add(
                        out=y_acc[:, tt * D:(tt + 1) * D],
                        in0=y_acc[:, tt * D:(tt + 1) * D], in1=xres[:, sl:sl + D]),
                        waits=[("L", res_done[("L", tt)])], inc="V")
                    res_done[tt] = s.val("V")
                    s.add("sync", dma(out[tt * 128:(tt + 1) * 128, :],
                                      y_acc[:, tt * D:(tt + 1) * D]),
                          waits=[("V", res_done[tt])], inc="L", inc_n=16)
                    if tt + 2 < 16:
                        s.add("sync", dma(xres[:, sl:sl + D],
                                          x_own[(tt + 2) * 128:(tt + 3) * 128, :]),
                              waits=[("V", res_done[tt])], inc="L", inc_n=16)
                        res_done[("L", tt + 2)] = s.val("L")

        prefetch_ab(0, 0, 0)
        prefetch_ab(0, 1, 0)
        emit_ewbc(0)
        emit_mm1_half(0, 0)
        prefetch_wl2(0, 0, 0)
        emit_mm1_half(0, 1)
        emit_mm2_half(0, 0)
        for n in range(1, NE):
            emit_ewbc(n)
            prefetch_wl2(n - 1, 1, s.val("T"))
            emit_mm1_half(n, 0)
            emit_mm2_half(n - 1, 1)
            prefetch_wl2(n, 0, s.val("T"))
            emit_mm1_half(n, 1)
            emit_mm2_half(n, 0)
        marks["mm1_done"] = s.val("T")
        prefetch_wl2(NE - 1, 1, s.val("T"))
        emit_mm2_half(NE - 1, 1)
        lv = s.val("L")
        s.add("sync", lambda e: e.nop(), waits=[("L", lv)])

    blockrun(p4)


    return nc, sbuf_used


def _prep_inputs(inputs):
    f32 = np.float32
    x = np.ascontiguousarray(inputs["L0_attn_x"], dtype=f32).reshape(2 * 8192, D)
    winT_h = np.ascontiguousarray(inputs["L0_attn_in_proj_w"].T, dtype=f32)
    wstack_h = np.zeros((DI, 64), dtype=f32)
    wstack_h[:, 0:16] = inputs["L0_attn_delta_w"].T
    wstack_h[:, 16:32] = inputs["L0_attn_B_w"].T
    wstack_h[:, 32:48] = inputs["L0_attn_C_w"].T
    wstack_h[:, 48] = 1.0 / DI
    cw = np.ascontiguousarray(inputs["L0_attn_conv_weight"], dtype=f32)
    conv_w_h = np.zeros((128, 64), dtype=f32)
    for ct in range(NCT):
        conv_w_h[:, ct * 4:(ct + 1) * 4] = cw[ct * 128:(ct + 1) * 128, :]
    conv_b_h = inputs["L0_attn_conv_bias"].astype(f32).reshape(NCT, 128).T.copy()
    d_col_h = inputs["L0_attn_D"].astype(f32).reshape(NCT, 128).T.copy()
    a_log_h = inputs["L0_attn_A_log"].astype(f32).reshape(1, DS)
    delta_b_h = inputs["L0_attn_delta_b"].astype(f32).reshape(1, DS)
    woutT_h = np.ascontiguousarray(inputs["L0_attn_out_proj_w"].T, dtype=f32)
    wrT_h = np.ascontiguousarray(inputs["L0_ffn_Wr"].T, dtype=f32)
    br_h = inputs["L0_ffn_br"].astype(f32).reshape(1, NE)
    bn_h = inputs["L0_ffn_biases_N"].astype(f32).reshape(1, NE)
    wl1_h = np.ascontiguousarray(inputs["L0_ffn_Wl1"], dtype=f32)
    bl1 = inputs["L0_ffn_bl1"].astype(f32)
    bl1c_h = np.zeros((128, 256), dtype=f32)
    for n in range(NE):
        bl1c_h[:, n * 32:(n + 1) * 32] = bl1[n].reshape(32, 128).T
    wl2_h = np.ascontiguousarray(inputs["L0_ffn_Wl2"], dtype=f32)
    bl2_h = np.ascontiguousarray(inputs["L0_ffn_bl2"], dtype=f32)

    maps = []
    for c in range(NCORE):
        t0 = c * T
        chain_pos = c % 4
        halo = (np.zeros((HALO, D), dtype=f32) if chain_pos == 0
                else x[t0 - HALO:t0].copy())
        oh = np.zeros((1, NCORE), dtype=f32)
        oh[0, c] = 1.0
        wmT = np.zeros((NCORE, NCORE), dtype=f32)
        mv = np.zeros((NCORE, 1), dtype=f32)
        chain = range((c // 4) * 4, (c // 4) * 4 + 4)
        for cp in chain:
            if cp < c:
                mv[cp, 0] = 1.0
                for j in chain:
                    if cp < j < c:
                        wmT[j, cp] = 1.0
        maps.append({
            "x_own": x[t0:t0 + T].copy(), "x_halo": halo,
            "winT": winT_h, "wstack": wstack_h, "conv_w": conv_w_h,
            "conv_b": conv_b_h, "d_col": d_col_h, "a_log": a_log_h,
            "delta_b": delta_b_h, "woutT": woutT_h, "wrT": wrT_h,
            "br_r": br_h, "bn_r": bn_h, "bn16": np.tile(bn_h, (1, 16)), "wl1": wl1_h, "bl1c": bl1c_h,
            "wl2": wl2_h, "bl2": bl2_h, "oh": oh, "wmaskT": wmT, "mvec": mv,
        })
    return maps


_NC_CACHE = {}


def kernel(**inputs):
    key = "main"
    if key not in _NC_CACHE:
        _NC_CACHE[key] = build(debug=False)
    nc, _ = _NC_CACHE[key]
    maps = _prep_inputs(inputs)
    res = run_bass_kernel_spmd(nc, maps, core_ids=list(range(NCORE)))
    outs = [res.results[c]["out"] for c in range(NCORE)]
    return np.concatenate(outs, axis=0).reshape(2, 8192, D).astype(np.float32)


def check_deadlock():
    sem = {k: 0 for k in ("L", "W", "T", "V", "S", "C")}
    for bname, ops in SCHEDS:
        ptr = {e: 0 for e in ENGS}
        stuck = False
        while True:
            progress = False
            for e in ENGS:
                while ptr[e] < len(ops[e]):
                    waits, inc, n = ops[e][ptr[e]]
                    if all(sem[s] >= v for (s, v) in waits):
                        if inc:
                            sem[inc] += n
                        ptr[e] += 1
                        progress = True
                    else:
                        break
            if all(ptr[e] == len(ops[e]) for e in ENGS):
                break
            if not progress:
                stuck = True
                print(f"DEADLOCK in block {bname}:")
                for e in ENGS:
                    if ptr[e] < len(ops[e]):
                        waits, inc, n = ops[e][ptr[e]]
                        unmet = [(s, v, sem[s]) for (s, v) in waits if sem[s] < v]
                        print(f"  {e} op#{ptr[e]}/{len(ops[e])} blocked on {unmet}")
                break
        if stuck:
            return False
    print("schedule: no deadlock")
    return True


if __name__ == "__main__":
    nc, used = build(debug=True)
    print(f"sbuf used: {used} bytes/partition")
    print(f"instructions: {nc.n_instructions()}")
    check_deadlock()
